# version 10
# baseline (speedup 1.0000x reference)
import contextlib
import numpy as np
import ml_dtypes
import concourse.bass as bass
import concourse.mybir as mybir
from concourse.bass_utils import run_bass_kernel_spmd

F32 = mybir.dt.float32
BF16 = mybir.dt.bfloat16
I32 = mybir.dt.int32
AF = mybir.ActivationFunctionType
ALU = mybir.AluOpType
DS = bass.ds

B, T, D = 2, 8192, 1024
HALF = T // 2
QB = T // 4
FCM = 4 * D
FFN = 3072
EPS = 1e-5
NC8 = D // 128
TT = 512
ROT = 32
NKV = 512

CSLOT = {"mk": 0, "mv": 1, "mr": 2, "cmk": 3, "cmr": 4, "td": 5, "tf": 6,
         "g1": 7, "b1": 8, "g2": 9, "b2": 10}
DSLOT = {"lam": 0, "eu": 1, "bk": 2, "bv": 3, "br": 4, "bcr": 5,
         "ck": 6, "cv": 7, "cr": 8}

FL_BND = 1
FL_KV = 2
FL_INA = 3
FL_INB = 4


def _role(pnc):
    return pnc // 4, pnc % 2, (pnc // 2) % 2


CHAIN_OPS = {
    "activation", "copy", "add", "mul", "square", "sqrt", "sign",
    "tensor_tensor", "tensor_scalar", "scalar_tensor_tensor",
    "tensor_tensor_scan", "tensor_copy", "tensor_add", "tensor_sub",
    "tensor_mul", "tensor_max", "tensor_relu", "tensor_scalar_add",
    "tensor_scalar_mul", "tensor_scalar_sub", "tensor_scalar_max",
    "tensor_reduce", "reduce_sum", "reduce_max", "bn_stats", "bn_aggr",
    "memset", "memzero", "reciprocal", "select", "stream_shuffle",
    "copy_predicated", "iota", "transpose", "max",
}


class _Chained:

    def __init__(self, eng, ee, key):
        self._eng = eng
        self._ee = ee
        self._key = key
        self._sem = ee.sems[key]

    def __getattr__(self, name):
        fn = getattr(self._eng, name)
        if name not in CHAIN_OPS:
            return fn
        ee, eng, sem, key = self._ee, self._eng, self._sem, self._key

        def wrapped(*a, **k):
            if ee.cnt[key] > 0:
                eng.wait_ge(sem, ee.cnt[key])
            ins = fn(*a, **k)
            ins.then_inc(sem, 1)
            ee.cnt[key] += 1
            return ins
        return wrapped


class E:

    def __init__(self, nc, block, stack):
        self.nc = nc
        self.block = block
        self.stack = stack
        self.sems = {}
        self.cnt = {}
        self.t = {}
        self.regs = {}
        self.bar_target = 0

    def sem(self, *names):
        for n in names:
            self.sems[n] = self.stack.enter_context(self.nc.semaphore(n))
            self.cnt[n] = 0

    CHAIN_KEY = {"vector": "v_done", "scalar": "sc_done"}

    def _wrap(self, emit, waits, inc, engname):
        wl = [(self.sems[s], v) for (s, v) in waits]
        it = None
        if inc:
            it = (self.sems[inc[0]], inc[1])
        ck = self.CHAIN_KEY.get(engname)

        def body(e):
            for sh, v in wl:
                if v > 0:
                    e.wait_ge(sh, v)
            ins = emit(_Chained(e, self, ck) if ck else e)
            if it is not None:
                assert ins is not None, f"{engname} op with inc returned None"
                ins.then_inc(it[0], it[1])
        getattr(self.block, engname)(body)
        if inc:
            self.cnt[inc[0]] += inc[1]

    def S(self, emit, waits=(), inc=None):
        self._wrap(emit, waits, inc, "sync")

    def V(self, emit, waits=(), inc=None):
        self._wrap(emit, waits, inc, "vector")

    def Tn(self, emit, waits=(), inc=None):
        self._wrap(emit, waits, inc, "tensor")

    def Sc(self, emit, waits=(), inc=None):
        self._wrap(emit, waits, inc, "scalar")

    def G(self, emit, waits=(), inc=None):
        self._wrap(emit, waits, inc, "gpsimd")

    def Gsw(self, emit, waits=(), inc=None, done_sem=None):
        wl = [(self.sems[s], v) for (s, v) in waits]
        it = (self.sems[inc[0]], inc[1]) if inc else None
        nc = self.nc

        def body(g):
            for sh, v in wl:
                if v > 0:
                    g.wait_ge(sh, v)
            for arm in nc.switch_core_id(engines=[g], n=128):
                ins = emit(g, int(arm) % 8)
                if it is not None and ins is not None:
                    ins.then_inc(it[0], it[1])
                if done_sem is not None:
                    g.sem_inc(self.sems[done_sem], 1)
        self.block.gpsimd(body)
        if inc:
            self.cnt[inc[0]] += inc[1]
        if done_sem is not None:
            self.cnt[done_sem] += 1

    def Sc_batch(self, items):
        resolved = []
        for emit, waits, inc in items:
            wl = [(self.sems[s], v) for (s, v) in waits]
            it = (self.sems[inc[0]], inc[1]) if inc else None
            resolved.append((emit, wl, it))

        def body(e):
            ch = _Chained(e, self, "sc_done")
            for emit, wl, it in resolved:
                for sh, v in wl:
                    if v > 0:
                        e.wait_ge(sh, v)
                ins = emit(ch)
                if it is not None:
                    ins.then_inc(it[0], it[1])
                    self.cnt[it[0]] += it[1]
        self.block.scalar(body)

    def now(self, sem):
        return self.cnt[sem]

    def wv(self, *sems):
        return [(s, self.cnt[s]) for s in sems]

    def barrier(self):
        bar = self.sems["bar"]
        self.bar_target += 5
        tgt = self.bar_target

        def f(e):
            e.sem_inc(bar, 1)
            e.wait_ge(bar, tgt)
        for nm in ("sync", "scalar", "vector", "tensor", "gpsimd"):
            getattr(self.block, nm)(f)

    def sb(self, name, shape, dt=F32):
        h = self.stack.enter_context(self.nc.sbuf_tensor(name, shape, dt))
        self.t[name] = h
        return h

    def ps(self, name, shape, dt=F32):
        h = self.stack.enter_context(self.nc.psum_tensor(name, shape, dt))
        self.t[name] = h
        return h


def cslot(ee, name, c):
    i = CSLOT[name] * 8 + c
    return ee.t["colpk"][:, i:i + 1]


def dslot(ee, name, c):
    i = DSLOT[name] * 8 + c
    return ee.t["dcol"][:, i:i + 1]



def build_program(dbg=(), race_checks=False, poll_backoff=True):
    nc = bass.Bass("TRN2", target_bir_lowering=False,
                   detect_race_conditions=race_checks)
    nc.disable_value_cache = True
    inp = {}

    def di(name, shape, dt=F32):
        inp[name] = nc.dram_tensor(name, shape, dt, kind="ExternalInput").ap()
        return inp[name]

    x_in = di("L0_attn_a_x", [B * T, D])
    for nm in ["ln1_g", "ln1_b", "ln2_g", "ln2_b", "tm_mix_k", "tm_mix_v",
               "tm_mix_r", "time_decay", "time_first", "cm_mix_k", "cm_mix_r"]:
        di("L0_attn_a_" + nm, [1, D])
    di("L0_attn_a_wk", [D, D]); di("L0_attn_a_wv", [D, D])
    di("L0_attn_a_wr", [D, D]); di("L0_attn_a_wo", [D, D])
    di("L0_attn_a_cwk", [D, FCM]); di("L0_attn_a_cwv", [FCM, D])
    di("L0_attn_a_cwr", [D, D])
    di("L0_attn_b_Wq", [D, D]); di("L0_attn_b_Wk", [D, NKV])
    di("L0_attn_b_Wv", [D, NKV]); di("L0_attn_b_Wo", [D, D])
    di("L0_attn_b_bo", [1, D])
    di("L0_ffn_w_gate", [FFN, D], I32); di("L0_ffn_w_up", [FFN, D], I32)
    di("L0_ffn_w_down", [D, FFN], I32)
    di("colpack", [128, 16 * 8])
    di("rope_cos", [128, T]); di("rope_sin", [128, T])
    di("eye_bf16", [128, 128], BF16)
    di("nonce_row", [1, 16], I32)

    out = nc.dram_tensor("out", [QB, D], F32, kind="ExternalOutput").ap()
    whoami = nc.dram_tensor("whoami", [1, 1], F32, kind="ExternalOutput").ap()
    dbg_t = {}
    for name, shape in [("dbg_inner", [HALF, D]), ("dbg_x1", [HALF, D]),
                        ("dbg_x2", [HALF, D]), ("dbg_x3", [QB, D])]:
        if name in dbg:
            dbg_t[name] = nc.dram_tensor(name, shape, F32,
                                         kind="ExternalOutput").ap()

    scr = {
        "inner": nc.dram_tensor("inner_scr", [HALF, D], F32).ap(),
        "x1": nc.dram_tensor("x1_scr", [HALF, D], F32).ap(),
        "x2": nc.dram_tensor("x2_scr", [HALF, D], F32).ap(),
        "x3": nc.dram_tensor("x3_scr", [QB, D], F32).ap(),
        "kk2": nc.dram_tensor("kk2_scr", [FCM // 128, 128, HALF], BF16).ap(),
        "mcr": nc.dram_tensor("mcr_scr", [NC8, 128, HALF], BF16).ap(),
        "hh": nc.dram_tensor("hh_scr", [FFN // 128, 128, HALF], BF16).ap(),
        "wgT": nc.dram_tensor("wgT_scr", [NC8, 128, FFN], BF16).ap(),
        "wuT": nc.dram_tensor("wuT_scr", [NC8, 128, FFN], BF16).ap(),
        "wdT": nc.dram_tensor("wdT_scr", [FFN // 128, 128, D], BF16).ap(),
    }
    psh = {
        "inner": nc.dram_tensor("psh_inner", [T, D], F32, addr_space="Shared").ap(),
        "bnd": nc.dram_tensor("psh_bnd", [2, D], F32, addr_space="Shared").ap(),
        "kT": nc.dram_tensor("psh_kT", [2 * NKV, HALF], BF16, addr_space="Shared").ap(),
        "v": nc.dram_tensor("psh_v", [2 * HALF, NKV], BF16, addr_space="Shared").ap(),
        "flag": nc.dram_tensor("psh_flag", [32, 1], I32, addr_space="Shared").ap(),
    }

    for name, shape, dt in [("dbg_x0t", [128, (TT // 128) * D], F32),
                            ("dbg_h1T", [128, NC8 * (TT + 128)], BF16),
                            ("dbg_mixK", [128, NC8 * TT], BF16),
                            ("dbg_kvr", [128, 3 * TT], F32),
                            ("dbg_rwkv", [128, NC8 * TT], BF16)]:
        if name in dbg:
            dbg_t[name] = nc.dram_tensor(name, shape, dt,
                                         kind="ExternalOutput").ap()
    with contextlib.ExitStack() as stack:
        block = stack.enter_context(nc.Block())
        ee = E(nc, block, stack)
        ee.poll_backoff = poll_backoff
        ee.sem("bar", "g_dma", "s_dma", "c_dma", "v_done", "t_done",
               "sc_done", "g_done", "poll", "scan")

        ee.sb("colpk", [128, 16 * 8])
        ee.sb("dcol", [128, 16 * 8])
        ee.sb("onesT", [128, TT])
        ee.sb("id_t", [128, 1])
        ee.sb("s_col", [128, 1])
        ee.sb("ims_col", [128, 1])
        ee.sb("nonce_t", [1, 16], I32)
        ee.sb("flrd_t", [1, 16], I32)
        ee.sb("carryA", [128, NC8])
        ee.sb("carryB", [128, NC8])
        ee.sb("eye_bf", [128, 128], BF16)
        ee.sb("b1bf", [128, 8], BF16)
        ee.sb("b1m", [128, 24], BF16)
        ee.sb("b1mf", [128, 8])
        ee.sb("cbias", [128, 4])
        ee.sb("bnd_row", [1, D])
        ee.sb("bnd_loc", [1, D])
        ee.sb("tdtf", [128, 8])

        emit_init(ee, inp)
        with nc.named_scope("p0w"):
            with contextlib.ExitStack() as pstack:
                emit_p0w(ee, pstack, inp, scr)
            ee.barrier()
        with nc.named_scope("p1a"):
            with contextlib.ExitStack() as pstack:
                emit_p1a(ee, pstack, inp, x_in, scr, psh, dbg_t)
            ee.barrier()
        if "dbg_inner" in dbg_t:
            ee.G(lambda g: g.dma_start(out=dbg_t["dbg_inner"][:, :],
                                       in_=scr["inner"][:, :]),
                 inc=("g_dma", 16))
            ee.G(lambda g, v=ee.now("g_dma"): g.wait_ge(ee.sems["g_dma"], v))
            ee.barrier()
        with nc.named_scope("p1b1"):
            with contextlib.ExitStack() as pstack:
                emit_p1b1(ee, pstack, inp, x_in, scr, psh)
            ee.barrier()
        with nc.named_scope("p1b2"):
            with contextlib.ExitStack() as pstack:
                emit_p1b2(ee, pstack, inp, x_in, scr, psh)
            ee.barrier()
        if "dbg_x1" in dbg_t:
            ee.G(lambda g: g.dma_start(out=dbg_t["dbg_x1"][:, :],
                                       in_=scr["x1"][:, :]),
                 inc=("g_dma", 16))
            ee.G(lambda g, v=ee.now("g_dma"): g.wait_ge(ee.sems["g_dma"], v))
            ee.barrier()
        with nc.named_scope("p2"):
            with contextlib.ExitStack() as pstack:
                emit_p2(ee, pstack, inp, scr)
            ee.barrier()
        if "dbg_x2" in dbg_t:
            ee.G(lambda g: g.dma_start(out=dbg_t["dbg_x2"][:, :],
                                       in_=scr["x2"][:, :]),
                 inc=("g_dma", 16))
            ee.G(lambda g, v=ee.now("g_dma"): g.wait_ge(ee.sems["g_dma"], v))
            ee.barrier()
        with nc.named_scope("p3"):
            emit_p3(ee, inp, scr, psh)
            ee.barrier()
        if "dbg_x3" in dbg_t:
            ee.G(lambda g: g.dma_start(out=dbg_t["dbg_x3"][:, :],
                                       in_=scr["x3"][:, :]),
                 inc=("g_dma", 16))
            ee.G(lambda g, v=ee.now("g_dma"): g.wait_ge(ee.sems["g_dma"], v))
            ee.barrier()
        with nc.named_scope("p4"):
            with contextlib.ExitStack() as pstack:
                emit_p4(ee, pstack, inp, scr, out)
            ee.barrier()
        emit_whoami(ee, whoami)
    return nc



def emit_init(ee, inp):
    nc = ee.nc

    def g_init(g):
        r = {}
        for nm in ["r_tgt", "r_poll", "r_pollcnt"]:
            r[nm] = g.alloc_register(nm)
        g.reg_mov(r["r_pollcnt"], 0)
        ee.regs.update(r)
        for arm in nc.switch_core_id(engines=[g], n=128):
            pnc = int(arm) % 8
            batch, s, h2 = _role(pnc)
            g.memset(ee.t["id_t"][:], float(pnc))
            g.memset(ee.t["s_col"][:], float(s))
            g.memset(ee.t["ims_col"][:], float(1 - s))
    ee.block.gpsimd(g_init)

    ee.G(lambda g: g.dma_start(out=ee.t["colpk"][:], in_=inp["colpack"][:, :]),
         inc=("g_dma", 16))
    ee.G(lambda g: g.dma_start(out=ee.t["nonce_t"][:], in_=inp["nonce_row"][:, :]),
         inc=("g_dma", 16))

    def gtdtf(g, pnc):
        cg = pnc % 2
        g.dma_start(
            out=ee.t["tdtf"][:, 0:4],
            in_=inp["colpack"][:, CSLOT["td"] * 8 + cg * 4:
                               CSLOT["td"] * 8 + cg * 4 + 4]).then_inc(
            ee.sems["g_dma"], 16)
        return g.dma_start(
            out=ee.t["tdtf"][:, 4:8],
            in_=inp["colpack"][:, CSLOT["tf"] * 8 + cg * 4:
                               CSLOT["tf"] * 8 + cg * 4 + 4])
    ee.Gsw(gtdtf, inc=("g_dma", 16))
    ee.cnt["g_dma"] += 16
    ee.S(lambda e: e.dma_start(out=ee.t["eye_bf"][:], in_=inp["eye_bf16"][:, :]),
         inc=("s_dma", 16))

    def v0(v):
        v.memset(ee.t["onesT"][:], 1.0)
        v.memset(ee.t["cbias"][:, 0:1], EPS * EPS)
        v.memset(ee.t["cbias"][:, 1:2], EPS)
        v.memset(ee.t["cbias"][:, 2:3], -16.0)
        v.memset(ee.t["bnd_row"][:], 0.0)
        v.memset(ee.t["carryA"][:], 0.0)
        v.memset(ee.t["carryB"][:], 0.0)
        v.memset(ee.t["bnd_loc"][:], 0.0)
        v.memset(ee.t["dcol"][:], 0.0)
        v.tensor_copy(out=ee.t["b1bf"][:],
                      in_=ee.t["colpk"][:, CSLOT["b1"] * 8:CSLOT["b1"] * 8 + 8])
        for mi, mname in enumerate(("mk", "mv", "mr")):
            v.tensor_tensor(out=ee.t["b1mf"][:],
                            in0=ee.t["colpk"][:, CSLOT["b1"] * 8:CSLOT["b1"] * 8 + 8],
                            in1=ee.t["colpk"][:, CSLOT[mname] * 8:CSLOT[mname] * 8 + 8],
                            op=ALU.mult)
            v.tensor_tensor(out=ee.t["b1mf"][:],
                            in0=ee.t["colpk"][:, CSLOT["b1"] * 8:CSLOT["b1"] * 8 + 8],
                            in1=ee.t["b1mf"][:], op=ALU.subtract)
            v.tensor_copy(out=ee.t["b1m"][:, mi * 8:(mi + 1) * 8], in_=ee.t["b1mf"][:])
        return v.memset(ee.t["lowbuf"][:], 0.0) if False else v.memset(ee.t["b1mf"][:], 0.0)
    ee.V(v0, waits=ee.wv("g_dma", "s_dma"))

    def sc0(sc):
        dc = ee.t["dcol"]
        sc.activation(out=dc[:, 0:4], in_=ee.t["tdtf"][:, 0:4], func=AF.Exp)
        sc.activation(out=dc[:, 0:4], in_=dc[:, 0:4], func=AF.Exp, scale=-1.0)
        return sc.activation(out=dc[:, 8:12], in_=ee.t["tdtf"][:, 4:8],
                             func=AF.Exp)
    ee.Sc(sc0, waits=ee.wv("v_done", "g_dma"))
    ee.barrier()


def _emit_poll(ee, g, psh, col, want_side):
    r = ee.regs
    sems = ee.sems
    g.reg_load(r["r_tgt"], ee.t["nonce_t"][0:1, col:col + 1])
    g.reg_mov(r["r_poll"], 1)
    with g.While(r["r_poll"]):
        if ee.poll_backoff:
            g.nop(cycle_cnt=6000)
        g.reg_alu(r["r_pollcnt"], r["r_pollcnt"], 16, ALU.add)
        g.dma_start(out=ee.t["flrd_t"][:, col:col + 1],
                    in_=psh["flag"][want_side * 16 + col:want_side * 16 + col + 1,
                                    :]).then_inc(sems["poll"], 16)
        g.wait_ge(sems["poll"], r["r_pollcnt"])
        g.reg_load(r["r_poll"], ee.t["flrd_t"][0:1, col:col + 1])
        g.reg_alu(r["r_poll"], r["r_poll"], r["r_tgt"], ALU.not_equal)


def _send_flag(ee, psh, col):
    def gf(g, pnc):
        s = pnc % 2
        return g.dma_start(out=psh["flag"][s * 16 + col:s * 16 + col + 1, :],
                           in_=ee.t["nonce_t"][0:1, col:col + 1])
    ee.Gsw(gf, inc=("g_dma", 16))
    ee.G(lambda g, v=ee.now("g_dma"): g.wait_ge(ee.sems["g_dma"], v))



def emit_p1a(ee, stack, inp, x_in, scr, psh, dbg_t=()):
    nc = ee.nc
    sems = ee.sems
    r = ee.regs
    NTILE = T // TT
    HTW = TT + 128
    NOC = 4
    W512 = 512
    WO_OFF = 3 * NC8 * W512

    def sb(name, shape, dt=F32):
        h = stack.enter_context(nc.sbuf_tensor(name, shape, dt))
        ee.t[name] = h
        return h

    def ps(name, shape, dt=F32):
        h = stack.enter_context(nc.psum_tensor(name, shape, dt))
        ee.t[name] = h
        return h

    w4 = sb("w4", [128, WO_OFF + NOC * D], BF16)
    wld = sb("wld", [128, D])
    rawbf = sb("rawbf", [128, W512], BF16)
    h1T = sb("h1T", [128, NC8 * HTW], BF16)
    mixK = sb("mixK", [128, NC8 * TT], BF16)
    mixV = sb("mixV", [128, NC8 * TT], BF16)
    mixR = sb("mixR", [128, NC8 * TT], BF16)
    rwkv = sb("rwkv", [128, NOC * TT], BF16)
    x0t = sb("x0t", [128, (TT // 128) * D])
    h1tm = sb("h1tm", [128, D], BF16)
    ddsc = sb("ddsc", [128, TT], BF16)
    lam_s = sb("lam_s", [128, TT])
    kc = sb("kc", [128, TT]); vcx = sb("vcx", [128, TT])
    rpre = sb("rpre", [128, NOC * TT], BF16)
    ekc = sb("ekc", [128, TT]); ekvc = sb("ekvc", [128, TT])
    t1c = sb("t1c", [128, TT])
    denb = sb("denb1", [128, NOC * TT])
    At = sb("At", [128, TT + 1])
    Bt = sb("Bt", [128, TT + 1])
    inner_t = sb("inner_t", [128, (TT // 128) * D])
    stat6 = sb("stat6", [128, 12])
    stat2 = sb("stat2", [128, 2])
    stc = sb("stc", [128, 8])
    stb = sb("stb", [128, 26])
    psA = ps("psA", [128, TT])
    psB = ps("psB", [128, TT])
    psT = ps("psT", [128, 128], BF16)
    psO = ps("psO", [128, D])
    psbias = ps("psbias", [128, 48])

    ee.V(lambda v: v.memset(h1T[:], 0.0))

    WN = [("L0_attn_a_wk", "g1"), ("L0_attn_a_wv", "g1"),
          ("L0_attn_a_wr", "g1")]
    for wi, (iname, gname) in enumerate(WN):
        for c in range(NC8):
            def gwld(g, pnc, iname=iname, c=c):
                cg = pnc % 2
                return g.dma_start(
                    out=wld[:, 0:W512],
                    in_=inp[iname][c * 128:(c + 1) * 128,
                                   cg * W512:(cg + 1) * W512])
            ee.Gsw(gwld, waits=ee.wv("v_done"), inc=("g_dma", 16))
            ee.V(lambda v: v.tensor_copy(out=rawbf[:], in_=wld[:, 0:W512]),
                 waits=ee.wv("g_dma", "t_done"))

            def tbias(t, wi=wi, c=c):
                mm = None
                for ec in range(NOC):
                    first = wi == 0 and c == 0 and ec == 0
                    last = wi == 2 and c == NC8 - 1 and ec == NOC - 1
                    mm = t.matmul(psbias[:, wi * 8 + ec:wi * 8 + ec + 1],
                                  rawbf[:, ec * 128:(ec + 1) * 128],
                                  ee.t["b1bf"][:, c:c + 1],
                                  start=first, stop=last,
                                  skip_group_check=True)
                    mm = t.matmul(psbias[:, 24 + wi * 8 + ec:24 + wi * 8 + ec + 1],
                                  rawbf[:, ec * 128:(ec + 1) * 128],
                                  ee.t["b1m"][:, wi * 8 + c:wi * 8 + c + 1],
                                  start=False, stop=last,
                                  skip_group_check=True)
                return mm
            ee.Tn(tbias, waits=ee.wv("v_done"), inc=("t_done", 1))

            def vcast(v, wi=wi, c=c, gname=gname):
                dst = w4[:, (wi * NC8 + c) * W512:(wi * NC8 + c + 1) * W512]
                return v.tensor_scalar(out=dst, in0=wld[:, 0:W512],
                                       scalar1=cslot(ee, gname, c),
                                       scalar2=None, op0=ALU.mult)
            ee.V(vcast, waits=ee.wv("g_dma"))
    for rc in range(NOC):
        def gwo(g, pnc, rc=rc):
            cg = pnc % 2
            rb = cg * W512 + rc * 128
            return g.dma_start(out=wld[:],
                               in_=inp["L0_attn_a_wo"][rb:rb + 128, :])
        ee.Gsw(gwo, waits=ee.wv("v_done"), inc=("g_dma", 16))
        ee.V(lambda v, rc=rc: v.tensor_copy(
            out=w4[:, WO_OFF + rc * D:WO_OFF + (rc + 1) * D], in_=wld[:]),
            waits=ee.wv("g_dma"))
    def sc_bias_ev(s):
        last = None
        for wi in range(3):
            s.copy(out=ee.t["dcol"][:, 16 + wi * 8:16 + wi * 8 + NOC],
                   in_=psbias[:, wi * 8:wi * 8 + NOC])
            last = s.copy(out=ee.t["dcol"][:, 48 + wi * 8:48 + wi * 8 + NOC],
                          in_=psbias[:, 24 + wi * 8:24 + wi * 8 + NOC])
        return last
    ee.Sc(sc_bias_ev, waits=ee.wv("t_done"))


    ee.V(lambda v: v.tensor_copy(out=lam_s[:], in_=ee.t["onesT"][:]),
         waits=ee.wv("sc_done"))

    for t_i in range(NTILE):
        last_tile = t_i == NTILE - 1

        def gx(g, pnc, t_i=t_i):
            batch, s, h2 = _role(pnc)
            xb = batch * T + t_i * TT
            return g.dma_start(
                out=x0t[:].rearrange("p (c d) -> p c d", d=D),
                in_=x_in[xb:xb + TT, :].rearrange("(c p) d -> p c d", p=128))
        ee.Gsw(gx, waits=ee.wv("v_done"), inc=("g_dma", 16))
        x_loaded = ee.now("g_dma")

        if t_i > 0:
            def vbcol(v):
                last = None
                for dck in range(NC8):
                    last = v.tensor_copy(
                        out=h1T[:, dck * HTW + 127:dck * HTW + 128],
                        in_=h1T[:, dck * HTW + 128 + TT - 1:dck * HTW + 128 + TT])
                return last
            ee.V(vbcol)

        rowchunks = list(range(TT // 128))
        nrc = len(rowchunks)
        off = 0
        for i, rcix in enumerate(rowchunks):
            src = x0t[:, rcix * D:(rcix + 1) * D]

            def vstat(v, src=src, i=i):
                v.bn_stats(out=stat6[:, 0:6], in_=src[:, 0:512])
                v.bn_stats(out=stat6[:, 6:12], in_=src[:, 512:1024])
                return v.bn_aggr(out=stb[:, 2 * i:2 * i + 2], in_=stat6[:])
            ee.V(vstat, waits=[("g_dma", x_loaded),
                               ("sc_done", ee.now("sc_done"))])

        def scst_all(s, nrc=nrc, off=off):
            s.activation(out=stb[:, 12:12 + nrc], in_=stb[:, 1:2 * nrc:2],
                         func=AF.Ln, scale=(1.0 + EPS), bias=ee.t["cbias"][:, 0:1])
            s.activation(out=stb[:, 12:12 + nrc], in_=stb[:, 12:12 + nrc],
                         func=AF.Exp, scale=-0.5)
            s.activation(out=stb[:, 18:22], in_=stb[:, 2 * off + 1:2 * off + 8:2],
                         func=AF.Ln, bias=ee.t["cbias"][:, 1:2])
            return s.activation(out=stb[:, 18:22], in_=stb[:, 18:22],
                                func=AF.Exp, scale=-0.5)
        ee.Sc(scst_all, waits=ee.wv("v_done"))
        gname0 = "ims_col" if t_i < NTILE // 2 else "s_col"
        ee.V(lambda v, gname0=gname0: v.tensor_scalar(
            out=stb[:, 22:26], in0=stb[:, 18:22],
            scalar1=ee.t[gname0][:, 0:1], scalar2=None, op0=ALU.mult),
            waits=ee.wv("sc_done"))
        stats_done = ee.now("sc_done")

        for i, rcix in enumerate(rowchunks):
            src = x0t[:, rcix * D:(rcix + 1) * D]
            ee.V(lambda v, src=src, i=i: v.tensor_scalar(
                out=h1tm[:], in0=src, scalar1=stb[:, 2 * i:2 * i + 1],
                scalar2=stb[:, 12 + i:13 + i],
                op0=ALU.subtract, op1=ALU.mult),
                waits=[("sc_done", stats_done), ("t_done", ee.now("t_done"))])
            vdn = ee.now("v_done")
            for dck in range(NC8):
                ee.Tn(lambda t, dck=dck:
                      t.transpose(psT[:], h1tm[:, dck * 128:(dck + 1) * 128],
                                  ee.t["eye_bf"][:]),
                      waits=[("v_done", max(vdn, ee.now("v_done")))],
                      inc=("t_done", 1))
                dst = h1T[:, dck * HTW + 128 + rcix * 128:
                          dck * HTW + 128 + (rcix + 1) * 128]
                ee.V(lambda v, dst=dst: v.tensor_copy(out=dst, in_=psT[:]),
                     waits=ee.wv("t_done"))

        if t_i == 0 and "dbg_x0t" in dbg_t:
            ee.G(lambda g: g.dma_start(out=dbg_t["dbg_x0t"][:, :], in_=x0t[:]),
                 waits=[("g_dma", x_loaded)], inc=("g_dma", 16))
            ee.G(lambda g, v=ee.now("g_dma"): g.wait_ge(sems["g_dma"], v))
            ee.barrier()
        if t_i == 0 and "dbg_h1T" in dbg_t:
            ee.G(lambda g: g.dma_start(out=dbg_t["dbg_h1T"][:, :], in_=h1T[:]),
                 waits=ee.wv("v_done"), inc=("g_dma", 16))
            ee.G(lambda g, v=ee.now("g_dma"): g.wait_ge(sems["g_dma"], v))
            ee.barrier()

        def vmix(v):
            last = None
            for dck in range(NC8):
                h = h1T[:, dck * HTW + 128:dck * HTW + 128 + TT]
                hx = h1T[:, dck * HTW + 127:dck * HTW + 127 + TT]
                v.tensor_tensor(out=ddsc[:], in0=h, in1=hx, op=ALU.subtract)
                for mname, mt in (("mk", mixK), ("mv", mixV), ("mr", mixR)):
                    last = v.scalar_tensor_tensor(
                        out=mt[:, dck * TT:(dck + 1) * TT], in0=ddsc[:],
                        scalar=cslot(ee, mname, dck), in1=hx,
                        op0=ALU.mult, op1=ALU.add)
            return last
        ee.V(vmix, waits=ee.wv("t_done"))
        mix_done = ee.now("v_done")

        if t_i == 0 and "dbg_mixK" in dbg_t:
            ee.G(lambda g: g.dma_start(out=dbg_t["dbg_mixK"][:, :], in_=mixK[:]),
                 waits=ee.wv("v_done"), inc=("g_dma", 16))
            ee.G(lambda g, v=ee.now("g_dma"): g.wait_ge(sems["g_dma"], v))
            ee.barrier()

        prev_r_evac = ee.now("sc_done")
        for oc in range(NOC):
            def tk(t, oc=oc):
                mm = None
                for dck in range(NC8):
                    mm = t.matmul(psA[:], w4[:, (0 * NC8 + dck) * W512 + oc * 128:
                                              (0 * NC8 + dck) * W512 + (oc + 1) * 128],
                                  mixK[:, dck * TT:(dck + 1) * TT],
                                  start=(dck == 0), stop=(dck == NC8 - 1))
                return mm
            ee.Tn(tk, waits=[("v_done", mix_done), ("sc_done", prev_r_evac)],
                  inc=("t_done", 1))
            ee.Sc(lambda s, oc=oc: s.activation(out=kc[:], in_=psA[:],
                                                func=AF.Identity,
                                                bias=dslot(ee, "bk", oc)),
                  waits=ee.wv("t_done", "v_done"))
            if t_i == 0:
                ee.V(lambda v, oc=oc: v.tensor_tensor(
                    out=kc[:, 0:1], in0=kc[:, 0:1], in1=dslot(ee, "ck", oc),
                    op=ALU.subtract),
                    waits=ee.wv("sc_done"))
            k_ev = ee.now("sc_done")

            def tv(t, oc=oc):
                mm = None
                for dck in range(NC8):
                    mm = t.matmul(psB[:], w4[:, (1 * NC8 + dck) * W512 + oc * 128:
                                              (1 * NC8 + dck) * W512 + (oc + 1) * 128],
                                  mixV[:, dck * TT:(dck + 1) * TT],
                                  start=(dck == 0), stop=(dck == NC8 - 1))
                return mm
            ee.Tn(tv, waits=[("sc_done", k_ev)], inc=("t_done", 1))
            ee.Sc(lambda s, oc=oc: s.activation(out=vcx[:], in_=psB[:],
                                                func=AF.Identity,
                                                bias=dslot(ee, "bv", oc)),
                  waits=ee.wv("t_done", "v_done"))
            if t_i == 0:
                ee.V(lambda v, oc=oc: v.tensor_tensor(
                    out=vcx[:, 0:1], in0=vcx[:, 0:1], in1=dslot(ee, "cv", oc),
                    op=ALU.subtract),
                    waits=ee.wv("sc_done"))
            v_ev = ee.now("sc_done")

            def tr(t, oc=oc):
                mm = None
                for dck in range(NC8):
                    mm = t.matmul(psA[:], w4[:, (2 * NC8 + dck) * W512 + oc * 128:
                                              (2 * NC8 + dck) * W512 + (oc + 1) * 128],
                                  mixR[:, dck * TT:(dck + 1) * TT],
                                  start=(dck == 0), stop=(dck == NC8 - 1))
                return mm
            ee.Tn(tr, waits=[("sc_done", k_ev)], inc=("t_done", 1))
            if t_i == 0:
                ee.Sc(lambda s, oc=oc: s.activation(out=t1c[:], in_=psA[:],
                                                    func=AF.Identity,
                                                    bias=dslot(ee, "br", oc)),
                      waits=ee.wv("t_done", "v_done"))
                def vfixr(v, oc=oc):
                    v.tensor_tensor(out=t1c[:, 0:1], in0=t1c[:, 0:1],
                                    in1=dslot(ee, "cr", oc), op=ALU.subtract)
                    return v.tensor_copy(out=rpre[:, oc * TT:(oc + 1) * TT],
                                         in_=t1c[:])
                ee.V(vfixr, waits=ee.wv("sc_done"))
                prev_r_evac = ee.now("sc_done")
            else:
                ee.Sc(lambda s, oc=oc: s.activation(
                    out=rpre[:, oc * TT:(oc + 1) * TT], in_=psA[:],
                    func=AF.Identity, bias=dslot(ee, "br", oc)),
                      waits=ee.wv("t_done"))
                prev_r_evac = ee.now("sc_done")

            ee.V(lambda v, oc=oc: v.tensor_scalar(
                out=lam_s[:], in0=ee.t["onesT"][:], scalar1=dslot(ee, "lam", oc),
                scalar2=None, op0=ALU.mult))
            ee.Sc(lambda s: s.activation(out=ekc[:], in_=kc[:], func=AF.Exp),
                  waits=[("sc_done", k_ev), ("v_done", ee.now("v_done"))])
            ek_done = ee.now("sc_done")
            ee.V(lambda v: v.tensor_tensor(out=ekvc[:], in0=ekc[:], in1=vcx[:],
                                           op=ALU.mult),
                 waits=[("sc_done", max(ek_done, v_ev))])

            def vscanA(v, oc=oc):
                return v.tensor_tensor_scan(
                    out=At[:, 1:1 + TT], data0=lam_s[:],
                    data1=ekvc[:], initial=ee.t["carryA"][:, oc:oc + 1],
                    op0=ALU.mult, op1=ALU.add)
            ee.V(vscanA, waits=ee.wv("v_done"))

            def vscanB(v, oc=oc):
                return v.tensor_tensor_scan(
                    out=Bt[:, 1:1 + TT], data0=lam_s[:],
                    data1=ekc[:], initial=ee.t["carryB"][:, oc:oc + 1],
                    op0=ALU.mult, op1=ALU.add)
            ee.V(vscanB, waits=[("sc_done", ek_done)])

            def vcarry(v, oc=oc):
                v.tensor_copy(out=At[:, 0:1], in_=ee.t["carryA"][:, oc:oc + 1])
                v.tensor_copy(out=Bt[:, 0:1], in_=ee.t["carryB"][:, oc:oc + 1])
                v.tensor_copy(out=ee.t["carryA"][:, oc:oc + 1],
                              in_=At[:, TT:TT + 1])
                return v.tensor_copy(out=ee.t["carryB"][:, oc:oc + 1],
                                     in_=Bt[:, TT:TT + 1])
            ee.V(vcarry)

            def vnum(v, oc=oc):
                v.scalar_tensor_tensor(out=t1c[:], in0=ekc[:],
                                       scalar=dslot(ee, "eu", oc), in1=vcx[:],
                                       op0=ALU.mult, op1=ALU.mult)
                v.tensor_tensor(out=rwkv[:, oc * TT:(oc + 1) * TT], in0=t1c[:],
                                in1=At[:, 0:TT], op=ALU.add)
                return v.scalar_tensor_tensor(
                    out=denb[:, oc * TT:(oc + 1) * TT], in0=ekc[:],
                    scalar=dslot(ee, "eu", oc),
                    in1=Bt[:, 0:TT], op0=ALU.mult, op1=ALU.add)
            ee.V(vnum, waits=ee.wv("v_done"))
        def scpost(s):
            s.activation(out=denb[:], in_=denb[:], func=AF.Ln)
            s.activation(out=denb[:], in_=denb[:], func=AF.Exp, scale=-1.0)
            return s.activation(out=rpre[:], in_=rpre[:], func=AF.Tanh,
                                scale=0.5)
        ee.Sc(scpost, waits=ee.wv("v_done"))

        def vrfin(v):
            v.tensor_scalar(out=rpre[:], in0=rpre[:], scalar1=0.5, scalar2=0.5,
                            op0=ALU.mult, op1=ALU.add)
            last = None
            for oc2 in range(NOC):
                sl = slice(oc2 * TT, (oc2 + 1) * TT)
                v.tensor_tensor(out=rwkv[:, sl], in0=rwkv[:, sl],
                                in1=denb[:, sl], op=ALU.mult)
                last = v.tensor_tensor(out=rwkv[:, sl], in0=rwkv[:, sl],
                                       in1=rpre[:, sl], op=ALU.mult)
            return last
        ee.V(vrfin, waits=ee.wv("sc_done"))
        rwkv_done = ee.now("v_done")
        if t_i == 0 and "dbg_rwkv" in dbg_t:
            ee.G(lambda g: g.dma_start(out=dbg_t["dbg_rwkv"][:, :], in_=rwkv[:]),
                 waits=ee.wv("v_done"), inc=("g_dma", 16))
            ee.G(lambda g, v=ee.now("g_dma"): g.wait_ge(sems["g_dma"], v))
            ee.barrier()

        for ts in range(TT // 128):
            def two(t, ts=ts):
                mm = None
                for dck in range(NOC):
                    lhs = rwkv[:, dck * TT + ts * 128:dck * TT + (ts + 1) * 128]
                    for nh in range(2):
                        mm = t.matmul(psO[:, nh * 512:(nh + 1) * 512], lhs,
                                      w4[:, WO_OFF + dck * D + nh * 512:
                                         WO_OFF + dck * D + (nh + 1) * 512],
                                      start=(dck == 0), stop=(dck == NOC - 1))
                return mm
            ee.Tn(two, waits=[("v_done", max(rwkv_done, ee.now("v_done")))],
                  inc=("t_done", 1))

            def vinner(v, ts=ts):
                v.tensor_scalar(out=inner_t[:, ts * D:(ts + 1) * D],
                                in0=x0t[:, ts * D:(ts + 1) * D],
                                scalar1=stb[:, 2 * ts:2 * ts + 1],
                                scalar2=stb[:, 22 + ts:23 + ts],
                                op0=ALU.subtract, op1=ALU.mult)
                return v.tensor_tensor(out=inner_t[:, ts * D:(ts + 1) * D],
                                       in0=inner_t[:, ts * D:(ts + 1) * D],
                                       in1=psO[:], op=ALU.add)
            ee.V(vinner, waits=ee.wv("t_done", "g_dma"))

        def ginner(g, pnc, t_i=t_i):
            batch, s, h2 = _role(pnc)
            if (t_i // (NTILE // 2)) == s:
                dst = scr["inner"][(t_i - s * (NTILE // 2)) * TT:
                                   (t_i - s * (NTILE // 2) + 1) * TT, :]
            else:
                dst = psh["inner"][t_i * TT:(t_i + 1) * TT, :]
            return g.dma_start(
                out=dst.rearrange("(c p) d -> p c d", p=128),
                in_=inner_t[:].rearrange("p (c d) -> p c d", d=D))
        ee.Gsw(ginner, waits=ee.wv("v_done"), inc=("g_dma", 16))

        if t_i == NTILE // 2 - 1:
            ee.G(lambda g, v=ee.now("g_dma"): g.wait_ge(sems["g_dma"], v))

            def gbnd(g, pnc):
                s = pnc % 2
                return g.dma_start(
                    out=psh["bnd"][s:s + 1, :],
                    in_=inner_t[127:128, (TT // 128 - 1) * D:(TT // 128) * D])
            ee.Gsw(gbnd, waits=ee.wv("v_done"), inc=("g_dma", 16))
            ee.G(lambda g, v=ee.now("g_dma"): g.wait_ge(sems["g_dma"], v))
            _send_flag(ee, psh, FL_BND)
            _send_flag(ee, psh, FL_INA)

        if last_tile:
            ee.G(lambda g, v=ee.now("g_dma"): g.wait_ge(sems["g_dma"], v))
            _send_flag(ee, psh, FL_INB)

    ee.G(lambda g, v=ee.now("g_dma"): g.wait_ge(sems["g_dma"], v))


def emit_whoami(ee, whoami):
    sems = ee.sems
    ee.G(lambda g: g.dma_start(out=whoami[:, :], in_=ee.t["id_t"][0:1, 0:1]),
         inc=("g_dma", 16))
    ee.G(lambda g, v=ee.now("g_dma"): g.wait_ge(sems["g_dma"], v))



def emit_p0w(ee, stack, inp, scr):
    nc = ee.nc

    def sb(name, shape, dt=F32):
        h = stack.enter_context(nc.sbuf_tensor(name, shape, dt))
        ee.t[name] = h
        return h

    def ps(name, shape, dt=F32):
        h = stack.enter_context(nc.psum_tensor(name, shape, dt))
        ee.t[name] = h
        return h

    wi32 = [sb("wi32%d" % p, [128, FFN], I32) for p in range(2)]
    wbfc = [sb("wbfc%d" % p, [128, FFN], BF16) for p in range(2)]
    tstage = [sb("tstage%d" % p, [128, FFN], BF16) for p in range(2)]
    psTw = [ps("psTw%d" % p, [128, FFN], BF16) for p in range(2)]

    items = []
    for wname, sname in (("L0_ffn_w_gate", "wgT"), ("L0_ffn_w_up", "wuT")):
        for hc in range(FFN // 128):
            items.append((wname, hc, NC8, sname))
    for ec in range(NC8):
        items.append(("L0_ffn_w_down", ec, FFN // 128, "wdT"))

    cast_done = [0, 0]
    store_done = [0, 0]
    stage_done = [0, 0]
    for idx, (wname, rc, nblk, sname) in enumerate(items):
        p = idx % 2
        ncol = nblk * 128
        ee.S(lambda e, wname=wname, rc=rc, ncol=ncol, p=p: e.dma_start(
            out=wi32[p][:, 0:ncol],
            in_=inp[wname][rc * 128:(rc + 1) * 128, :]),
            waits=[("v_done", cast_done[p])], inc=("s_dma", 16))
        ee.V(lambda v, p=p, ncol=ncol: v.tensor_copy(
            out=wbfc[p][:, 0:ncol], in_=wi32[p][:, 0:ncol]),
            waits=ee.wv("s_dma"))
        cast_done[p] = ee.now("v_done")

        def ttr(t, p=p, nblk=nblk):
            last = None
            for j in range(nblk):
                last = t.transpose(psTw[p][:, j * 128:(j + 1) * 128],
                                   wbfc[p][:, j * 128:(j + 1) * 128],
                                   ee.t["eye_bf"][:])
            return last
        ee.Tn(ttr, waits=[("v_done", cast_done[p]),
                          ("v_done", stage_done[p])], inc=("t_done", 1))
        ee.V(lambda v, p=p, ncol=ncol: v.tensor_copy(
            out=tstage[p][:, 0:ncol], in_=psTw[p][:, 0:ncol]),
            waits=ee.wv("t_done") + [("g_dma", store_done[p])])
        stage_done[p] = ee.now("v_done")

        ee.G(lambda g, sname=sname, rc=rc, p=p, ncol=ncol: g.dma_start(
            out=scr[sname][:, :, rc * 128:(rc + 1) * 128]
            .rearrange("c p h -> p c h"),
            in_=tstage[p][:, 0:ncol].rearrange("p (c h) -> p c h", h=128)),
            waits=ee.wv("v_done"), inc=("g_dma", 16))
        store_done[p] = ee.now("g_dma")
    ee.G(lambda g, v=ee.now("g_dma"): g.wait_ge(ee.sems["g_dma"], v))



def emit_p1b1(ee, stack, inp, x_in, scr, psh):
    nc = ee.nc
    sems = ee.sems
    NTILE = HALF // TT
    HTW = TT + 128
    NFC = FCM // 128

    def sb(name, shape, dt=F32):
        h = stack.enter_context(nc.sbuf_tensor(name, shape, dt))
        ee.t[name] = h
        return h

    def ps(name, shape, dt=F32):
        h = stack.enter_context(nc.psum_tensor(name, shape, dt))
        ee.t[name] = h
        return h

    cwkb = sb("cwkb", [128, NC8 * FCM], BF16)
    wld2 = sb("wld2", [128, FCM])
    rawb2 = sb("rawb2", [128, FCM], BF16)
    dcol2 = sb("dcol2", [128, 128])
    b2bf = sb("b2bf", [128, 8], BF16)
    b2m = sb("b2m", [128, 16], BF16)
    h2T = sb("h2T", [128, NC8 * HTW], BF16)
    mixCK = sb("mixCK", [128, NC8 * TT], BF16)
    mixCR = sb("mixCR", [128, NC8 * TT], BF16)
    inn_t = sb("inn_t", [128, (TT // 128) * D])
    h2tm = sb("h2tm", [128, D], BF16)
    ddsc2 = sb("ddsc2", [128, TT], BF16)
    kkbuf = sb("kkbuf", [128, NFC * TT], BF16)
    krel = sb("krel", [128, TT])
    stat6b = sb("stat6b", [128, 12])
    stat2b = sb("stat2b", [128, 2])
    stcb = sb("stcb", [128, 8])
    stcb2 = sb("stcb2", [128, 12])
    bndh2 = sb("bndh2", [1, D], BF16)
    bndst = sb("bndst", [1, 16])
    psA2 = ps("psA2", [128, TT])
    psT2 = ps("psT2", [128, 128], BF16)
    psb2 = ps("psb2", [128, 80])

    def vb2(v):
        v.memset(h2T[:], 0.0)
        v.tensor_copy(out=b2bf[:],
                      in_=ee.t["colpk"][:, CSLOT["b2"] * 8:CSLOT["b2"] * 8 + 8])
        for mi, mname in enumerate(("cmk", "cmr")):
            v.tensor_tensor(out=ee.t["b1mf"][:],
                            in0=ee.t["colpk"][:, CSLOT["b2"] * 8:CSLOT["b2"] * 8 + 8],
                            in1=ee.t["colpk"][:, CSLOT[mname] * 8:CSLOT[mname] * 8 + 8],
                            op=ALU.mult)
            v.tensor_tensor(out=ee.t["b1mf"][:],
                            in0=ee.t["colpk"][:, CSLOT["b2"] * 8:CSLOT["b2"] * 8 + 8],
                            in1=ee.t["b1mf"][:], op=ALU.subtract)
            v.tensor_copy(out=b2m[:, mi * 8:(mi + 1) * 8], in_=ee.t["b1mf"][:])
        return v.memset(bndh2[:], 0.0)
    ee.V(vb2)

    for c in range(NC8):
        ee.S(lambda e, c=c: e.dma_start(
            out=wld2[:], in_=inp["L0_attn_a_cwk"][c * 128:(c + 1) * 128, :]),
            waits=ee.wv("v_done"), inc=("s_dma", 16))
        ee.V(lambda v: v.tensor_copy(out=rawb2[:], in_=wld2[:]),
             waits=ee.wv("s_dma", "t_done"))

        def tb2(t, c=c):
            mm = None
            for fc in range(NFC):
                mm = t.matmul(psb2[:, fc:fc + 1],
                              rawb2[:, fc * 128:(fc + 1) * 128],
                              b2bf[:, c:c + 1],
                              start=(c == 0 and fc == 0), stop=False,
                              skip_group_check=True)
                mm = t.matmul(psb2[:, 32 + fc:32 + fc + 1],
                              rawb2[:, fc * 128:(fc + 1) * 128],
                              b2m[:, c:c + 1],
                              start=False, stop=False,
                              skip_group_check=True)
            return mm
        ee.Tn(tb2, waits=ee.wv("v_done"), inc=("t_done", 1))
        ee.V(lambda v, c=c: v.tensor_scalar(
            out=cwkb[:, c * FCM:(c + 1) * FCM], in0=wld2[:],
            scalar1=cslot(ee, "g2", c), scalar2=None, op0=ALU.mult),
            waits=ee.wv("s_dma"))
    cwrb = sb("cwrb", [128, NC8 * D], BF16)
    for c in range(NC8):
        ee.S(lambda e, c=c: e.dma_start(
            out=wld2[:, 0:D], in_=inp["L0_attn_a_cwr"][c * 128:(c + 1) * 128, :]),
            waits=ee.wv("v_done"), inc=("s_dma", 16))
        ee.V(lambda v: v.tensor_copy(out=rawb2[:, 0:D], in_=wld2[:, 0:D]),
             waits=ee.wv("s_dma", "t_done"))

        def tb3(t, c=c):
            mm = None
            for ec in range(NC8):
                mm = t.matmul(psb2[:, 64 + ec:64 + ec + 1],
                              rawb2[:, ec * 128:(ec + 1) * 128],
                              b2bf[:, c:c + 1],
                              start=False,
                              stop=False, skip_group_check=True)
                mm = t.matmul(psb2[:, 72 + ec:72 + ec + 1],
                              rawb2[:, ec * 128:(ec + 1) * 128],
                              b2m[:, 8 + c:8 + c + 1],
                              start=False,
                              stop=(c == NC8 - 1 and ec == NC8 - 1),
                              skip_group_check=True)
            return mm
        ee.Tn(tb3, waits=ee.wv("v_done"), inc=("t_done", 1))
        ee.V(lambda v, c=c: v.tensor_scalar(
            out=cwrb[:, c * D:(c + 1) * D], in0=wld2[:, 0:D],
            scalar1=cslot(ee, "g2", c), scalar2=None, op0=ALU.mult),
            waits=ee.wv("s_dma"))
    ee.Sc(lambda s: s.copy(out=dcol2[:, 0:80], in_=psb2[:, 0:80]),
          waits=ee.wv("t_done"))

    def gbndr(g, pnc):
        s = pnc % 2
        if s == 1:
            _emit_poll(ee, g, psh, FL_BND, want_side=0)
            g.reg_alu(ee.regs["r_pollcnt"], ee.regs["r_pollcnt"], 32, ALU.add)
            g.dma_start(out=ee.t["bnd_row"][:, :],
                        in_=psh["bnd"][0:1, :]).then_inc(sems["poll"], 16)
            g.dma_start(out=ee.t["bnd_loc"][:, :],
                        in_=psh["bnd"][1:2, :]).then_inc(sems["poll"], 16)
            g.wait_ge(sems["poll"], ee.regs["r_pollcnt"])
        return None
    ee.Gsw(gbndr, done_sem="g_done")

    def ginnp(g, pnc):
        s = pnc % 2
        if s == 0:
            _emit_poll(ee, g, psh, FL_INA, want_side=1)
        else:
            _emit_poll(ee, g, psh, FL_INB, want_side=0)
        return None
    ee.Gsw(ginnp, done_sem="g_done")

    ee.V(lambda v: v.scalar_tensor_tensor(
        out=ee.t["bnd_row"][:], in0=ee.t["bnd_loc"][:],
        scalar=ee.t["s_col"][0:1, 0:1], in1=ee.t["bnd_row"][:],
        op0=ALU.mult, op1=ALU.add),
        waits=ee.wv("g_done"))

    def vbndstat(v):
        v.bn_stats(out=bndst[:, 0:6], in_=ee.t["bnd_row"][:, 0:512])
        v.bn_stats(out=bndst[:, 6:12], in_=ee.t["bnd_row"][:, 512:1024])
        return v.bn_aggr(out=bndst[:, 12:14], in_=bndst[:, 0:12])
    ee.V(vbndstat, waits=ee.wv("g_done"))

    def scbnd(s):
        s.activation(out=bndst[:, 14:15], in_=bndst[:, 13:14], func=AF.Ln,
                     bias=ee.t["cbias"][0:1, 1:2])
        return s.activation(out=bndst[:, 14:15], in_=bndst[:, 14:15],
                            func=AF.Exp, scale=-0.5)
    ee.Sc(scbnd, waits=ee.wv("v_done"))
    ee.V(lambda v: v.tensor_scalar(out=bndh2[:, :], in0=ee.t["bnd_row"][:, :],
                                   scalar1=bndst[:, 12:13],
                                   scalar2=bndst[:, 14:15],
                                   op0=ALU.subtract, op1=ALU.mult),
         waits=ee.wv("sc_done"))
    vdn0 = ee.now("v_done")
    for dck in range(NC8):
        ee.Tn(lambda t, dck=dck: t.transpose(
            psT2[:, 0:1], bndh2[:, dck * 128:(dck + 1) * 128],
            ee.t["eye_bf"][0:1, 0:1]),
            waits=[("v_done", max(vdn0, ee.now("v_done")))], inc=("t_done", 1))
        ee.V(lambda v, dck=dck: v.tensor_copy(
            out=h2T[:, dck * HTW + 127:dck * HTW + 128],
            in_=psT2[:, 0:1]),
            waits=ee.wv("t_done"))

    prev_wb = 0
    for t_i in range(NTILE):
        ee.S(lambda e, t_i=t_i: e.dma_start(
            out=inn_t[:].rearrange("p (c d) -> p c d", d=D),
            in_=scr["inner"][t_i * TT:(t_i + 1) * TT, :]
            .rearrange("(c p) d -> p c d", p=128)),
            waits=ee.wv("v_done") + [("g_dma", prev_wb)], inc=("s_dma", 16))

        inn2 = wld2

        def ginn2(g, pnc, t_i=t_i):
            batch, s, h2 = _role(pnc)
            rb = s * HALF + t_i * TT
            return g.dma_start(
                out=inn2[:].rearrange("p (c d) -> p c d", d=D),
                in_=psh["inner"][rb:rb + TT, :]
                .rearrange("(c p) d -> p c d", p=128))
        ee.Gsw(ginn2, waits=ee.wv("v_done") + [("g_done", ee.now("g_done"))],
               inc=("g_dma", 16))
        s_loaded = ee.now("s_dma")
        ee.V(lambda v: v.tensor_tensor(out=inn_t[:], in0=inn_t[:], in1=inn2[:],
                                       op=ALU.add),
             waits=[("s_dma", s_loaded), ("g_dma", ee.now("g_dma"))])
        ee.G(lambda g, t_i=t_i: g.dma_start(
            out=scr["inner"][t_i * TT:(t_i + 1) * TT, :]
            .rearrange("(c p) d -> p c d", p=128),
            in_=inn_t[:].rearrange("p (c d) -> p c d", d=D)),
            waits=ee.wv("v_done"), inc=("g_dma", 16))
        prev_wb = ee.now("g_dma")
        if t_i > 0:
            def vbcol2(v):
                last = None
                for dck in range(NC8):
                    last = v.tensor_copy(
                        out=h2T[:, dck * HTW + 127:dck * HTW + 128],
                        in_=h2T[:, dck * HTW + 128 + TT - 1:dck * HTW + 128 + TT])
                return last
            ee.V(vbcol2)
        for rcix in range(TT // 128):
            srcc = inn_t[:, rcix * D:(rcix + 1) * D]

            def vstatb(v, srcc=srcc, rcix=rcix):
                v.bn_stats(out=stat6b[:, 0:6], in_=srcc[:, 0:512])
                v.bn_stats(out=stat6b[:, 6:12], in_=srcc[:, 512:1024])
                return v.bn_aggr(out=stcb2[:, 2 * rcix:2 * rcix + 2],
                                 in_=stat6b[:])
            ee.V(vstatb, waits=[("s_dma", s_loaded),
                                ("sc_done", ee.now("sc_done"))])

        def scstb_all(s):
            s.activation(out=stcb2[:, 8:12], in_=stcb2[:, 1:8:2], func=AF.Ln,
                         bias=ee.t["cbias"][:, 1:2])
            return s.activation(out=stcb2[:, 8:12], in_=stcb2[:, 8:12],
                                func=AF.Exp, scale=-0.5)
        ee.Sc(scstb_all, waits=ee.wv("v_done"))
        bstats_done = ee.now("sc_done")
        for rcix in range(TT // 128):
            srcc = inn_t[:, rcix * D:(rcix + 1) * D]
            ee.V(lambda v, srcc=srcc, rcix=rcix: v.tensor_scalar(
                out=h2tm[:], in0=srcc, scalar1=stcb2[:, 2 * rcix:2 * rcix + 1],
                scalar2=stcb2[:, 8 + rcix:9 + rcix],
                op0=ALU.subtract, op1=ALU.mult),
                waits=[("sc_done", bstats_done), ("t_done", ee.now("t_done"))])
            vdn = ee.now("v_done")
            for dck in range(NC8):
                ee.Tn(lambda t, dck=dck:
                      t.transpose(psT2[:], h2tm[:, dck * 128:(dck + 1) * 128],
                                  ee.t["eye_bf"][:]),
                      waits=[("v_done", max(vdn, ee.now("v_done")))],
                      inc=("t_done", 1))
                dst = h2T[:, dck * HTW + 128 + rcix * 128:
                          dck * HTW + 128 + (rcix + 1) * 128]
                ee.V(lambda v, dst=dst: v.tensor_copy(out=dst, in_=psT2[:]),
                     waits=ee.wv("t_done"))

        def vmix2(v):
            last = None
            for dck in range(NC8):
                h = h2T[:, dck * HTW + 128:dck * HTW + 128 + TT]
                hx = h2T[:, dck * HTW + 127:dck * HTW + 127 + TT]
                v.tensor_tensor(out=ddsc2[:], in0=h, in1=hx, op=ALU.subtract)
                for mname, mt in (("cmk", mixCK), ("cmr", mixCR)):
                    last = v.scalar_tensor_tensor(
                        out=mt[:, dck * TT:(dck + 1) * TT], in0=ddsc2[:],
                        scalar=cslot(ee, mname, dck), in1=hx,
                        op0=ALU.mult, op1=ALU.add)
            return last
        ee.V(vmix2, waits=ee.wv("t_done"))
        mix_done = ee.now("v_done")
        ee.G(lambda g, t_i=t_i: g.dma_start(
            out=scr["mcr"][:, :, t_i * TT:(t_i + 1) * TT]
            .rearrange("c p t -> p c t"),
            in_=mixCR[:].rearrange("p (c t) -> p c t", t=TT)),
            waits=ee.wv("v_done"), inc=("g_dma", 16))

        for fc in range(NFC):
            def tkk(t, fc=fc):
                mm = None
                for dck in range(NC8):
                    mm = t.matmul(psA2[:],
                                  cwkb[:, dck * FCM + fc * 128:dck * FCM + (fc + 1) * 128],
                                  mixCK[:, dck * TT:(dck + 1) * TT],
                                  start=(dck == 0), stop=(dck == NC8 - 1))
                return mm
            ee.Tn(tkk, waits=[("v_done", mix_done),
                              ("sc_done", ee.now("sc_done"))], inc=("t_done", 1))
            if t_i == 0:
                ee.Sc(lambda s, fc=fc: s.activation(
                    out=krel[:], in_=psA2[:], func=AF.Identity,
                    bias=dcol2[:, fc:fc + 1]),
                    waits=ee.wv("t_done", "v_done"))

                def vfixkk(v, fc=fc):
                    v.tensor_tensor(out=stcb[:, 5:6], in0=dcol2[:, 32 + fc:32 + fc + 1],
                                    in1=ee.t["ims_col"][:, 0:1], op=ALU.mult)
                    return v.tensor_tensor(out=krel[:, 0:1], in0=krel[:, 0:1],
                                           in1=stcb[:, 5:6], op=ALU.subtract)
                ee.V(vfixkk, waits=ee.wv("sc_done"))
                ee.Sc(lambda s: s.activation(out=krel[:], in_=krel[:],
                                             func=AF.Relu),
                      waits=ee.wv("v_done"))
            else:
                ee.Sc(lambda s, fc=fc: s.activation(
                    out=krel[:], in_=psA2[:], func=AF.Relu,
                    bias=dcol2[:, fc:fc + 1]),
                    waits=ee.wv("t_done", "v_done"))
            ee.V(lambda v, fc=fc: v.tensor_tensor(
                out=kkbuf[:, fc * TT:(fc + 1) * TT], in0=krel[:], in1=krel[:],
                op=ALU.mult),
                waits=ee.wv("sc_done"))
        ee.G(lambda g, t_i=t_i: g.dma_start(
            out=scr["kk2"][:, :, t_i * TT:(t_i + 1) * TT]
            .rearrange("c p t -> p c t"),
            in_=kkbuf[:].rearrange("p (c t) -> p c t", t=TT)),
            waits=ee.wv("v_done"), inc=("g_dma", 16))
    ee.G(lambda g, v=ee.now("g_dma"): g.wait_ge(sems["g_dma"], v))
    ee.Sc(lambda s: s.copy(out=ee.t["dcol"][:, 72:88], in_=dcol2[:, 64:80]),
          waits=ee.wv("sc_done"))



def emit_p1b2(ee, stack, inp, x_in, scr, psh):
    nc = ee.nc
    sems = ee.sems
    NFC = FCM // 128
    NG = HALF // TT

    def sb(name, shape, dt=F32):
        h = stack.enter_context(nc.sbuf_tensor(name, shape, dt))
        ee.t[name] = h
        return h

    def ps(name, shape, dt=F32):
        h = stack.enter_context(nc.psum_tensor(name, shape, dt))
        ee.t[name] = h
        return h

    cwvb = sb("cwvb", [128, NFC * D], BF16)
    cwrb2 = sb("cwrb2", [128, NC8 * D], BF16)
    wld3 = sb("wld3", [128, D])
    kk2g = sb("kk2g", [128, NFC * TT], BF16)
    mcrg = sb("mcrg", [128, NC8 * TT], BF16)
    x0g = sb("x0g", [128, (TT // 128) * D])
    inng = sb("inng", [128, (TT // 128) * D])
    rrg = sb("rrg", [128, (TT // 128) * D])
    bcrb = sb("bcrb", [128, D])
    ccrr = sb("ccrr", [1, D])
    onebf = sb("onebf", [1, 128], BF16)
    rowbf = sb("rowbf", [1, D], BF16)
    psM = [ps("psM%d" % i, [128, D]) for i in range(4)]

    for c in range(NFC):
        ee.S(lambda e, c=c: e.dma_start(
            out=wld3[:], in_=inp["L0_attn_a_cwv"][c * 128:(c + 1) * 128, :]),
            waits=ee.wv("v_done"), inc=("s_dma", 16))
        ee.V(lambda v, c=c: v.tensor_copy(out=cwvb[:, c * D:(c + 1) * D],
                                          in_=wld3[:]),
             waits=ee.wv("s_dma"))
    for c in range(NC8):
        ee.S(lambda e, c=c: e.dma_start(
            out=wld3[:], in_=inp["L0_attn_a_cwr"][c * 128:(c + 1) * 128, :]),
            waits=ee.wv("v_done"), inc=("s_dma", 16))
        ee.V(lambda v, c=c: v.tensor_scalar(
            out=cwrb2[:, c * D:(c + 1) * D], in0=wld3[:],
            scalar1=cslot(ee, "g2", c), scalar2=None, op0=ALU.mult),
            waits=ee.wv("s_dma"))
    ee.V(lambda v: v.memset(onebf[:], 1.0))
    colbf = sb("colbf", [128, 16], BF16)
    ee.V(lambda v: v.tensor_copy(out=colbf[:], in_=ee.t["dcol"][:, 72:88]),
         waits=ee.wv("sc_done"))
    for ec in range(NC8):
        ee.Tn(lambda t, ec=ec: t.matmul(psM[0][0:1, 0:128], colbf[:, ec:ec + 1],
                                        ee.t["eye_bf"][:], start=True, stop=True),
              waits=ee.wv("v_done"), inc=("t_done", 1))
        ee.V(lambda v, ec=ec: v.tensor_copy(out=rowbf[:, ec * 128:(ec + 1) * 128],
                                            in_=psM[0][0:1, 0:128]),
             waits=ee.wv("t_done"))
    for ec in range(NC8):
        ee.Tn(lambda t, ec=ec: t.matmul(psM[0][0:1, 0:128], colbf[:, 8 + ec:8 + ec + 1],
                                        ee.t["eye_bf"][:], start=True, stop=True),
              waits=ee.wv("v_done"), inc=("t_done", 1))
        ee.V(lambda v, ec=ec: v.tensor_copy(out=ccrr[:, ec * 128:(ec + 1) * 128],
                                            in_=psM[0][0:1, 0:128]),
             waits=ee.wv("t_done"))
    for nh in range(2):
        ee.Tn(lambda t, nh=nh: t.matmul(psM[0][:, nh * 512:(nh + 1) * 512],
                                        onebf[:, 0:128],
                                        rowbf[:, nh * 512:(nh + 1) * 512],
                                        start=True, stop=True),
              waits=ee.wv("v_done"), inc=("t_done", 1))
    ee.V(lambda v: v.tensor_copy(out=bcrb[:], in_=psM[0][:]),
         waits=ee.wv("t_done"))

    for gi in range(NG):
        ee.S(lambda e, gi=gi: e.dma_start(
            out=kk2g[:].rearrange("p (c t) -> p c t", t=TT),
            in_=scr["kk2"][:, :, gi * TT:(gi + 1) * TT]
            .rearrange("c p t -> p c t")),
            waits=ee.wv("v_done", "t_done"), inc=("s_dma", 16))
        ee.S(lambda e, gi=gi: e.dma_start(
            out=mcrg[:].rearrange("p (c t) -> p c t", t=TT),
            in_=scr["mcr"][:, :, gi * TT:(gi + 1) * TT]
            .rearrange("c p t -> p c t")),
            inc=("s_dma", 16))
        ee.S(lambda e, gi=gi: e.dma_start(
            out=inng[:].rearrange("p (c d) -> p c d", d=D),
            in_=scr["inner"][gi * TT:(gi + 1) * TT, :]
            .rearrange("(c p) d -> p c d", p=128)),
            inc=("s_dma", 16))

        def gx0(g, pnc, gi=gi):
            batch, s, h2 = _role(pnc)
            xb = batch * T + s * HALF + gi * TT
            return g.dma_start(
                out=x0g[:].rearrange("p (c d) -> p c d", d=D),
                in_=x_in[xb:xb + TT, :].rearrange("(c p) d -> p c d", p=128))
        ee.Gsw(gx0, waits=ee.wv("v_done"), inc=("g_dma", 16))
        ld = [("s_dma", ee.now("s_dma")), ("g_dma", ee.now("g_dma"))]

        for ts in range(TT // 128):
            def trr(t, ts=ts):
                mm = None
                for dck in range(NC8):
                    lhs = mcrg[:, dck * TT + ts * 128:dck * TT + (ts + 1) * 128]
                    for nh in range(2):
                        mm = t.matmul(psM[ts][:, nh * 512:(nh + 1) * 512], lhs,
                                      cwrb2[:, dck * D + nh * 512:dck * D + (nh + 1) * 512],
                                      start=(dck == 0), stop=(dck == NC8 - 1))
                return mm
            ee.Tn(trr, waits=ld + ee.wv("v_done"), inc=("t_done", 1))

            def vrr(v, ts=ts, gi=gi):
                v.tensor_tensor(out=rrg[:, ts * D:(ts + 1) * D], in0=psM[ts][:],
                                in1=bcrb[:], op=ALU.add)
                if gi == 0 and ts == 0:
                    v.tensor_scalar(out=ccrr[:, :], in0=ccrr[:, :],
                                    scalar1=ee.t["ims_col"][0:1, 0:1],
                                    scalar2=None, op0=ALU.mult)
                    return v.tensor_tensor(out=rrg[0:1, 0:D],
                                           in0=rrg[0:1, 0:D], in1=ccrr[:, :],
                                           op=ALU.subtract)
                return None
            wl = ee.wv("t_done")

            def vrr2(v, vrr=vrr):
                r = vrr(v)
                return r
            ee.V(vrr2, waits=wl)
        ee.Sc(lambda s: s.activation(out=rrg[:], in_=rrg[:], func=AF.Tanh,
                                     scale=0.5),
              waits=ee.wv("v_done"))
        ee.V(lambda v: v.tensor_scalar(out=rrg[:], in0=rrg[:], scalar1=0.5,
                                       scalar2=0.5, op0=ALU.mult, op1=ALU.add),
             waits=ee.wv("sc_done"))

        for ts in range(TT // 128):
            def tcm(t, ts=ts):
                mm = None
                for fc in range(NFC):
                    lhs = kk2g[:, fc * TT + ts * 128:fc * TT + (ts + 1) * 128]
                    for nh in range(2):
                        mm = t.matmul(psM[ts][:, nh * 512:(nh + 1) * 512], lhs,
                                      cwvb[:, fc * D + nh * 512:fc * D + (nh + 1) * 512],
                                      start=(fc == 0), stop=(fc == NFC - 1))
                return mm
            ee.Tn(tcm, waits=ee.wv("sc_done") + [("v_done", ee.now("v_done"))],
                  inc=("t_done", 1))

            def vx1(v, ts=ts):
                sl = slice(ts * D, (ts + 1) * D)
                v.tensor_tensor(out=x0g[:, sl], in0=x0g[:, sl], in1=inng[:, sl],
                                op=ALU.add)
                v.scalar_tensor_tensor(out=rrg[:, sl], in0=rrg[:, sl],
                                       scalar=1.0, in1=psM[ts][:],
                                       op0=ALU.mult, op1=ALU.mult)
                return v.tensor_tensor(out=inng[:, sl], in0=x0g[:, sl],
                                       in1=rrg[:, sl], op=ALU.add)
            ee.V(vx1, waits=ee.wv("t_done"))
        ee.G(lambda g, gi=gi: g.dma_start(
            out=scr["x1"][gi * TT:(gi + 1) * TT, :]
            .rearrange("(c p) d -> p c d", p=128),
            in_=inng[:].rearrange("p (c d) -> p c d", d=D)),
            waits=ee.wv("v_done"), inc=("g_dma", 16))
    ee.G(lambda g, v=ee.now("g_dma"): g.wait_ge(sems["g_dma"], v))



def emit_ffn(ee, stack, inp, scr, src_name, dst, ntok, dbgtag=""):
    nc = ee.nc
    sems = ee.sems
    NH = FFN // 128
    NG = ntok // TT

    def sb(name, shape, dt=F32):
        h = stack.enter_context(nc.sbuf_tensor(dbgtag + name, shape, dt))
        ee.t[dbgtag + name] = h
        return h

    def ps(name, shape, dt=F32):
        h = stack.enter_context(nc.psum_tensor(dbgtag + name, shape, dt))
        ee.t[dbgtag + name] = h
        return h

    inner1 = contextlib.ExitStack()

    def sb1(name, shape, dt=F32):
        h = inner1.enter_context(nc.sbuf_tensor(dbgtag + name, shape, dt))
        ee.t[dbgtag + name] = h
        return h
    wgTb = sb1("wgTb", [128, NC8 * FFN], BF16)
    wuTb = sb1("wuTb", [128, NC8 * FFN], BF16)
    xsg = sb1("xsg", [128, (TT // 128) * D])
    xbu = sb1("xbu", [128, D], BF16)
    xbT = sb1("xbT", [128, NC8 * TT], BF16)
    hhb = sb1("hhb", [128, NH * TT], BF16)
    gtmp = sb1("gtmp", [128, TT], BF16)
    stat6f = sb1("stat6f", [128, 12])
    stat2f = sb1("stat2f", [128, 2])
    stcf = sb1("stcf", [128, 12])
    psF = ee.t[dbgtag + "psF"] = inner1.enter_context(
        nc.psum_tensor(dbgtag + "psF", [128, TT], F32))
    psG = ee.t[dbgtag + "psG"] = inner1.enter_context(
        nc.psum_tensor(dbgtag + "psG", [128, TT], F32))
    psTf = ee.t[dbgtag + "psTf"] = inner1.enter_context(
        nc.psum_tensor(dbgtag + "psTf", [128, 128], BF16))

    for dc in range(NC8):
        ee.S(lambda e, dc=dc: e.dma_start(out=wgTb[:, dc * FFN:(dc + 1) * FFN],
                                          in_=scr["wgT"][dc, :, :]),
             inc=("s_dma", 16))
        ee.S(lambda e, dc=dc: e.dma_start(out=wuTb[:, dc * FFN:(dc + 1) * FFN],
                                          in_=scr["wuT"][dc, :, :]),
             inc=("s_dma", 16))
    wload = ee.now("s_dma")

    hh_parts = []
    for gi in range(NG):
        ee.S(lambda e, gi=gi: e.dma_start(
            out=xsg[:].rearrange("p (c d) -> p c d", d=D),
            in_=scr[src_name][gi * TT:(gi + 1) * TT, :]
            .rearrange("(c p) d -> p c d", p=128)),
            waits=ee.wv("v_done"), inc=("s_dma", 16))
        sld = ee.now("s_dma")
        for rc in range(TT // 128):
            srcc = xsg[:, rc * D:(rc + 1) * D]

            def vstf(v, srcc=srcc, rc=rc):
                v.bn_stats(out=stat6f[:, 0:6], in_=srcc[:, 0:512])
                v.bn_stats(out=stat6f[:, 6:12], in_=srcc[:, 512:1024])
                return v.bn_aggr(out=stcf[:, 2 * rc:2 * rc + 2], in_=stat6f[:])
            ee.V(vstf, waits=[("s_dma", sld), ("sc_done", ee.now("sc_done"))])

        def scstf_all(s):
            s.activation(out=stcf[:, 8:12], in_=stcf[:, 1:8:2], func=AF.Ln,
                         bias=ee.t["cbias"][:, 1:2])
            return s.activation(out=stcf[:, 8:12], in_=stcf[:, 8:12],
                                func=AF.Exp, scale=-0.5)
        ee.Sc(scstf_all, waits=ee.wv("v_done"))
        fstats_done = ee.now("sc_done")
        for rc in range(TT // 128):
            srcc = xsg[:, rc * D:(rc + 1) * D]
            ee.V(lambda v, srcc=srcc, rc=rc: v.tensor_scalar(
                out=xbu[:], in0=srcc, scalar1=stcf[:, 2 * rc:2 * rc + 1],
                scalar2=stcf[:, 8 + rc:9 + rc], op0=ALU.subtract, op1=ALU.mult),
                waits=[("sc_done", fstats_done), ("t_done", ee.now("t_done"))])
            vdn = ee.now("v_done")
            for dck in range(NC8):
                ee.Tn(lambda t, dck=dck:
                      t.transpose(psTf[:], xbu[:, dck * 128:(dck + 1) * 128],
                                  ee.t["eye_bf"][:]),
                      waits=[("v_done", max(vdn, ee.now("v_done")))],
                      inc=("t_done", 1))
                dstc = xbT[:, dck * TT + rc * 128:dck * TT + (rc + 1) * 128]
                ee.V(lambda v, dstc=dstc: v.tensor_copy(out=dstc, in_=psTf[:]),
                     waits=ee.wv("t_done"))
        xT_done = ee.now("v_done")
        for hc in range(NH):
            def tgate(t, hc=hc):
                mm = None
                for dck in range(NC8):
                    mm = t.matmul(psF[:],
                                  wgTb[:, dck * FFN + hc * 128:dck * FFN + (hc + 1) * 128],
                                  xbT[:, dck * TT:(dck + 1) * TT],
                                  start=(dck == 0), stop=(dck == NC8 - 1))
                return mm
            ee.Tn(tgate, waits=[("v_done", max(xT_done, ee.now("v_done"))),
                                ("s_dma", wload),
                                ("sc_done", ee.now("sc_done"))],
                  inc=("t_done", 1))
            ee.Sc(lambda s: s.activation(out=gtmp[:], in_=psF[:], func=AF.Silu,
                                         scale=1.0 / 127.0),
                  waits=ee.wv("t_done"))
            g_ev = ee.now("sc_done")

            def tup(t, hc=hc):
                mm = None
                for dck in range(NC8):
                    mm = t.matmul(psG[:],
                                  wuTb[:, dck * FFN + hc * 128:dck * FFN + (hc + 1) * 128],
                                  xbT[:, dck * TT:(dck + 1) * TT],
                                  start=(dck == 0), stop=(dck == NC8 - 1))
                return mm
            ee.Tn(tup, waits=[("v_done", ee.now("v_done"))], inc=("t_done", 1))
            ee.V(lambda v, hc=hc: v.scalar_tensor_tensor(
                out=hhb[:, hc * TT:(hc + 1) * TT], in0=gtmp[:], scalar=1.0,
                in1=psG[:], op0=ALU.mult, op1=ALU.mult),
                waits=[("sc_done", g_ev), ("t_done", ee.now("t_done"))])
        ee.G(lambda g, gi=gi: g.dma_start(
            out=scr["hh"][:, :, gi * TT:(gi + 1) * TT]
            .rearrange("c p t -> p c t"),
            in_=hhb[:].rearrange("p (c t) -> p c t", t=TT)),
            waits=ee.wv("v_done"), inc=("g_dma", 16))
    ee.G(lambda g, v=ee.now("g_dma"): g.wait_ge(sems["g_dma"], v))
    ee.barrier()
    inner1.close()

    wdTb = sb("wdTb", [128, NH * D], BF16)
    hhg = sb("hhg", [128, NH * TT], BF16)
    xrg = sb("xrg", [128, (TT // 128) * D])
    psD = [ps("psD%d" % i, [128, D]) for i in range(4)]
    for fc in range(NH):
        ee.S(lambda e, fc=fc: e.dma_start(out=wdTb[:, fc * D:(fc + 1) * D],
                                          in_=scr["wdT"][fc, :, :]),
             inc=("s_dma", 16))
    wdl = ee.now("s_dma")
    for gi in range(NG):
        ee.S(lambda e, gi=gi: e.dma_start(
            out=hhg[:].rearrange("p (c t) -> p c t", t=TT),
            in_=scr["hh"][:, :, gi * TT:(gi + 1) * TT]
            .rearrange("c p t -> p c t")),
            waits=ee.wv("v_done"), inc=("s_dma", 16))
        ee.S(lambda e, gi=gi: e.dma_start(
            out=xrg[:].rearrange("p (c d) -> p c d", d=D),
            in_=scr[src_name][gi * TT:(gi + 1) * TT, :]
            .rearrange("(c p) d -> p c d", p=128)),
            inc=("s_dma", 16))
        sld = ee.now("s_dma")
        for ts in range(TT // 128):
            def tdn(t, ts=ts):
                mm = None
                for fc in range(NH):
                    lhs = hhg[:, fc * TT + ts * 128:fc * TT + (ts + 1) * 128]
                    for nh in range(2):
                        mm = t.matmul(psD[ts][:, nh * 512:(nh + 1) * 512], lhs,
                                      wdTb[:, fc * D + nh * 512:fc * D + (nh + 1) * 512],
                                      start=(fc == 0), stop=(fc == NH - 1))
                return mm
            ee.Tn(tdn, waits=[("s_dma", max(sld, wdl)),
                              ("v_done", ee.now("v_done"))],
                  inc=("t_done", 1))

            def vx2(v, ts=ts):
                sl = slice(ts * D, (ts + 1) * D)
                v.tensor_scalar(out=xrg[:, sl], in0=xrg[:, sl], scalar1=1.0,
                                scalar2=None, op0=ALU.mult)
                return v.scalar_tensor_tensor(
                    out=xrg[:, sl], in0=psD[ts][:],
                    scalar=1.0 / (127.0 * 127.0), in1=xrg[:, sl],
                    op0=ALU.mult, op1=ALU.add)
            ee.V(vx2, waits=ee.wv("t_done"))
        ee.G(lambda g, gi=gi: g.dma_start(
            out=dst[gi * TT:(gi + 1) * TT, :].rearrange("(c p) d -> p c d", p=128),
            in_=xrg[:].rearrange("p (c d) -> p c d", d=D)),
            waits=ee.wv("v_done"), inc=("g_dma", 16))
    ee.G(lambda g, v=ee.now("g_dma"): g.wait_ge(sems["g_dma"], v))



def emit_p3(ee, inp, scr, psh):
    nc = ee.nc
    sems = ee.sems
    import contextlib as _ctx
    SH = 16

    with _ctx.ExitStack() as ost:
        def osb(name, shape, dt=F32):
            h = ost.enter_context(nc.sbuf_tensor(name, shape, dt))
            ee.t[name] = h
            return h
        qTb = None
        attoT = None

        with _ctx.ExitStack() as ast_:
            def asb(name, shape, dt=F32):
                h = ast_.enter_context(nc.sbuf_tensor(name, shape, dt))
                ee.t[name] = h
                return h
            def aps(name, shape, dt=F32):
                h = ast_.enter_context(nc.psum_tensor(name, shape, dt))
                ee.t[name] = h
                return h
            xcT = asb("xcT", [128, NC8 * HALF], BF16)
            x2g = asb("x2g", [128, (TT // 128) * D])
            xcu = asb("xcu", [128, D], BF16)
            wkb = asb("wkb", [128, NC8 * NKV], BF16)
            wvb = asb("wvb", [128, NC8 * NKV], BF16)
            wldp = asb("wldp", [128, NKV])
            ctab = asb("ctab", [128, 1024])
            stab = asb("stab", [128, 1024])
            ktmp = asb("ktmp", [128, 1024])
            kshu = asb("kshu", [128, 1024])
            krop = asb("krop", [128, 1024])
            kvst = asb("kvst", [128, 4 * HALF], BF16)
            st6p = asb("st6p", [128, 12])
            st2p = asb("st2p", [128, 2])
            stcp = asb("stcp", [128, 12])
            psP = aps("psP", [128, 1024])
            psTp = aps("psTp", [128, 128], BF16)

            for c in range(NC8):
                ee.S(lambda e, c=c: e.dma_start(
                    out=wldp[:], in_=inp["L0_attn_b_Wk"][c * 128:(c + 1) * 128, :]),
                    waits=ee.wv("v_done"), inc=("s_dma", 16))
                ee.V(lambda v, c=c: v.tensor_copy(out=wkb[:, c * NKV:(c + 1) * NKV],
                                                  in_=wldp[:]),
                     waits=ee.wv("s_dma"))
                ee.S(lambda e, c=c: e.dma_start(
                    out=wldp[:], in_=inp["L0_attn_b_Wv"][c * 128:(c + 1) * 128, :]),
                    waits=ee.wv("v_done"), inc=("s_dma", 16))
                ee.V(lambda v, c=c: v.tensor_copy(out=wvb[:, c * NKV:(c + 1) * NKV],
                                                  in_=wldp[:]),
                     waits=ee.wv("s_dma"))

            for gi in range(HALF // TT):
                ee.S(lambda e, gi=gi: e.dma_start(
                    out=x2g[:].rearrange("p (c d) -> p c d", d=D),
                    in_=scr["x2"][gi * TT:(gi + 1) * TT, :]
                    .rearrange("(c p) d -> p c d", p=128)),
                    waits=ee.wv("v_done"), inc=("s_dma", 16))
                sld = ee.now("s_dma")
                for rc in range(TT // 128):
                    srcc = x2g[:, rc * D:(rc + 1) * D]

                    def vstp(v, srcc=srcc, rc=rc):
                        v.bn_stats(out=st6p[:, 0:6], in_=srcc[:, 0:512])
                        v.bn_stats(out=st6p[:, 6:12], in_=srcc[:, 512:1024])
                        return v.bn_aggr(out=stcp[:, 2 * rc:2 * rc + 2],
                                         in_=st6p[:])
                    ee.V(vstp, waits=[("s_dma", sld),
                                      ("sc_done", ee.now("sc_done"))])

                def scp_all(s):
                    s.activation(out=stcp[:, 8:12], in_=stcp[:, 1:8:2],
                                 func=AF.Ln, bias=ee.t["cbias"][:, 1:2])
                    return s.activation(out=stcp[:, 8:12], in_=stcp[:, 8:12],
                                        func=AF.Exp, scale=-0.5)
                ee.Sc(scp_all, waits=ee.wv("v_done"))
                pstats_done = ee.now("sc_done")
                for rc in range(TT // 128):
                    srcc = x2g[:, rc * D:(rc + 1) * D]
                    ee.V(lambda v, srcc=srcc, rc=rc: v.tensor_scalar(
                        out=xcu[:], in0=srcc, scalar1=stcp[:, 2 * rc:2 * rc + 1],
                        scalar2=stcp[:, 8 + rc:9 + rc],
                        op0=ALU.subtract, op1=ALU.mult),
                        waits=[("sc_done", pstats_done),
                               ("t_done", ee.now("t_done"))])
                    vdn = ee.now("v_done")
                    for dck in range(NC8):
                        ee.Tn(lambda t, dck=dck: t.transpose(
                            psTp[:, 0:128], xcu[:, dck * 128:(dck + 1) * 128],
                            ee.t["eye_bf"][:]),
                            waits=[("v_done", max(vdn, ee.now("v_done")))],
                            inc=("t_done", 1))
                        dstc = xcT[:, dck * HALF + gi * TT + rc * 128:
                                   dck * HALF + gi * TT + (rc + 1) * 128]
                        ee.V(lambda v, dstc=dstc: v.tensor_copy(out=dstc,
                                                                in_=psTp[:, 0:128]),
                             waits=ee.wv("t_done"))
            xcT_done = ee.now("v_done")

            def gtab(g, pnc, tb=None, which=None):
                pass
            for tb in range(HALF // 1024):
                def gct(g, pnc, tb=tb):
                    batch, s, h2 = _role(pnc)
                    spos = s * HALF
                    g.dma_start(out=ctab[:],
                                in_=inp["rope_cos"][:, spos + tb * 1024:
                                                    spos + (tb + 1) * 1024]
                                ).then_inc(sems["g_dma"], 16)
                    return g.dma_start(out=stab[:],
                                       in_=inp["rope_sin"][:, spos + tb * 1024:
                                                           spos + (tb + 1) * 1024])
                ee.Gsw(gct, waits=ee.wv("v_done"), inc=("g_dma", 16))
                ee.cnt["g_dma"] += 16
                tabld = ee.now("g_dma")
                for kc4 in range(4):
                    def tkp(t, tb=tb, kc4=kc4):
                        mm = None
                        for dck in range(NC8):
                            for nn in range(2):
                                mm = t.matmul(
                                    psP[:, nn * 512:(nn + 1) * 512],
                                    wkb[:, dck * NKV + kc4 * 128:dck * NKV + (kc4 + 1) * 128],
                                    xcT[:, dck * HALF + tb * 1024 + nn * 512:
                                        dck * HALF + tb * 1024 + (nn + 1) * 512],
                                    start=(dck == 0), stop=(dck == NC8 - 1))
                        return mm
                    ee.Tn(tkp, waits=[("v_done", max(xcT_done, ee.now("v_done"))),
                                      ("sc_done", ee.now("sc_done"))],
                          inc=("t_done", 1))
                    ee.Sc(lambda s: s.copy(out=ktmp[:], in_=psP[:]),
                          waits=ee.wv("t_done"))

                    def vrope(v, tb=tb, kc4=kc4):
                        mask = []
                        for i in range(0, 32, 2):
                            mask += [i + 1, i]
                        v.stream_shuffle(out=kshu[:], in_=ktmp[:], mask=mask)
                        v.tensor_tensor(out=krop[:], in0=ktmp[:], in1=ctab[:],
                                        op=ALU.mult)
                        v.scalar_tensor_tensor(out=kshu[:], in0=kshu[:],
                                               scalar=1.0, in1=stab[:],
                                               op0=ALU.mult, op1=ALU.mult)
                        return v.tensor_tensor(
                            out=kvst[:, kc4 * HALF + tb * 1024:
                                     kc4 * HALF + (tb + 1) * 1024],
                            in0=krop[:], in1=kshu[:], op=ALU.add)
                    ee.V(vrope, waits=[("g_dma", tabld),
                                       ("sc_done", ee.now("sc_done"))])
            def gkw(g, pnc):
                batch, s, h2 = _role(pnc)
                return g.dma_start(
                    out=psh["kT"][s * NKV:(s + 1) * NKV, :]
                    .rearrange("(c p) t -> p c t", p=128),
                    in_=kvst[:].rearrange("p (c t) -> p c t", t=HALF))
            ee.Gsw(gkw, waits=ee.wv("v_done"), inc=("g_dma", 16))
            ee.G(lambda g, v=ee.now("g_dma"): g.wait_ge(sems["g_dma"], v))

            for tc in range(HALF // 128):
                def tvp(t, tc=tc):
                    mm = None
                    for dck in range(NC8):
                        mm = t.matmul(
                            psP[:, 0:NKV],
                            xcT[:, dck * HALF + tc * 128:dck * HALF + (tc + 1) * 128],
                            wvb[:, dck * NKV:(dck + 1) * NKV],
                            start=(dck == 0), stop=(dck == NC8 - 1))
                    return mm
                ee.Tn(tvp, waits=[("sc_done", ee.now("sc_done")),
                                  ("v_done", ee.now("v_done"))],
                      inc=("t_done", 1))
                ee.Sc(lambda s, tc=tc: s.copy(
                    out=kvst[:, (tc % 32) * NKV:(tc % 32) * NKV + NKV]
                    .rearrange("p t -> p t") if False else
                    kvst[:, (tc % 32) * NKV:(tc % 32 + 1) * NKV],
                    in_=psP[:, 0:NKV]),
                    waits=ee.wv("t_done"))
                if tc % 32 == 31:
                    blk = tc // 32

                    def gvw(g, pnc, blk=blk):
                        batch, s, h2 = _role(pnc)
                        return g.dma_start(
                            out=psh["v"][s * HALF + blk * 4096:
                                         s * HALF + (blk + 1) * 4096, :]
                            .rearrange("(c p) t -> p c t", p=128),
                            in_=kvst[:, 0:32 * NKV]
                            .rearrange("p (c t) -> p c t", t=NKV))
                    ee.Gsw(gvw, waits=ee.wv("sc_done"), inc=("g_dma", 16))
                    ee.G(lambda g, v=ee.now("g_dma"): g.wait_ge(sems["g_dma"], v))
            _send_flag(ee, psh, FL_KV)

        ee.barrier()
        qTb = osb("qTb", [128, NC8 * QB], BF16)
        attoT = osb("attoT", [128, NC8 * QB], BF16)

        with _ctx.ExitStack() as bst:
            def bsb(name, shape, dt=F32):
                h = bst.enter_context(nc.sbuf_tensor(name, shape, dt))
                ee.t[name] = h
                return h
            def bps(name, shape, dt=F32):
                h = bst.enter_context(nc.psum_tensor(name, shape, dt))
                ee.t[name] = h
                return h
            xcq = bsb("xcq", [128, NC8 * QB], BF16)
            x2g = bsb("x2gB", [128, (TT // 128) * D])
            xcu = bsb("xcuB", [128, D], BF16)
            wqb1 = bsb("wqb1", [128, NC8 * 512], BF16)
            wqb2 = bsb("wqb2", [128, NC8 * 512], BF16)
            wldp = bsb("wldpB", [128, 512])
            ctab = bsb("ctabB", [128, 1024])
            stab = bsb("stabB", [128, 1024])
            ktmp = bsb("ktmpB", [128, 1024])
            kshu = bsb("kshuB", [128, 1024])
            krop = bsb("kropB", [128, 1024])
            st6p = bsb("st6pB", [128, 12])
            st2p = bsb("st2pB", [128, 2])
            stcp = bsb("stcpB", [128, 12])
            psP = bps("psPB", [128, 1024])
            psTp = bps("psTpB", [128, 128], BF16)

            for c in range(NC8):
                for half_i, wdst in ((0, wqb1), (1, wqb2)):
                    ee.S(lambda e, c=c, half_i=half_i: e.dma_start(
                        out=wldp[:],
                        in_=inp["L0_attn_b_Wq"][c * 128:(c + 1) * 128,
                                                half_i * 512:(half_i + 1) * 512]),
                        waits=ee.wv("v_done"), inc=("s_dma", 16))
                    ee.V(lambda v, c=c, wdst=wdst: v.tensor_copy(
                        out=wdst[:, c * 512:(c + 1) * 512], in_=wldp[:]),
                        waits=ee.wv("s_dma"))

            for gi in range(QB // TT):
                def gx2qb(g, pnc, gi=gi):
                    batch, s, h2 = _role(pnc)
                    off = h2 * QB + gi * TT
                    return g.dma_start(
                        out=x2g[:].rearrange("p (c d) -> p c d", d=D),
                        in_=scr["x2"][off:off + TT, :]
                        .rearrange("(c p) d -> p c d", p=128))
                ee.Gsw(gx2qb, waits=ee.wv("v_done"), inc=("g_dma", 16))
                sld = ee.now("g_dma")
                for rc in range(TT // 128):
                    srcc = x2g[:, rc * D:(rc + 1) * D]

                    def vstq(v, srcc=srcc, rc=rc):
                        v.bn_stats(out=st6p[:, 0:6], in_=srcc[:, 0:512])
                        v.bn_stats(out=st6p[:, 6:12], in_=srcc[:, 512:1024])
                        return v.bn_aggr(out=stcp[:, 2 * rc:2 * rc + 2],
                                         in_=st6p[:])
                    ee.V(vstq, waits=[("g_dma", sld),
                                      ("sc_done", ee.now("sc_done"))])

                def scq_all(s):
                    s.activation(out=stcp[:, 8:12], in_=stcp[:, 1:8:2],
                                 func=AF.Ln, bias=ee.t["cbias"][:, 1:2])
                    return s.activation(out=stcp[:, 8:12], in_=stcp[:, 8:12],
                                        func=AF.Exp, scale=-0.5)
                ee.Sc(scq_all, waits=ee.wv("v_done"))
                qstats_done = ee.now("sc_done")
                for rc in range(TT // 128):
                    srcc = x2g[:, rc * D:(rc + 1) * D]
                    ee.V(lambda v, srcc=srcc, rc=rc: v.tensor_scalar(
                        out=xcu[:], in0=srcc, scalar1=stcp[:, 2 * rc:2 * rc + 1],
                        scalar2=stcp[:, 8 + rc:9 + rc],
                        op0=ALU.subtract, op1=ALU.mult),
                        waits=[("sc_done", qstats_done),
                               ("t_done", ee.now("t_done"))])
                    vdn = ee.now("v_done")
                    for dck in range(NC8):
                        ee.Tn(lambda t, dck=dck: t.transpose(
                            psTp[:, 0:128], xcu[:, dck * 128:(dck + 1) * 128],
                            ee.t["eye_bf"][:]),
                            waits=[("v_done", max(vdn, ee.now("v_done")))],
                            inc=("t_done", 1))
                        dstc = xcq[:, dck * QB + gi * TT + rc * 128:
                                   dck * QB + gi * TT + (rc + 1) * 128]
                        ee.V(lambda v, dstc=dstc: v.tensor_copy(
                            out=dstc, in_=psTp[:, 0:128]),
                            waits=ee.wv("t_done"))
            xcq_done = ee.now("v_done")

            for tb in range(QB // 1024):
                def gqtab(g, pnc, tb=tb):
                    batch, s, h2 = _role(pnc)
                    qpos = s * HALF + h2 * QB
                    g.dma_start(out=ctab[:],
                                in_=inp["rope_cos"][:, qpos + tb * 1024:
                                                    qpos + (tb + 1) * 1024]
                                ).then_inc(sems["g_dma"], 16)
                    return g.dma_start(out=stab[:],
                                       in_=inp["rope_sin"][:, qpos + tb * 1024:
                                                           qpos + (tb + 1) * 1024])
                ee.Gsw(gqtab, waits=ee.wv("v_done"), inc=("g_dma", 16))
                ee.cnt["g_dma"] += 16
                tabld = ee.now("g_dma")
                for qc4 in range(NC8):
                    wsrc = wqb1 if qc4 < 4 else wqb2
                    qoff = (qc4 % 4) * 128

                    def tqp(t, tb=tb, qc4=qc4, wsrc=wsrc, qoff=qoff):
                        mm = None
                        for dck in range(NC8):
                            for nn in range(2):
                                mm = t.matmul(
                                    psP[:, nn * 512:(nn + 1) * 512],
                                    wsrc[:, dck * 512 + qoff:dck * 512 + qoff + 128],
                                    xcq[:, dck * QB + tb * 1024 + nn * 512:
                                        dck * QB + tb * 1024 + (nn + 1) * 512],
                                    start=(dck == 0), stop=(dck == NC8 - 1))
                        return mm
                    ee.Tn(tqp, waits=[("v_done", max(xcq_done, ee.now("v_done"))),
                                      ("sc_done", ee.now("sc_done")),
                                      ("g_dma", ee.now("g_dma"))],
                          inc=("t_done", 1))
                    ee.Sc(lambda s: s.activation(out=ktmp[:], in_=psP[:],
                                                 func=AF.Identity, scale=0.125),
                          waits=ee.wv("t_done"))

                    def vqrope(v, tb=tb, qc4=qc4):
                        mask = []
                        for i in range(0, 32, 2):
                            mask += [i + 1, i]
                        v.stream_shuffle(out=kshu[:], in_=ktmp[:], mask=mask)
                        v.tensor_tensor(out=krop[:], in0=ktmp[:], in1=ctab[:],
                                        op=ALU.mult)
                        v.scalar_tensor_tensor(out=kshu[:], in0=kshu[:],
                                               scalar=1.0, in1=stab[:],
                                               op0=ALU.mult, op1=ALU.mult)
                        return v.tensor_tensor(
                            out=qTb[:, qc4 * QB + tb * 1024:
                                    qc4 * QB + (tb + 1) * 1024],
                            in0=krop[:], in1=kshu[:], op=ALU.add)
                    ee.V(vqrope, waits=[("g_dma", tabld),
                                        ("sc_done", ee.now("sc_done"))])
            ee.G(lambda g, v=ee.now("g_dma"): g.wait_ge(sems["g_dma"], v))

        ee.barrier()

        with _ctx.ExitStack() as cst:
            def csb(name, shape, dt=F32):
                h = cst.enter_context(nc.sbuf_tensor(name, shape, dt))
                ee.t[name] = h
                return h
            def cps(name, shape, dt=F32):
                h = cst.enter_context(nc.psum_tensor(name, shape, dt))
                ee.t[name] = h
                return h
            khalf = csb("khalf", [128, 2 * T], BF16)
            vhalf = csb("vhalf", [128, 64 * 260], BF16)
            Ebuf = csb("Ebuf", [128, 2 * 1024], BF16)
            q64 = csb("q64", [128, 2 * 512], BF16)
            denb = csb("denb", [128, 16])
            avs = csb("avs", [64, 512])
            denr = csb("denr", [1, 512])
            denrb = csb("denrb", [1, 512], BF16)
            ones64 = csb("ones64", [1, 64], BF16)
            psS = cps("psS", [128, 2 * 1024])
            psAV = cps("psAV", [128, 2 * 512])
            psBC = cps("psBC", [64, 2 * 512])

            vev_done = {0: 0, 1: 0}
            vsc_done = {0: 0, 1: 0}
            q_staged = {0: 0, 1: 0}
            q_last_sc = {0: 0, 1: 0}
            pending = [None]
            prev_exp = [None]

            ee.V(lambda v: v.memset(ones64[:], 1.0))
            def gpkv(g, pnc):
                s = pnc % 2
                _emit_poll(ee, g, psh, FL_KV, want_side=1 - s)
                return None
            ee.Gsw(gpkv, done_sem="g_done")

            for hs in range(2):
                def gkh(g, hs=hs):
                    for lc in range(2):
                        for sd in range(2):
                            g.dma_start(
                                out=khalf[:, lc * T + sd * HALF:
                                          lc * T + (sd + 1) * HALF],
                                in_=psh["kT"][sd * NKV + (2 * hs + lc) * 128:
                                              sd * NKV + (2 * hs + lc + 1) * 128, :]
                                ).then_inc(sems["g_dma"], 16)
                    return None
                ee.G(lambda g, hs=hs: gkh(None if False else g, hs),
                     waits=ee.wv("g_done", "t_done", "sc_done"))
                ee.cnt["g_dma"] += 64
                def gvh(g, hs=hs):
                    for sd in range(2):
                        for hh in range(4):
                            g.dma_start(
                                out=vhalf[:, sd * 32 * 260:(sd + 1) * 32 * 260]
                                .rearrange("p (g r) -> p g r", r=260)
                                [:, :, hh * 65:hh * 65 + 64],
                                in_=psh["v"][sd * HALF:(sd + 1) * HALF,
                                             hs * 256 + hh * 64:
                                             hs * 256 + (hh + 1) * 64]
                                .rearrange("(g p) d -> p g d", p=128)
                                ).then_inc(sems["g_dma"], 16)
                    return None
                ee.G(lambda g, hs=hs: gvh(g, hs))
                ee.cnt["g_dma"] += 128
                ee.V(lambda v: v.memset(
                    vhalf[:].rearrange("p (g h d) -> p g h d", h=4, d=65)
                    [:, :, :, 64:65], 1.0),
                    waits=ee.wv("g_dma"))
                kvld = ee.now("g_dma")

                groups = [(qg, h8) for qg in range(4) for h8 in range(8)]

                def gparm(h8, hs=hs):
                    qh = hs * 8 + h8
                    kvh = qh // 2
                    lci = (kvh - hs * 4) // 2
                    rbase = ((kvh - hs * 4) % 2) * 64
                    qrow = (qh % 2) * 64
                    qchunk = qh // 2
                    return qh, kvh, lci, rbase, qrow, qchunk

                def stage_q(gi, hs=hs):
                    qg, h8 = groups[gi]
                    qh, kvh, lci, rbase, qrow, qchunk = gparm(h8)
                    gpn = (hs * 32 + gi) % 2

                    def vq64(v, qrow=qrow, rbase=rbase, qchunk=qchunk,
                             qg=qg, gpn=gpn):
                        return v.tensor_copy(
                            out=q64[rbase:rbase + 64, gpn * 512:(gpn + 1) * 512],
                            in_=qTb[qrow:qrow + 64,
                                    qchunk * QB + qg * 512:
                                    qchunk * QB + (qg + 1) * 512])
                    ee.V(vq64, waits=[("t_done", q_last_sc[gpn])])
                    q_staged[gpn] = ee.now("v_done")

                stage_q(0)
                for gi, (qg, h8) in enumerate(groups):
                    qh, kvh, lci, rbase, qrow, qchunk = gparm(h8)
                    gp = (hs * 32 + gi) % 2
                    exp_of = {}
                    NU = 32
                    exp_base = ee.now("sc_done")
                    sc_items = []

                    def emit_sc2(u, gp=gp, lci=lci, rbase=rbase,
                                 prev_exp=prev_exp[0]):
                        def tsc2(t, u=u, lci=lci, rbase=rbase, gp=gp):
                            mm = None
                            for j in range(2):
                                kt = u * 2 + j
                                mm = t.matmul(
                                    psS[:, (u % 2) * 1024 + j * 512:
                                        (u % 2) * 1024 + (j + 1) * 512],
                                    khalf[:, lci * T + kt * 128:
                                          lci * T + (kt + 1) * 128]
                                    [rbase:rbase + 64, :],
                                    q64[rbase:rbase + 64,
                                        gp * 512:(gp + 1) * 512],
                                    start=True, stop=True)
                            return mm
                        w = [("g_dma", kvld), ("v_done", q_staged[gp])]
                        if u >= 2:
                            w.append(("sc_done", exp_of[u - 2]))
                        elif prev_exp is not None:
                            w.append(("sc_done", prev_exp[NU - 2 + u]))
                        ee.Tn(tsc2, waits=w, inc=("t_done", 1))

                    def emit_exp2(u):
                        def sexp2(s, u=u):
                            return s.activation(
                                out=Ebuf[:, (u % 2) * 1024:(u % 2 + 1) * 1024],
                                in_=psS[:, (u % 2) * 1024:(u % 2 + 1) * 1024],
                                func=AF.Exp, bias=ee.t["cbias"][:, 2:3])
                        sc_items.append(
                            (sexp2, [("t_done", ee.now("t_done"))], None))
                        exp_of[u] = exp_base + u + 1

                    def emit_av2(u, gp=gp, kvh=kvh):
                        def tav2(t, u=u, kvh=kvh, hs=hs, gp=gp):
                            mm = None
                            hh = kvh - hs * 4
                            for j in range(2):
                                kt = u * 2 + j
                                mm = t.matmul(
                                    psAV[0:65, gp * 512:(gp + 1) * 512],
                                    vhalf[:, kt * 260 + hh * 65:
                                          kt * 260 + (hh + 1) * 65],
                                    Ebuf[:, (u % 2) * 1024 + j * 512:
                                         (u % 2) * 1024 + (j + 1) * 512],
                                    start=(kt == 0), stop=(kt == 2 * NU - 1))
                            return mm
                        w = [("sc_done", exp_of[u])]
                        if u == 0:
                            w.append(("v_done", vev_done[gp]))
                        ee.Tn(tav2, waits=w, inc=("t_done", 1))

                    emit_sc2(0)
                    emit_exp2(0)
                    emit_sc2(1)
                    emit_av2(0)
                    emit_exp2(1)
                    if pending[0] is not None:
                        pending[0]()
                        pending[0] = None
                    if gi + 1 < len(groups):
                        stage_q(gi + 1)
                    for u in range(2, NU):
                        emit_sc2(u)
                        emit_av2(u - 1)
                        emit_exp2(u)
                    q_last_sc[gp] = ee.now("t_done")
                    emit_av2(NU - 1)
                    av31_cnt = ee.now("t_done")
                    ee.Sc_batch(sc_items)
                    prev_exp[0] = exp_of

                    def vev(v, gp=gp):
                        v.tensor_copy(out=denr[:],
                                      in_=psAV[64:65, gp * 512:gp * 512 + 512])
                        v.reciprocal(out=denr[:], in_=denr[:])
                        v.tensor_copy(out=denrb[:], in_=denr[:])
                        return v.tensor_copy(out=avs[:],
                                             in_=psAV[0:64,
                                                      gp * 512:gp * 512 + 512])
                    ee.V(vev, waits=[("t_done", av31_cnt)])
                    vev_done[gp] = ee.now("v_done")

                    def mk_tail(gp=gp, qh=qh, qg=qg, vevd=vev_done[gp],
                                vscd=vsc_done[gp]):
                        def tail():
                            ee.Tn(lambda t: t.matmul(
                                psBC[:, gp * 512:(gp + 1) * 512], ones64[:],
                                denrb[:], start=True, stop=True),
                                waits=[("v_done", max(vevd, vscd))],
                                inc=("t_done", 1))
                            bc_cnt = ee.now("t_done")

                            def vsc(v, qh=qh, qg=qg, gp=gp):
                                qchunk = qh // 2
                                qrow = (qh % 2) * 64
                                return v.scalar_tensor_tensor(
                                    out=attoT[qrow:qrow + 64,
                                              qchunk * QB + qg * 512:
                                              qchunk * QB + (qg + 1) * 512],
                                    in0=avs[:], scalar=1.0,
                                    in1=psBC[:, gp * 512:(gp + 1) * 512],
                                    op0=ALU.mult, op1=ALU.mult)
                            ee.V(vsc, waits=[("t_done", bc_cnt)])
                            vsc_done[gp] = ee.now("v_done")
                        return tail
                    pending[0] = mk_tail()
                if pending[0] is not None:
                    pending[0]()
                    pending[0] = None
        ee.barrier()

        with _ctx.ExitStack() as dst_:
            def dsb(name, shape, dt=F32):
                h = dst_.enter_context(nc.sbuf_tensor(name, shape, dt))
                ee.t[name] = h
                return h
            def dps(name, shape, dt=F32):
                h = dst_.enter_context(nc.psum_tensor(name, shape, dt))
                ee.t[name] = h
                return h
            wob = dsb("wob", [128, NC8 * D], BF16)
            wldo = dsb("wldo", [128, D])
            bobf = dsb("bobf", [1, D], BF16)
            bob = dsb("bob", [128, D])
            ones128 = dsb("ones128", [1, 128], BF16)
            x2q = dsb("x2q", [128, (TT // 128) * D])
            x3st = dsb("x3st", [128, (TT // 128) * D])
            psW = [dps("psW%d" % i, [128, D]) for i in range(4)]

            for c in range(NC8):
                ee.S(lambda e, c=c: e.dma_start(
                    out=wldo[:], in_=inp["L0_attn_b_Wo"][c * 128:(c + 1) * 128, :]),
                    waits=ee.wv("v_done"), inc=("s_dma", 16))
                ee.V(lambda v, c=c: v.tensor_copy(out=wob[:, c * D:(c + 1) * D],
                                                  in_=wldo[:]),
                     waits=ee.wv("s_dma"))
            ee.S(lambda e: e.dma_start(out=wldo[0:1, :], in_=inp["L0_attn_b_bo"][:, :]),
                 inc=("s_dma", 16))

            def vbo(v):
                v.memset(ones128[:], 1.0)
                return v.tensor_copy(out=bobf[:], in_=wldo[0:1, :])
            ee.V(vbo, waits=ee.wv("s_dma"))
            for nh in range(2):
                ee.Tn(lambda t, nh=nh: t.matmul(
                    psW[0][:, nh * 512:(nh + 1) * 512], ones128[:],
                    bobf[:, nh * 512:(nh + 1) * 512], start=True, stop=True),
                    waits=ee.wv("v_done"), inc=("t_done", 1))
            ee.V(lambda v: v.tensor_copy(out=bob[:], in_=psW[0][:]),
                 waits=ee.wv("t_done"))

            for qg in range(4):
                def gx2q(g, pnc, qg=qg):
                    batch, s, h2 = _role(pnc)
                    off = h2 * QB + qg * TT
                    return g.dma_start(
                        out=x2q[:].rearrange("p (c d) -> p c d", d=D),
                        in_=scr["x2"][off:off + TT, :]
                        .rearrange("(c p) d -> p c d", p=128))
                ee.Gsw(gx2q, waits=ee.wv("v_done"), inc=("g_dma", 16))
                x2ld = ee.now("g_dma")
                for ts in range(TT // 128):
                    def tow(t, qg=qg, ts=ts):
                        mm = None
                        for dck in range(NC8):
                            lhs = attoT[:, dck * QB + qg * TT + ts * 128:
                                        dck * QB + qg * TT + (ts + 1) * 128]
                            for nh in range(2):
                                mm = t.matmul(
                                    psW[ts][:, nh * 512:(nh + 1) * 512], lhs,
                                    wob[:, dck * D + nh * 512:dck * D + (nh + 1) * 512],
                                    start=(dck == 0), stop=(dck == NC8 - 1))
                        return mm
                    ee.Tn(tow, waits=[("v_done", ee.now("v_done")),
                                      ("g_dma", x2ld)], inc=("t_done", 1))

                    def vx3(v, ts=ts):
                        sl = slice(ts * D, (ts + 1) * D)
                        v.tensor_tensor(out=x3st[:, sl], in0=psW[ts][:],
                                        in1=bob[:], op=ALU.add)
                        return v.tensor_tensor(out=x3st[:, sl], in0=x3st[:, sl],
                                               in1=x2q[:, sl], op=ALU.add)
                    ee.V(vx3, waits=ee.wv("t_done"))
                ee.G(lambda g, qg=qg: g.dma_start(
                    out=scr["x3"][qg * TT:(qg + 1) * TT, :]
                    .rearrange("(c p) d -> p c d", p=128),
                    in_=x3st[:].rearrange("p (c d) -> p c d", d=D)),
                    waits=ee.wv("v_done"), inc=("g_dma", 16))
            ee.G(lambda g, v=ee.now("g_dma"): g.wait_ge(sems["g_dma"], v))


def emit_p2(ee, stack, inp, scr):
    emit_ffn(ee, stack, inp, scr, "x1", scr["x2"], HALF, dbgtag="f1")


def emit_p4(ee, stack, inp, scr, out):
    emit_ffn(ee, stack, inp, scr, "x3", out, QB, dbgtag="f2")




def _colpack(inputs):
    pk = np.zeros((128, 16 * 8), np.float32)
    src = {"mk": "L0_attn_a_tm_mix_k", "mv": "L0_attn_a_tm_mix_v",
           "mr": "L0_attn_a_tm_mix_r", "cmk": "L0_attn_a_cm_mix_k",
           "cmr": "L0_attn_a_cm_mix_r", "td": "L0_attn_a_time_decay",
           "tf": "L0_attn_a_time_first", "g1": "L0_attn_a_ln1_g",
           "b1": "L0_attn_a_ln1_b", "g2": "L0_attn_a_ln2_g",
           "b2": "L0_attn_a_ln2_b"}
    for k, nm in src.items():
        v = np.asarray(inputs[nm], np.float32).reshape(-1)
        pk[:, CSLOT[k] * 8:CSLOT[k] * 8 + 8] = v.reshape(8, 128).T
    return pk


def _derived_inputs(inputs):
    pos = np.arange(T, dtype=np.float64)
    rows = np.arange(128)
    j = rows % 64
    inv_freq = 1.0 / (10000.0 ** ((2 * (j // 2)) / ROT))
    freqs = pos[None, :] * inv_freq[:, None]
    cos = np.cos(freqs)
    sin = np.sin(freqs) * np.where(j % 2 == 0, -1.0, 1.0)[:, None]
    rot = (j < ROT)[:, None]
    nonce = int(np.random.randint(1 << 20, 1 << 30))
    return {
        "rope_cos": np.where(rot, cos, 1.0).astype(np.float32),
        "rope_sin": np.where(rot, sin, 0.0).astype(np.float32),
        "colpack": _colpack(inputs),
        "eye_bf16": np.eye(128, dtype=ml_dtypes.bfloat16),
        "nonce_row": (nonce + np.arange(16)).astype(np.int32).reshape(1, 16),
    }


def _make_in_map(inputs):
    in_map = {}
    x = np.asarray(inputs["L0_attn_a_x"], dtype=np.float32)
    in_map["L0_attn_a_x"] = np.ascontiguousarray(x.reshape(B * T, D))
    for k, v in inputs.items():
        if k == "L0_attn_a_x":
            continue
        a = np.asarray(v)
        if a.ndim == 1:
            a = a.reshape(1, -1)
        in_map[k] = np.ascontiguousarray(a)
    in_map.update(_derived_inputs(inputs))
    return in_map


def kernel(**inputs):
    in_map = _make_in_map(inputs)
    nc = build_program()
    res = run_bass_kernel_spmd(nc, [in_map] * 8, core_ids=list(range(8)))
    full = np.zeros((B, T, D), dtype=np.float32)
    for rr in res.results:
        pnc = int(round(float(rr["whoami"][0, 0])))
        batch, s, h2 = _role(pnc)
        qstart = s * HALF + h2 * QB
        full[batch, qstart:qstart + QB, :] = rr["out"]
    return full


# version 11
# speedup vs baseline: 1.2342x; 1.2342x over previous
import contextlib
import numpy as np
import ml_dtypes
import concourse.bass as bass
import concourse.mybir as mybir
from concourse.bass_utils import run_bass_kernel_spmd

F32 = mybir.dt.float32
BF16 = mybir.dt.bfloat16
I32 = mybir.dt.int32
AF = mybir.ActivationFunctionType
ALU = mybir.AluOpType
DS = bass.ds

B, T, D = 2, 8192, 1024
HALF = T // 2
QB = T // 4
FCM = 4 * D
FFN = 3072
EPS = 1e-5
NC8 = D // 128
TT = 512
ROT = 32
NKV = 512

CSLOT = {"mk": 0, "mv": 1, "mr": 2, "cmk": 3, "cmr": 4, "td": 5, "tf": 6,
         "g1": 7, "b1": 8, "g2": 9, "b2": 10}
DSLOT = {"lam": 0, "eu": 1, "bk": 2, "bv": 3, "br": 4, "bcr": 5,
         "ck": 6, "cv": 7, "cr": 8}

FL_BND = 1
FL_KV = 2
FL_INA = 3
FL_INB = 4


def _role(pnc):
    return pnc // 4, pnc % 2, (pnc // 2) % 2


CHAIN_OPS = {
    "activation", "copy", "add", "mul", "square", "sqrt", "sign",
    "tensor_tensor", "tensor_scalar", "scalar_tensor_tensor",
    "tensor_tensor_scan", "tensor_copy", "tensor_add", "tensor_sub",
    "tensor_mul", "tensor_max", "tensor_relu", "tensor_scalar_add",
    "tensor_scalar_mul", "tensor_scalar_sub", "tensor_scalar_max",
    "tensor_reduce", "reduce_sum", "reduce_max", "bn_stats", "bn_aggr",
    "memset", "memzero", "reciprocal", "select", "stream_shuffle",
    "copy_predicated", "iota", "transpose", "max",
}


class _Chained:

    def __init__(self, eng, ee, key):
        self._eng = eng
        self._ee = ee
        self._key = key
        self._sem = ee.sems[key]

    def __getattr__(self, name):
        fn = getattr(self._eng, name)
        if name not in CHAIN_OPS:
            return fn
        ee, eng, sem, key = self._ee, self._eng, self._sem, self._key

        def wrapped(*a, **k):
            if ee.cnt[key] > 0:
                eng.wait_ge(sem, ee.cnt[key])
            ins = fn(*a, **k)
            ins.then_inc(sem, 1)
            ee.cnt[key] += 1
            return ins
        return wrapped


class E:

    def __init__(self, nc, block, stack):
        self.nc = nc
        self.block = block
        self.stack = stack
        self.sems = {}
        self.cnt = {}
        self.t = {}
        self.regs = {}
        self.bar_target = 0

    def sem(self, *names):
        for n in names:
            self.sems[n] = self.stack.enter_context(self.nc.semaphore(n))
            self.cnt[n] = 0

    CHAIN_KEY = {"vector": "v_done", "scalar": "sc_done"}

    def _wrap(self, emit, waits, inc, engname):
        wl = [(self.sems[s], v) for (s, v) in waits]
        it = None
        if inc:
            it = (self.sems[inc[0]], inc[1])
        ck = self.CHAIN_KEY.get(engname)

        def body(e):
            for sh, v in wl:
                if v > 0:
                    e.wait_ge(sh, v)
            ins = emit(_Chained(e, self, ck) if ck else e)
            if it is not None:
                assert ins is not None, f"{engname} op with inc returned None"
                ins.then_inc(it[0], it[1])
        getattr(self.block, engname)(body)
        if inc:
            self.cnt[inc[0]] += inc[1]

    def S(self, emit, waits=(), inc=None):
        self._wrap(emit, waits, inc, "sync")

    def V(self, emit, waits=(), inc=None):
        self._wrap(emit, waits, inc, "vector")

    def Tn(self, emit, waits=(), inc=None):
        self._wrap(emit, waits, inc, "tensor")

    def Sc(self, emit, waits=(), inc=None):
        self._wrap(emit, waits, inc, "scalar")

    def G(self, emit, waits=(), inc=None):
        self._wrap(emit, waits, inc, "gpsimd")

    def Gsw(self, emit, waits=(), inc=None, done_sem=None):
        wl = [(self.sems[s], v) for (s, v) in waits]
        it = (self.sems[inc[0]], inc[1]) if inc else None
        nc = self.nc

        def body(g):
            for sh, v in wl:
                if v > 0:
                    g.wait_ge(sh, v)
            for arm in nc.switch_core_id(engines=[g], n=128):
                ins = emit(g, int(arm) % 8)
                if it is not None and ins is not None:
                    ins.then_inc(it[0], it[1])
                if done_sem is not None:
                    g.sem_inc(self.sems[done_sem], 1)
        self.block.gpsimd(body)
        if inc:
            self.cnt[inc[0]] += inc[1]
        if done_sem is not None:
            self.cnt[done_sem] += 1

    def Sc_batch(self, items):
        resolved = []
        for emit, waits, inc in items:
            wl = [(self.sems[s], v) for (s, v) in waits]
            it = (self.sems[inc[0]], inc[1]) if inc else None
            resolved.append((emit, wl, it))

        def body(e):
            ch = _Chained(e, self, "sc_done")
            for emit, wl, it in resolved:
                for sh, v in wl:
                    if v > 0:
                        e.wait_ge(sh, v)
                ins = emit(ch)
                if it is not None:
                    ins.then_inc(it[0], it[1])
                    self.cnt[it[0]] += it[1]
        self.block.scalar(body)

    def now(self, sem):
        return self.cnt[sem]

    def wv(self, *sems):
        return [(s, self.cnt[s]) for s in sems]

    def barrier(self):
        bar = self.sems["bar"]
        self.bar_target += 5
        tgt = self.bar_target

        def f(e):
            e.sem_inc(bar, 1)
            e.wait_ge(bar, tgt)
        for nm in ("sync", "scalar", "vector", "tensor", "gpsimd"):
            getattr(self.block, nm)(f)

    def sb(self, name, shape, dt=F32):
        h = self.stack.enter_context(self.nc.sbuf_tensor(name, shape, dt))
        self.t[name] = h
        return h

    def ps(self, name, shape, dt=F32):
        h = self.stack.enter_context(self.nc.psum_tensor(name, shape, dt))
        self.t[name] = h
        return h


def cslot(ee, name, c):
    i = CSLOT[name] * 8 + c
    return ee.t["colpk"][:, i:i + 1]


def dslot(ee, name, c):
    i = DSLOT[name] * 8 + c
    return ee.t["dcol"][:, i:i + 1]



def build_program(dbg=(), race_checks=False, poll_backoff=True):
    nc = bass.Bass("TRN2", target_bir_lowering=False,
                   detect_race_conditions=race_checks)
    nc.disable_value_cache = True
    inp = {}

    def di(name, shape, dt=F32):
        inp[name] = nc.dram_tensor(name, shape, dt, kind="ExternalInput").ap()
        return inp[name]

    x_in = di("L0_attn_a_x", [B * T, D])
    for nm in ["ln1_g", "ln1_b", "ln2_g", "ln2_b", "tm_mix_k", "tm_mix_v",
               "tm_mix_r", "time_decay", "time_first", "cm_mix_k", "cm_mix_r"]:
        di("L0_attn_a_" + nm, [1, D])
    di("L0_attn_a_wk", [D, D]); di("L0_attn_a_wv", [D, D])
    di("L0_attn_a_wr", [D, D]); di("L0_attn_a_wo", [D, D])
    di("L0_attn_a_cwk", [D, FCM]); di("L0_attn_a_cwv", [FCM, D])
    di("L0_attn_a_cwr", [D, D])
    di("L0_attn_b_Wq", [D, D]); di("L0_attn_b_Wk", [D, NKV])
    di("L0_attn_b_Wv", [D, NKV]); di("L0_attn_b_Wo", [D, D])
    di("L0_attn_b_bo", [1, D])
    di("L0_ffn_w_gate", [FFN, D], I32); di("L0_ffn_w_up", [FFN, D], I32)
    di("L0_ffn_w_down", [D, FFN], I32)
    di("colpack", [128, 16 * 8])
    di("rope_cos", [128, T]); di("rope_sin", [128, T])
    di("eye_bf16", [128, 128], BF16)
    di("nonce_row", [1, 16], I32)

    out = nc.dram_tensor("out", [QB, D], F32, kind="ExternalOutput").ap()
    whoami = nc.dram_tensor("whoami", [1, 1], F32, kind="ExternalOutput").ap()
    dbg_t = {}
    for name, shape in [("dbg_inner", [HALF, D]), ("dbg_x1", [HALF, D]),
                        ("dbg_x2", [HALF, D]), ("dbg_x3", [QB, D])]:
        if name in dbg:
            dbg_t[name] = nc.dram_tensor(name, shape, F32,
                                         kind="ExternalOutput").ap()

    scr = {
        "inner": nc.dram_tensor("inner_scr", [HALF, D], F32).ap(),
        "x1": nc.dram_tensor("x1_scr", [HALF, D], F32).ap(),
        "x2": nc.dram_tensor("x2_scr", [HALF, D], F32).ap(),
        "x3": nc.dram_tensor("x3_scr", [QB, D], F32).ap(),
        "kk2": nc.dram_tensor("kk2_scr", [FCM // 128, 128, HALF], BF16).ap(),
        "mcr": nc.dram_tensor("mcr_scr", [NC8, 128, HALF], BF16).ap(),
        "hh": nc.dram_tensor("hh_scr", [FFN // 128, 128, HALF], BF16).ap(),
        "wgT": nc.dram_tensor("wgT_scr", [NC8, 128, FFN], BF16).ap(),
        "wuT": nc.dram_tensor("wuT_scr", [NC8, 128, FFN], BF16).ap(),
        "wdT": nc.dram_tensor("wdT_scr", [FFN // 128, 128, D], BF16).ap(),
    }
    psh = {
        "inner": nc.dram_tensor("psh_inner", [T, D], F32, addr_space="Shared").ap(),
        "bnd": nc.dram_tensor("psh_bnd", [2, D], F32, addr_space="Shared").ap(),
        "kT": nc.dram_tensor("psh_kT", [2 * NKV, HALF], BF16, addr_space="Shared").ap(),
        "v": nc.dram_tensor("psh_v", [2 * HALF, NKV], BF16, addr_space="Shared").ap(),
        "flag": nc.dram_tensor("psh_flag", [32, 1], I32, addr_space="Shared").ap(),
    }

    for name, shape, dt in [("dbg_x0t", [128, (TT // 128) * D], F32),
                            ("dbg_h1T", [128, NC8 * (TT + 128)], BF16),
                            ("dbg_mixK", [128, NC8 * TT], BF16),
                            ("dbg_kvr", [128, 3 * TT], F32),
                            ("dbg_rwkv", [128, NC8 * TT], BF16)]:
        if name in dbg:
            dbg_t[name] = nc.dram_tensor(name, shape, dt,
                                         kind="ExternalOutput").ap()
    with contextlib.ExitStack() as stack:
        block = stack.enter_context(nc.Block())
        ee = E(nc, block, stack)
        ee.poll_backoff = poll_backoff
        ee.sem("bar", "g_dma", "s_dma", "c_dma", "v_done", "t_done",
               "sc_done", "g_done", "poll", "scan")

        ee.sb("colpk", [128, 16 * 8])
        ee.sb("dcol", [128, 16 * 8])
        ee.sb("onesT", [128, TT])
        ee.sb("id_t", [128, 1])
        ee.sb("s_col", [128, 1])
        ee.sb("ims_col", [128, 1])
        ee.sb("nonce_t", [1, 16], I32)
        ee.sb("flrd_t", [1, 16], I32)
        ee.sb("carryA", [128, NC8])
        ee.sb("carryB", [128, NC8])
        ee.sb("eye_bf", [128, 128], BF16)
        ee.sb("b1bf", [128, 8], BF16)
        ee.sb("b1m", [128, 24], BF16)
        ee.sb("b1mf", [128, 8])
        ee.sb("cbias", [128, 4])
        ee.sb("bnd_row", [1, D])
        ee.sb("bnd_loc", [1, D])
        ee.sb("tdtf", [128, 8])

        emit_init(ee, inp)
        with nc.named_scope("p0w"):
            with contextlib.ExitStack() as pstack:
                emit_p0w(ee, pstack, inp, scr)
            ee.barrier()
        with nc.named_scope("p1a"):
            with contextlib.ExitStack() as pstack:
                emit_p1a(ee, pstack, inp, x_in, scr, psh, dbg_t)
            ee.barrier()
        if "dbg_inner" in dbg_t:
            ee.G(lambda g: g.dma_start(out=dbg_t["dbg_inner"][:, :],
                                       in_=scr["inner"][:, :]),
                 inc=("g_dma", 16))
            ee.G(lambda g, v=ee.now("g_dma"): g.wait_ge(ee.sems["g_dma"], v))
            ee.barrier()
        with nc.named_scope("p1b1"):
            with contextlib.ExitStack() as pstack:
                emit_p1b1(ee, pstack, inp, x_in, scr, psh)
            ee.barrier()
        with nc.named_scope("p1b2"):
            with contextlib.ExitStack() as pstack:
                emit_p1b2(ee, pstack, inp, x_in, scr, psh)
            ee.barrier()
        if "dbg_x1" in dbg_t:
            ee.G(lambda g: g.dma_start(out=dbg_t["dbg_x1"][:, :],
                                       in_=scr["x1"][:, :]),
                 inc=("g_dma", 16))
            ee.G(lambda g, v=ee.now("g_dma"): g.wait_ge(ee.sems["g_dma"], v))
            ee.barrier()
        with nc.named_scope("p2"):
            with contextlib.ExitStack() as pstack:
                emit_p2(ee, pstack, inp, scr)
            ee.barrier()
        if "dbg_x2" in dbg_t:
            ee.G(lambda g: g.dma_start(out=dbg_t["dbg_x2"][:, :],
                                       in_=scr["x2"][:, :]),
                 inc=("g_dma", 16))
            ee.G(lambda g, v=ee.now("g_dma"): g.wait_ge(ee.sems["g_dma"], v))
            ee.barrier()
        with nc.named_scope("p3"):
            emit_p3(ee, inp, scr, psh)
            ee.barrier()
        if "dbg_x3" in dbg_t:
            ee.G(lambda g: g.dma_start(out=dbg_t["dbg_x3"][:, :],
                                       in_=scr["x3"][:, :]),
                 inc=("g_dma", 16))
            ee.G(lambda g, v=ee.now("g_dma"): g.wait_ge(ee.sems["g_dma"], v))
            ee.barrier()
        with nc.named_scope("p4"):
            with contextlib.ExitStack() as pstack:
                emit_p4(ee, pstack, inp, scr, out)
            ee.barrier()
        emit_whoami(ee, whoami)
    return nc



def emit_init(ee, inp):
    nc = ee.nc

    def g_init(g):
        r = {}
        for nm in ["r_tgt", "r_poll", "r_pollcnt"]:
            r[nm] = g.alloc_register(nm)
        g.reg_mov(r["r_pollcnt"], 0)
        ee.regs.update(r)
        for arm in nc.switch_core_id(engines=[g], n=128):
            pnc = int(arm) % 8
            batch, s, h2 = _role(pnc)
            g.memset(ee.t["id_t"][:], float(pnc))
            g.memset(ee.t["s_col"][:], float(s))
            g.memset(ee.t["ims_col"][:], float(1 - s))
    ee.block.gpsimd(g_init)

    ee.G(lambda g: g.dma_start(out=ee.t["colpk"][:], in_=inp["colpack"][:, :]),
         inc=("g_dma", 16))
    ee.G(lambda g: g.dma_start(out=ee.t["nonce_t"][:], in_=inp["nonce_row"][:, :]),
         inc=("g_dma", 16))

    def gtdtf(g, pnc):
        cg = pnc % 2
        g.dma_start(
            out=ee.t["tdtf"][:, 0:4],
            in_=inp["colpack"][:, CSLOT["td"] * 8 + cg * 4:
                               CSLOT["td"] * 8 + cg * 4 + 4]).then_inc(
            ee.sems["g_dma"], 16)
        return g.dma_start(
            out=ee.t["tdtf"][:, 4:8],
            in_=inp["colpack"][:, CSLOT["tf"] * 8 + cg * 4:
                               CSLOT["tf"] * 8 + cg * 4 + 4])
    ee.Gsw(gtdtf, inc=("g_dma", 16))
    ee.cnt["g_dma"] += 16
    ee.S(lambda e: e.dma_start(out=ee.t["eye_bf"][:], in_=inp["eye_bf16"][:, :]),
         inc=("s_dma", 16))

    def v0(v):
        v.memset(ee.t["onesT"][:], 1.0)
        v.memset(ee.t["cbias"][:, 0:1], EPS * EPS)
        v.memset(ee.t["cbias"][:, 1:2], EPS)
        v.memset(ee.t["cbias"][:, 2:3], -16.0)
        v.memset(ee.t["bnd_row"][:], 0.0)
        v.memset(ee.t["carryA"][:], 0.0)
        v.memset(ee.t["carryB"][:], 0.0)
        v.memset(ee.t["bnd_loc"][:], 0.0)
        v.memset(ee.t["dcol"][:], 0.0)
        v.tensor_copy(out=ee.t["b1bf"][:],
                      in_=ee.t["colpk"][:, CSLOT["b1"] * 8:CSLOT["b1"] * 8 + 8])
        for mi, mname in enumerate(("mk", "mv", "mr")):
            v.tensor_tensor(out=ee.t["b1mf"][:],
                            in0=ee.t["colpk"][:, CSLOT["b1"] * 8:CSLOT["b1"] * 8 + 8],
                            in1=ee.t["colpk"][:, CSLOT[mname] * 8:CSLOT[mname] * 8 + 8],
                            op=ALU.mult)
            v.tensor_tensor(out=ee.t["b1mf"][:],
                            in0=ee.t["colpk"][:, CSLOT["b1"] * 8:CSLOT["b1"] * 8 + 8],
                            in1=ee.t["b1mf"][:], op=ALU.subtract)
            v.tensor_copy(out=ee.t["b1m"][:, mi * 8:(mi + 1) * 8], in_=ee.t["b1mf"][:])
        return v.memset(ee.t["lowbuf"][:], 0.0) if False else v.memset(ee.t["b1mf"][:], 0.0)
    ee.V(v0, waits=ee.wv("g_dma", "s_dma"))

    def sc0(sc):
        dc = ee.t["dcol"]
        sc.activation(out=dc[:, 0:4], in_=ee.t["tdtf"][:, 0:4], func=AF.Exp)
        sc.activation(out=dc[:, 0:4], in_=dc[:, 0:4], func=AF.Exp, scale=-1.0)
        return sc.activation(out=dc[:, 8:12], in_=ee.t["tdtf"][:, 4:8],
                             func=AF.Exp)
    ee.Sc(sc0, waits=ee.wv("v_done", "g_dma"))
    ee.barrier()


def _emit_poll(ee, g, psh, col, want_side):
    r = ee.regs
    sems = ee.sems
    g.reg_load(r["r_tgt"], ee.t["nonce_t"][0:1, col:col + 1])
    g.reg_mov(r["r_poll"], 1)
    with g.While(r["r_poll"]):
        if ee.poll_backoff:
            g.nop(cycle_cnt=6000)
        g.reg_alu(r["r_pollcnt"], r["r_pollcnt"], 16, ALU.add)
        g.dma_start(out=ee.t["flrd_t"][:, col:col + 1],
                    in_=psh["flag"][want_side * 16 + col:want_side * 16 + col + 1,
                                    :]).then_inc(sems["poll"], 16)
        g.wait_ge(sems["poll"], r["r_pollcnt"])
        g.reg_load(r["r_poll"], ee.t["flrd_t"][0:1, col:col + 1])
        g.reg_alu(r["r_poll"], r["r_poll"], r["r_tgt"], ALU.not_equal)


def _send_flag(ee, psh, col):
    def gf(g, pnc):
        s = pnc % 2
        return g.dma_start(out=psh["flag"][s * 16 + col:s * 16 + col + 1, :],
                           in_=ee.t["nonce_t"][0:1, col:col + 1])
    ee.Gsw(gf, inc=("g_dma", 16))
    ee.G(lambda g, v=ee.now("g_dma"): g.wait_ge(ee.sems["g_dma"], v))



def emit_p1a(ee, stack, inp, x_in, scr, psh, dbg_t=()):
    nc = ee.nc
    sems = ee.sems
    r = ee.regs
    NTILE = T // TT
    HTW = TT + 128
    NOC = 4
    W512 = 512
    WO_OFF = 3 * NC8 * W512

    def sb(name, shape, dt=F32):
        h = stack.enter_context(nc.sbuf_tensor(name, shape, dt))
        ee.t[name] = h
        return h

    def ps(name, shape, dt=F32):
        h = stack.enter_context(nc.psum_tensor(name, shape, dt))
        ee.t[name] = h
        return h

    w4 = sb("w4", [128, WO_OFF + NOC * D], BF16)
    wld = sb("wld", [128, D])
    rawbf = sb("rawbf", [128, W512], BF16)
    h1T = sb("h1T", [128, NC8 * HTW], BF16)
    mixK = sb("mixK", [128, NC8 * TT], BF16)
    mixV = sb("mixV", [128, NC8 * TT], BF16)
    mixR = sb("mixR", [128, NC8 * TT], BF16)
    rwkv = sb("rwkv", [128, NOC * TT], BF16)
    x0t = sb("x0t", [128, (TT // 128) * D])
    h1tm = sb("h1tm", [128, D], BF16)
    ddsc = sb("ddsc", [128, TT], BF16)
    lam_s = sb("lam_s", [128, TT])
    kc = sb("kc", [128, TT]); vcx = sb("vcx", [128, TT])
    rpre = sb("rpre", [128, NOC * TT], BF16)
    ekc = sb("ekc", [128, TT]); ekvc = sb("ekvc", [128, TT])
    t1c = sb("t1c", [128, TT])
    denb = sb("denb1", [128, NOC * TT])
    At = sb("At", [128, TT + 1])
    Bt = sb("Bt", [128, TT + 1])
    inner_t = sb("inner_t", [128, (TT // 128) * D])
    stat6 = sb("stat6", [128, 12])
    stat2 = sb("stat2", [128, 2])
    stc = sb("stc", [128, 8])
    stb = sb("stb", [128, 26])
    psA = ps("psA", [128, TT])
    psB = ps("psB", [128, TT])
    psT = ps("psT", [128, 128], BF16)
    psO = ps("psO", [128, D])
    psbias = ps("psbias", [128, 48])

    ee.V(lambda v: v.memset(h1T[:], 0.0))

    WN = [("L0_attn_a_wk", "g1"), ("L0_attn_a_wv", "g1"),
          ("L0_attn_a_wr", "g1")]
    for wi, (iname, gname) in enumerate(WN):
        for c in range(NC8):
            def gwld(g, pnc, iname=iname, c=c):
                cg = pnc % 2
                return g.dma_start(
                    out=wld[:, 0:W512],
                    in_=inp[iname][c * 128:(c + 1) * 128,
                                   cg * W512:(cg + 1) * W512])
            ee.Gsw(gwld, waits=ee.wv("v_done"), inc=("g_dma", 16))
            ee.V(lambda v: v.tensor_copy(out=rawbf[:], in_=wld[:, 0:W512]),
                 waits=ee.wv("g_dma", "t_done"))

            def tbias(t, wi=wi, c=c):
                mm = None
                for ec in range(NOC):
                    first = wi == 0 and c == 0 and ec == 0
                    last = wi == 2 and c == NC8 - 1 and ec == NOC - 1
                    mm = t.matmul(psbias[:, wi * 8 + ec:wi * 8 + ec + 1],
                                  rawbf[:, ec * 128:(ec + 1) * 128],
                                  ee.t["b1bf"][:, c:c + 1],
                                  start=first, stop=last,
                                  skip_group_check=True)
                    mm = t.matmul(psbias[:, 24 + wi * 8 + ec:24 + wi * 8 + ec + 1],
                                  rawbf[:, ec * 128:(ec + 1) * 128],
                                  ee.t["b1m"][:, wi * 8 + c:wi * 8 + c + 1],
                                  start=False, stop=last,
                                  skip_group_check=True)
                return mm
            ee.Tn(tbias, waits=ee.wv("v_done"), inc=("t_done", 1))

            def vcast(v, wi=wi, c=c, gname=gname):
                dst = w4[:, (wi * NC8 + c) * W512:(wi * NC8 + c + 1) * W512]
                return v.tensor_scalar(out=dst, in0=wld[:, 0:W512],
                                       scalar1=cslot(ee, gname, c),
                                       scalar2=None, op0=ALU.mult)
            ee.V(vcast, waits=ee.wv("g_dma"))
    for rc in range(NOC):
        def gwo(g, pnc, rc=rc):
            cg = pnc % 2
            rb = cg * W512 + rc * 128
            return g.dma_start(out=wld[:],
                               in_=inp["L0_attn_a_wo"][rb:rb + 128, :])
        ee.Gsw(gwo, waits=ee.wv("v_done"), inc=("g_dma", 16))
        ee.V(lambda v, rc=rc: v.tensor_copy(
            out=w4[:, WO_OFF + rc * D:WO_OFF + (rc + 1) * D], in_=wld[:]),
            waits=ee.wv("g_dma"))
    def sc_bias_ev(s):
        last = None
        for wi in range(3):
            s.copy(out=ee.t["dcol"][:, 16 + wi * 8:16 + wi * 8 + NOC],
                   in_=psbias[:, wi * 8:wi * 8 + NOC])
            last = s.copy(out=ee.t["dcol"][:, 48 + wi * 8:48 + wi * 8 + NOC],
                          in_=psbias[:, 24 + wi * 8:24 + wi * 8 + NOC])
        return last
    ee.Sc(sc_bias_ev, waits=ee.wv("t_done"))


    ee.V(lambda v: v.tensor_copy(out=lam_s[:], in_=ee.t["onesT"][:]),
         waits=ee.wv("sc_done"))

    for t_i in range(NTILE):
        last_tile = t_i == NTILE - 1

        def gx(g, pnc, t_i=t_i):
            batch, s, h2 = _role(pnc)
            xb = batch * T + t_i * TT
            return g.dma_start(
                out=x0t[:].rearrange("p (c d) -> p c d", d=D),
                in_=x_in[xb:xb + TT, :].rearrange("(c p) d -> p c d", p=128))
        ee.Gsw(gx, waits=ee.wv("v_done"), inc=("g_dma", 16))
        x_loaded = ee.now("g_dma")

        if t_i > 0:
            def vbcol(v):
                last = None
                for dck in range(NC8):
                    last = v.tensor_copy(
                        out=h1T[:, dck * HTW + 127:dck * HTW + 128],
                        in_=h1T[:, dck * HTW + 128 + TT - 1:dck * HTW + 128 + TT])
                return last
            ee.V(vbcol)

        rowchunks = list(range(TT // 128))
        nrc = len(rowchunks)
        off = 0
        for i, rcix in enumerate(rowchunks):
            src = x0t[:, rcix * D:(rcix + 1) * D]

            def vstat(v, src=src, i=i):
                v.bn_stats(out=stat6[:, 0:6], in_=src[:, 0:512])
                v.bn_stats(out=stat6[:, 6:12], in_=src[:, 512:1024])
                return v.bn_aggr(out=stb[:, 2 * i:2 * i + 2], in_=stat6[:])
            ee.V(vstat, waits=[("g_dma", x_loaded),
                               ("sc_done", ee.now("sc_done"))])

        def scst_all(s, nrc=nrc, off=off):
            s.activation(out=stb[:, 12:12 + nrc], in_=stb[:, 1:2 * nrc:2],
                         func=AF.Ln, scale=(1.0 + EPS), bias=ee.t["cbias"][:, 0:1])
            s.activation(out=stb[:, 12:12 + nrc], in_=stb[:, 12:12 + nrc],
                         func=AF.Exp, scale=-0.5)
            s.activation(out=stb[:, 18:22], in_=stb[:, 2 * off + 1:2 * off + 8:2],
                         func=AF.Ln, bias=ee.t["cbias"][:, 1:2])
            return s.activation(out=stb[:, 18:22], in_=stb[:, 18:22],
                                func=AF.Exp, scale=-0.5)
        ee.Sc(scst_all, waits=ee.wv("v_done"))
        gname0 = "ims_col" if t_i < NTILE // 2 else "s_col"
        ee.V(lambda v, gname0=gname0: v.tensor_scalar(
            out=stb[:, 22:26], in0=stb[:, 18:22],
            scalar1=ee.t[gname0][:, 0:1], scalar2=None, op0=ALU.mult),
            waits=ee.wv("sc_done"))
        stats_done = ee.now("sc_done")

        for i, rcix in enumerate(rowchunks):
            src = x0t[:, rcix * D:(rcix + 1) * D]
            ee.V(lambda v, src=src, i=i: v.tensor_scalar(
                out=h1tm[:], in0=src, scalar1=stb[:, 2 * i:2 * i + 1],
                scalar2=stb[:, 12 + i:13 + i],
                op0=ALU.subtract, op1=ALU.mult),
                waits=[("sc_done", stats_done), ("t_done", ee.now("t_done"))])
            vdn = ee.now("v_done")
            for dck in range(NC8):
                ee.Tn(lambda t, dck=dck:
                      t.transpose(psT[:], h1tm[:, dck * 128:(dck + 1) * 128],
                                  ee.t["eye_bf"][:]),
                      waits=[("v_done", max(vdn, ee.now("v_done")))],
                      inc=("t_done", 1))
                dst = h1T[:, dck * HTW + 128 + rcix * 128:
                          dck * HTW + 128 + (rcix + 1) * 128]
                ee.V(lambda v, dst=dst: v.tensor_copy(out=dst, in_=psT[:]),
                     waits=ee.wv("t_done"))

        if t_i == 0 and "dbg_x0t" in dbg_t:
            ee.G(lambda g: g.dma_start(out=dbg_t["dbg_x0t"][:, :], in_=x0t[:]),
                 waits=[("g_dma", x_loaded)], inc=("g_dma", 16))
            ee.G(lambda g, v=ee.now("g_dma"): g.wait_ge(sems["g_dma"], v))
            ee.barrier()
        if t_i == 0 and "dbg_h1T" in dbg_t:
            ee.G(lambda g: g.dma_start(out=dbg_t["dbg_h1T"][:, :], in_=h1T[:]),
                 waits=ee.wv("v_done"), inc=("g_dma", 16))
            ee.G(lambda g, v=ee.now("g_dma"): g.wait_ge(sems["g_dma"], v))
            ee.barrier()

        def vmix(v):
            last = None
            for dck in range(NC8):
                h = h1T[:, dck * HTW + 128:dck * HTW + 128 + TT]
                hx = h1T[:, dck * HTW + 127:dck * HTW + 127 + TT]
                v.tensor_tensor(out=ddsc[:], in0=h, in1=hx, op=ALU.subtract)
                for mname, mt in (("mk", mixK), ("mv", mixV), ("mr", mixR)):
                    last = v.scalar_tensor_tensor(
                        out=mt[:, dck * TT:(dck + 1) * TT], in0=ddsc[:],
                        scalar=cslot(ee, mname, dck), in1=hx,
                        op0=ALU.mult, op1=ALU.add)
            return last
        ee.V(vmix, waits=ee.wv("t_done"))
        mix_done = ee.now("v_done")

        if t_i == 0 and "dbg_mixK" in dbg_t:
            ee.G(lambda g: g.dma_start(out=dbg_t["dbg_mixK"][:, :], in_=mixK[:]),
                 waits=ee.wv("v_done"), inc=("g_dma", 16))
            ee.G(lambda g, v=ee.now("g_dma"): g.wait_ge(sems["g_dma"], v))
            ee.barrier()

        prev_r_evac = ee.now("sc_done")
        for oc in range(NOC):
            def tk(t, oc=oc):
                mm = None
                for dck in range(NC8):
                    mm = t.matmul(psA[:], w4[:, (0 * NC8 + dck) * W512 + oc * 128:
                                              (0 * NC8 + dck) * W512 + (oc + 1) * 128],
                                  mixK[:, dck * TT:(dck + 1) * TT],
                                  start=(dck == 0), stop=(dck == NC8 - 1))
                return mm
            ee.Tn(tk, waits=[("v_done", mix_done), ("sc_done", prev_r_evac)],
                  inc=("t_done", 1))
            ee.Sc(lambda s, oc=oc: s.activation(out=kc[:], in_=psA[:],
                                                func=AF.Identity,
                                                bias=dslot(ee, "bk", oc)),
                  waits=ee.wv("t_done", "v_done"))
            if t_i == 0:
                ee.V(lambda v, oc=oc: v.tensor_tensor(
                    out=kc[:, 0:1], in0=kc[:, 0:1], in1=dslot(ee, "ck", oc),
                    op=ALU.subtract),
                    waits=ee.wv("sc_done"))
            k_ev = ee.now("sc_done")

            def tv(t, oc=oc):
                mm = None
                for dck in range(NC8):
                    mm = t.matmul(psB[:], w4[:, (1 * NC8 + dck) * W512 + oc * 128:
                                              (1 * NC8 + dck) * W512 + (oc + 1) * 128],
                                  mixV[:, dck * TT:(dck + 1) * TT],
                                  start=(dck == 0), stop=(dck == NC8 - 1))
                return mm
            ee.Tn(tv, waits=[("sc_done", k_ev)], inc=("t_done", 1))
            ee.Sc(lambda s, oc=oc: s.activation(out=vcx[:], in_=psB[:],
                                                func=AF.Identity,
                                                bias=dslot(ee, "bv", oc)),
                  waits=ee.wv("t_done", "v_done"))
            if t_i == 0:
                ee.V(lambda v, oc=oc: v.tensor_tensor(
                    out=vcx[:, 0:1], in0=vcx[:, 0:1], in1=dslot(ee, "cv", oc),
                    op=ALU.subtract),
                    waits=ee.wv("sc_done"))
            v_ev = ee.now("sc_done")

            def tr(t, oc=oc):
                mm = None
                for dck in range(NC8):
                    mm = t.matmul(psA[:], w4[:, (2 * NC8 + dck) * W512 + oc * 128:
                                              (2 * NC8 + dck) * W512 + (oc + 1) * 128],
                                  mixR[:, dck * TT:(dck + 1) * TT],
                                  start=(dck == 0), stop=(dck == NC8 - 1))
                return mm
            ee.Tn(tr, waits=[("sc_done", k_ev)], inc=("t_done", 1))
            if t_i == 0:
                ee.Sc(lambda s, oc=oc: s.activation(out=t1c[:], in_=psA[:],
                                                    func=AF.Identity,
                                                    bias=dslot(ee, "br", oc)),
                      waits=ee.wv("t_done", "v_done"))
                def vfixr(v, oc=oc):
                    v.tensor_tensor(out=t1c[:, 0:1], in0=t1c[:, 0:1],
                                    in1=dslot(ee, "cr", oc), op=ALU.subtract)
                    return v.tensor_copy(out=rpre[:, oc * TT:(oc + 1) * TT],
                                         in_=t1c[:])
                ee.V(vfixr, waits=ee.wv("sc_done"))
                prev_r_evac = ee.now("sc_done")
            else:
                ee.Sc(lambda s, oc=oc: s.activation(
                    out=rpre[:, oc * TT:(oc + 1) * TT], in_=psA[:],
                    func=AF.Identity, bias=dslot(ee, "br", oc)),
                      waits=ee.wv("t_done"))
                prev_r_evac = ee.now("sc_done")

            ee.V(lambda v, oc=oc: v.tensor_scalar(
                out=lam_s[:], in0=ee.t["onesT"][:], scalar1=dslot(ee, "lam", oc),
                scalar2=None, op0=ALU.mult))
            ee.Sc(lambda s: s.activation(out=ekc[:], in_=kc[:], func=AF.Exp),
                  waits=[("sc_done", k_ev), ("v_done", ee.now("v_done"))])
            ek_done = ee.now("sc_done")
            ee.V(lambda v: v.tensor_tensor(out=ekvc[:], in0=ekc[:], in1=vcx[:],
                                           op=ALU.mult),
                 waits=[("sc_done", max(ek_done, v_ev))])

            def vscanA(v, oc=oc):
                return v.tensor_tensor_scan(
                    out=At[:, 1:1 + TT], data0=lam_s[:],
                    data1=ekvc[:], initial=ee.t["carryA"][:, oc:oc + 1],
                    op0=ALU.mult, op1=ALU.add)
            ee.V(vscanA, waits=ee.wv("v_done"))

            def vscanB(v, oc=oc):
                return v.tensor_tensor_scan(
                    out=Bt[:, 1:1 + TT], data0=lam_s[:],
                    data1=ekc[:], initial=ee.t["carryB"][:, oc:oc + 1],
                    op0=ALU.mult, op1=ALU.add)
            ee.V(vscanB, waits=[("sc_done", ek_done)])

            def vcarry(v, oc=oc):
                v.tensor_copy(out=At[:, 0:1], in_=ee.t["carryA"][:, oc:oc + 1])
                v.tensor_copy(out=Bt[:, 0:1], in_=ee.t["carryB"][:, oc:oc + 1])
                v.tensor_copy(out=ee.t["carryA"][:, oc:oc + 1],
                              in_=At[:, TT:TT + 1])
                return v.tensor_copy(out=ee.t["carryB"][:, oc:oc + 1],
                                     in_=Bt[:, TT:TT + 1])
            ee.V(vcarry)

            def vnum(v, oc=oc):
                v.scalar_tensor_tensor(out=t1c[:], in0=ekc[:],
                                       scalar=dslot(ee, "eu", oc), in1=vcx[:],
                                       op0=ALU.mult, op1=ALU.mult)
                v.tensor_tensor(out=rwkv[:, oc * TT:(oc + 1) * TT], in0=t1c[:],
                                in1=At[:, 0:TT], op=ALU.add)
                return v.scalar_tensor_tensor(
                    out=denb[:, oc * TT:(oc + 1) * TT], in0=ekc[:],
                    scalar=dslot(ee, "eu", oc),
                    in1=Bt[:, 0:TT], op0=ALU.mult, op1=ALU.add)
            ee.V(vnum, waits=ee.wv("v_done"))
        def scpost(s):
            s.activation(out=denb[:], in_=denb[:], func=AF.Ln)
            s.activation(out=denb[:], in_=denb[:], func=AF.Exp, scale=-1.0)
            return s.activation(out=rpre[:], in_=rpre[:], func=AF.Tanh,
                                scale=0.5)
        ee.Sc(scpost, waits=ee.wv("v_done"))

        def vrfin(v):
            v.tensor_scalar(out=rpre[:], in0=rpre[:], scalar1=0.5, scalar2=0.5,
                            op0=ALU.mult, op1=ALU.add)
            last = None
            for oc2 in range(NOC):
                sl = slice(oc2 * TT, (oc2 + 1) * TT)
                v.tensor_tensor(out=rwkv[:, sl], in0=rwkv[:, sl],
                                in1=denb[:, sl], op=ALU.mult)
                last = v.tensor_tensor(out=rwkv[:, sl], in0=rwkv[:, sl],
                                       in1=rpre[:, sl], op=ALU.mult)
            return last
        ee.V(vrfin, waits=ee.wv("sc_done"))
        rwkv_done = ee.now("v_done")
        if t_i == 0 and "dbg_rwkv" in dbg_t:
            ee.G(lambda g: g.dma_start(out=dbg_t["dbg_rwkv"][:, :], in_=rwkv[:]),
                 waits=ee.wv("v_done"), inc=("g_dma", 16))
            ee.G(lambda g, v=ee.now("g_dma"): g.wait_ge(sems["g_dma"], v))
            ee.barrier()

        for ts in range(TT // 128):
            def two(t, ts=ts):
                mm = None
                for dck in range(NOC):
                    lhs = rwkv[:, dck * TT + ts * 128:dck * TT + (ts + 1) * 128]
                    for nh in range(2):
                        mm = t.matmul(psO[:, nh * 512:(nh + 1) * 512], lhs,
                                      w4[:, WO_OFF + dck * D + nh * 512:
                                         WO_OFF + dck * D + (nh + 1) * 512],
                                      start=(dck == 0), stop=(dck == NOC - 1))
                return mm
            ee.Tn(two, waits=[("v_done", max(rwkv_done, ee.now("v_done")))],
                  inc=("t_done", 1))

            def vinner(v, ts=ts):
                v.tensor_scalar(out=inner_t[:, ts * D:(ts + 1) * D],
                                in0=x0t[:, ts * D:(ts + 1) * D],
                                scalar1=stb[:, 2 * ts:2 * ts + 1],
                                scalar2=stb[:, 22 + ts:23 + ts],
                                op0=ALU.subtract, op1=ALU.mult)
                return v.tensor_tensor(out=inner_t[:, ts * D:(ts + 1) * D],
                                       in0=inner_t[:, ts * D:(ts + 1) * D],
                                       in1=psO[:], op=ALU.add)
            ee.V(vinner, waits=ee.wv("t_done", "g_dma"))

        def ginner(g, pnc, t_i=t_i):
            batch, s, h2 = _role(pnc)
            if (t_i // (NTILE // 2)) == s:
                dst = scr["inner"][(t_i - s * (NTILE // 2)) * TT:
                                   (t_i - s * (NTILE // 2) + 1) * TT, :]
            else:
                dst = psh["inner"][t_i * TT:(t_i + 1) * TT, :]
            return g.dma_start(
                out=dst.rearrange("(c p) d -> p c d", p=128),
                in_=inner_t[:].rearrange("p (c d) -> p c d", d=D))
        ee.Gsw(ginner, waits=ee.wv("v_done"), inc=("g_dma", 16))

        if t_i == NTILE // 2 - 1:
            ee.G(lambda g, v=ee.now("g_dma"): g.wait_ge(sems["g_dma"], v))

            def gbnd(g, pnc):
                s = pnc % 2
                return g.dma_start(
                    out=psh["bnd"][s:s + 1, :],
                    in_=inner_t[127:128, (TT // 128 - 1) * D:(TT // 128) * D])
            ee.Gsw(gbnd, waits=ee.wv("v_done"), inc=("g_dma", 16))
            ee.G(lambda g, v=ee.now("g_dma"): g.wait_ge(sems["g_dma"], v))
            _send_flag(ee, psh, FL_BND)
            _send_flag(ee, psh, FL_INA)

        if last_tile:
            ee.G(lambda g, v=ee.now("g_dma"): g.wait_ge(sems["g_dma"], v))
            _send_flag(ee, psh, FL_INB)

    ee.G(lambda g, v=ee.now("g_dma"): g.wait_ge(sems["g_dma"], v))


def emit_whoami(ee, whoami):
    sems = ee.sems
    ee.G(lambda g: g.dma_start(out=whoami[:, :], in_=ee.t["id_t"][0:1, 0:1]),
         inc=("g_dma", 16))
    ee.G(lambda g, v=ee.now("g_dma"): g.wait_ge(sems["g_dma"], v))



def emit_p0w(ee, stack, inp, scr):
    nc = ee.nc

    def sb(name, shape, dt=F32):
        h = stack.enter_context(nc.sbuf_tensor(name, shape, dt))
        ee.t[name] = h
        return h

    def ps(name, shape, dt=F32):
        h = stack.enter_context(nc.psum_tensor(name, shape, dt))
        ee.t[name] = h
        return h

    wi32 = [sb("wi32%d" % p, [128, FFN], I32) for p in range(2)]
    wbfc = [sb("wbfc%d" % p, [128, FFN], BF16) for p in range(2)]
    tstage = [sb("tstage%d" % p, [128, FFN], BF16) for p in range(2)]
    psTw = [ps("psTw%d" % p, [128, FFN], BF16) for p in range(2)]

    items = []
    for wname, sname in (("L0_ffn_w_gate", "wgT"), ("L0_ffn_w_up", "wuT")):
        for hc in range(FFN // 128):
            items.append((wname, hc, NC8, sname))
    for ec in range(NC8):
        items.append(("L0_ffn_w_down", ec, FFN // 128, "wdT"))

    cast_done = [0, 0]
    store_done = [0, 0]
    stage_done = [0, 0]
    for idx, (wname, rc, nblk, sname) in enumerate(items):
        p = idx % 2
        ncol = nblk * 128
        ee.S(lambda e, wname=wname, rc=rc, ncol=ncol, p=p: e.dma_start(
            out=wi32[p][:, 0:ncol],
            in_=inp[wname][rc * 128:(rc + 1) * 128, :]),
            waits=[("v_done", cast_done[p])], inc=("s_dma", 16))
        ee.V(lambda v, p=p, ncol=ncol: v.tensor_copy(
            out=wbfc[p][:, 0:ncol], in_=wi32[p][:, 0:ncol]),
            waits=ee.wv("s_dma"))
        cast_done[p] = ee.now("v_done")

        def ttr(t, p=p, nblk=nblk):
            last = None
            for j in range(nblk):
                last = t.transpose(psTw[p][:, j * 128:(j + 1) * 128],
                                   wbfc[p][:, j * 128:(j + 1) * 128],
                                   ee.t["eye_bf"][:])
            return last
        ee.Tn(ttr, waits=[("v_done", cast_done[p]),
                          ("v_done", stage_done[p])], inc=("t_done", 1))
        ee.V(lambda v, p=p, ncol=ncol: v.tensor_copy(
            out=tstage[p][:, 0:ncol], in_=psTw[p][:, 0:ncol]),
            waits=ee.wv("t_done") + [("g_dma", store_done[p])])
        stage_done[p] = ee.now("v_done")

        ee.G(lambda g, sname=sname, rc=rc, p=p, ncol=ncol: g.dma_start(
            out=scr[sname][:, :, rc * 128:(rc + 1) * 128]
            .rearrange("c p h -> p c h"),
            in_=tstage[p][:, 0:ncol].rearrange("p (c h) -> p c h", h=128)),
            waits=ee.wv("v_done"), inc=("g_dma", 16))
        store_done[p] = ee.now("g_dma")
    ee.G(lambda g, v=ee.now("g_dma"): g.wait_ge(ee.sems["g_dma"], v))



def emit_p1b1(ee, stack, inp, x_in, scr, psh):
    nc = ee.nc
    sems = ee.sems
    NTILE = HALF // TT
    HTW = TT + 128
    NFC = FCM // 128

    def sb(name, shape, dt=F32):
        h = stack.enter_context(nc.sbuf_tensor(name, shape, dt))
        ee.t[name] = h
        return h

    def ps(name, shape, dt=F32):
        h = stack.enter_context(nc.psum_tensor(name, shape, dt))
        ee.t[name] = h
        return h

    cwkb = sb("cwkb", [128, NC8 * FCM], BF16)
    wld2 = sb("wld2", [128, FCM])
    rawb2 = sb("rawb2", [128, FCM], BF16)
    dcol2 = sb("dcol2", [128, 128])
    b2bf = sb("b2bf", [128, 8], BF16)
    b2m = sb("b2m", [128, 16], BF16)
    h2T = sb("h2T", [128, NC8 * HTW], BF16)
    mixCK = sb("mixCK", [128, NC8 * TT], BF16)
    mixCR = sb("mixCR", [128, NC8 * TT], BF16)
    inn_t = sb("inn_t", [128, (TT // 128) * D])
    h2tm = sb("h2tm", [128, D], BF16)
    ddsc2 = sb("ddsc2", [128, TT], BF16)
    kkbuf = sb("kkbuf", [128, NFC * TT], BF16)
    krel = sb("krel", [128, 2 * TT])
    stat6b = sb("stat6b", [128, 12])
    stat2b = sb("stat2b", [128, 2])
    stcb = sb("stcb", [128, 8])
    stcb2 = sb("stcb2", [128, 12])
    bndh2 = sb("bndh2", [1, D], BF16)
    bndst = sb("bndst", [1, 16])
    psA2 = ps("psA2", [128, 2 * TT])
    psT2 = ps("psT2", [128, 128], BF16)
    psb2 = ps("psb2", [128, 80])

    def vb2(v):
        v.memset(h2T[:], 0.0)
        v.tensor_copy(out=b2bf[:],
                      in_=ee.t["colpk"][:, CSLOT["b2"] * 8:CSLOT["b2"] * 8 + 8])
        for mi, mname in enumerate(("cmk", "cmr")):
            v.tensor_tensor(out=ee.t["b1mf"][:],
                            in0=ee.t["colpk"][:, CSLOT["b2"] * 8:CSLOT["b2"] * 8 + 8],
                            in1=ee.t["colpk"][:, CSLOT[mname] * 8:CSLOT[mname] * 8 + 8],
                            op=ALU.mult)
            v.tensor_tensor(out=ee.t["b1mf"][:],
                            in0=ee.t["colpk"][:, CSLOT["b2"] * 8:CSLOT["b2"] * 8 + 8],
                            in1=ee.t["b1mf"][:], op=ALU.subtract)
            v.tensor_copy(out=b2m[:, mi * 8:(mi + 1) * 8], in_=ee.t["b1mf"][:])
        return v.memset(bndh2[:], 0.0)
    ee.V(vb2)

    for c in range(NC8):
        ee.S(lambda e, c=c: e.dma_start(
            out=wld2[:], in_=inp["L0_attn_a_cwk"][c * 128:(c + 1) * 128, :]),
            waits=ee.wv("v_done"), inc=("s_dma", 16))
        ee.V(lambda v: v.tensor_copy(out=rawb2[:], in_=wld2[:]),
             waits=ee.wv("s_dma", "t_done"))

        def tb2(t, c=c):
            mm = None
            for fc in range(NFC):
                mm = t.matmul(psb2[:, fc:fc + 1],
                              rawb2[:, fc * 128:(fc + 1) * 128],
                              b2bf[:, c:c + 1],
                              start=(c == 0 and fc == 0), stop=False,
                              skip_group_check=True)
                mm = t.matmul(psb2[:, 32 + fc:32 + fc + 1],
                              rawb2[:, fc * 128:(fc + 1) * 128],
                              b2m[:, c:c + 1],
                              start=False, stop=False,
                              skip_group_check=True)
            return mm
        ee.Tn(tb2, waits=ee.wv("v_done"), inc=("t_done", 1))
        ee.V(lambda v, c=c: v.tensor_scalar(
            out=cwkb[:, c * FCM:(c + 1) * FCM], in0=wld2[:],
            scalar1=cslot(ee, "g2", c), scalar2=None, op0=ALU.mult),
            waits=ee.wv("s_dma"))
    cwrb = sb("cwrb", [128, NC8 * D], BF16)
    for c in range(NC8):
        ee.S(lambda e, c=c: e.dma_start(
            out=wld2[:, 0:D], in_=inp["L0_attn_a_cwr"][c * 128:(c + 1) * 128, :]),
            waits=ee.wv("v_done"), inc=("s_dma", 16))
        ee.V(lambda v: v.tensor_copy(out=rawb2[:, 0:D], in_=wld2[:, 0:D]),
             waits=ee.wv("s_dma", "t_done"))

        def tb3(t, c=c):
            mm = None
            for ec in range(NC8):
                mm = t.matmul(psb2[:, 64 + ec:64 + ec + 1],
                              rawb2[:, ec * 128:(ec + 1) * 128],
                              b2bf[:, c:c + 1],
                              start=False,
                              stop=False, skip_group_check=True)
                mm = t.matmul(psb2[:, 72 + ec:72 + ec + 1],
                              rawb2[:, ec * 128:(ec + 1) * 128],
                              b2m[:, 8 + c:8 + c + 1],
                              start=False,
                              stop=(c == NC8 - 1 and ec == NC8 - 1),
                              skip_group_check=True)
            return mm
        ee.Tn(tb3, waits=ee.wv("v_done"), inc=("t_done", 1))
        ee.V(lambda v, c=c: v.tensor_scalar(
            out=cwrb[:, c * D:(c + 1) * D], in0=wld2[:, 0:D],
            scalar1=cslot(ee, "g2", c), scalar2=None, op0=ALU.mult),
            waits=ee.wv("s_dma"))
    ee.Sc(lambda s: s.copy(out=dcol2[:, 0:80], in_=psb2[:, 0:80]),
          waits=ee.wv("t_done"))

    def gbndr(g, pnc):
        s = pnc % 2
        if s == 1:
            _emit_poll(ee, g, psh, FL_BND, want_side=0)
            g.reg_alu(ee.regs["r_pollcnt"], ee.regs["r_pollcnt"], 32, ALU.add)
            g.dma_start(out=ee.t["bnd_row"][:, :],
                        in_=psh["bnd"][0:1, :]).then_inc(sems["poll"], 16)
            g.dma_start(out=ee.t["bnd_loc"][:, :],
                        in_=psh["bnd"][1:2, :]).then_inc(sems["poll"], 16)
            g.wait_ge(sems["poll"], ee.regs["r_pollcnt"])
        return None
    ee.Gsw(gbndr, done_sem="g_done")

    def ginnp(g, pnc):
        s = pnc % 2
        if s == 0:
            _emit_poll(ee, g, psh, FL_INA, want_side=1)
        else:
            _emit_poll(ee, g, psh, FL_INB, want_side=0)
        return None
    ee.Gsw(ginnp, done_sem="g_done")

    ee.V(lambda v: v.scalar_tensor_tensor(
        out=ee.t["bnd_row"][:], in0=ee.t["bnd_loc"][:],
        scalar=ee.t["s_col"][0:1, 0:1], in1=ee.t["bnd_row"][:],
        op0=ALU.mult, op1=ALU.add),
        waits=ee.wv("g_done"))

    def vbndstat(v):
        v.bn_stats(out=bndst[:, 0:6], in_=ee.t["bnd_row"][:, 0:512])
        v.bn_stats(out=bndst[:, 6:12], in_=ee.t["bnd_row"][:, 512:1024])
        return v.bn_aggr(out=bndst[:, 12:14], in_=bndst[:, 0:12])
    ee.V(vbndstat, waits=ee.wv("g_done"))

    def scbnd(s):
        s.activation(out=bndst[:, 14:15], in_=bndst[:, 13:14], func=AF.Ln,
                     bias=ee.t["cbias"][0:1, 1:2])
        return s.activation(out=bndst[:, 14:15], in_=bndst[:, 14:15],
                            func=AF.Exp, scale=-0.5)
    ee.Sc(scbnd, waits=ee.wv("v_done"))
    ee.V(lambda v: v.tensor_scalar(out=bndh2[:, :], in0=ee.t["bnd_row"][:, :],
                                   scalar1=bndst[:, 12:13],
                                   scalar2=bndst[:, 14:15],
                                   op0=ALU.subtract, op1=ALU.mult),
         waits=ee.wv("sc_done"))
    vdn0 = ee.now("v_done")
    for dck in range(NC8):
        ee.Tn(lambda t, dck=dck: t.transpose(
            psT2[:, 0:1], bndh2[:, dck * 128:(dck + 1) * 128],
            ee.t["eye_bf"][0:1, 0:1]),
            waits=[("v_done", max(vdn0, ee.now("v_done")))], inc=("t_done", 1))
        ee.V(lambda v, dck=dck: v.tensor_copy(
            out=h2T[:, dck * HTW + 127:dck * HTW + 128],
            in_=psT2[:, 0:1]),
            waits=ee.wv("t_done"))

    prev_wb = 0
    evac_done = {0: 0, 1: 0}
    for t_i in range(NTILE):
        ee.S(lambda e, t_i=t_i: e.dma_start(
            out=inn_t[:].rearrange("p (c d) -> p c d", d=D),
            in_=scr["inner"][t_i * TT:(t_i + 1) * TT, :]
            .rearrange("(c p) d -> p c d", p=128)),
            waits=ee.wv("v_done") + [("g_dma", prev_wb)], inc=("s_dma", 16))

        inn2 = wld2

        def ginn2(g, pnc, t_i=t_i):
            batch, s, h2 = _role(pnc)
            rb = s * HALF + t_i * TT
            return g.dma_start(
                out=inn2[:].rearrange("p (c d) -> p c d", d=D),
                in_=psh["inner"][rb:rb + TT, :]
                .rearrange("(c p) d -> p c d", p=128))
        ee.Gsw(ginn2, waits=ee.wv("v_done") + [("g_done", ee.now("g_done"))],
               inc=("g_dma", 16))
        s_loaded = ee.now("s_dma")
        ee.V(lambda v: v.tensor_tensor(out=inn_t[:], in0=inn_t[:], in1=inn2[:],
                                       op=ALU.add),
             waits=[("s_dma", s_loaded), ("g_dma", ee.now("g_dma"))])
        ee.G(lambda g, t_i=t_i: g.dma_start(
            out=scr["inner"][t_i * TT:(t_i + 1) * TT, :]
            .rearrange("(c p) d -> p c d", p=128),
            in_=inn_t[:].rearrange("p (c d) -> p c d", d=D)),
            waits=ee.wv("v_done"), inc=("g_dma", 16))
        prev_wb = ee.now("g_dma")
        if t_i > 0:
            def vbcol2(v):
                last = None
                for dck in range(NC8):
                    last = v.tensor_copy(
                        out=h2T[:, dck * HTW + 127:dck * HTW + 128],
                        in_=h2T[:, dck * HTW + 128 + TT - 1:dck * HTW + 128 + TT])
                return last
            ee.V(vbcol2)
        for rcix in range(TT // 128):
            srcc = inn_t[:, rcix * D:(rcix + 1) * D]

            def vstatb(v, srcc=srcc, rcix=rcix):
                v.bn_stats(out=stat6b[:, 0:6], in_=srcc[:, 0:512])
                v.bn_stats(out=stat6b[:, 6:12], in_=srcc[:, 512:1024])
                return v.bn_aggr(out=stcb2[:, 2 * rcix:2 * rcix + 2],
                                 in_=stat6b[:])
            ee.V(vstatb, waits=[("s_dma", s_loaded),
                                ("sc_done", ee.now("sc_done"))])

        def scstb_all(s):
            s.activation(out=stcb2[:, 8:12], in_=stcb2[:, 1:8:2], func=AF.Ln,
                         bias=ee.t["cbias"][:, 1:2])
            return s.activation(out=stcb2[:, 8:12], in_=stcb2[:, 8:12],
                                func=AF.Exp, scale=-0.5)
        ee.Sc(scstb_all, waits=ee.wv("v_done"))
        bstats_done = ee.now("sc_done")
        for rcix in range(TT // 128):
            srcc = inn_t[:, rcix * D:(rcix + 1) * D]
            ee.V(lambda v, srcc=srcc, rcix=rcix: v.tensor_scalar(
                out=h2tm[:], in0=srcc, scalar1=stcb2[:, 2 * rcix:2 * rcix + 1],
                scalar2=stcb2[:, 8 + rcix:9 + rcix],
                op0=ALU.subtract, op1=ALU.mult),
                waits=[("sc_done", bstats_done), ("t_done", ee.now("t_done"))])
            vdn = ee.now("v_done")
            for dck in range(NC8):
                ee.Tn(lambda t, dck=dck:
                      t.transpose(psT2[:], h2tm[:, dck * 128:(dck + 1) * 128],
                                  ee.t["eye_bf"][:]),
                      waits=[("v_done", max(vdn, ee.now("v_done")))],
                      inc=("t_done", 1))
                dst = h2T[:, dck * HTW + 128 + rcix * 128:
                          dck * HTW + 128 + (rcix + 1) * 128]
                ee.V(lambda v, dst=dst: v.tensor_copy(out=dst, in_=psT2[:]),
                     waits=ee.wv("t_done"))

        def vmix2(v):
            last = None
            for dck in range(NC8):
                h = h2T[:, dck * HTW + 128:dck * HTW + 128 + TT]
                hx = h2T[:, dck * HTW + 127:dck * HTW + 127 + TT]
                v.tensor_tensor(out=ddsc2[:], in0=h, in1=hx, op=ALU.subtract)
                for mname, mt in (("cmk", mixCK), ("cmr", mixCR)):
                    last = v.scalar_tensor_tensor(
                        out=mt[:, dck * TT:(dck + 1) * TT], in0=ddsc2[:],
                        scalar=cslot(ee, mname, dck), in1=hx,
                        op0=ALU.mult, op1=ALU.add)
            return last
        ee.V(vmix2, waits=ee.wv("t_done"))
        mix_done = ee.now("v_done")
        ee.G(lambda g, t_i=t_i: g.dma_start(
            out=scr["mcr"][:, :, t_i * TT:(t_i + 1) * TT]
            .rearrange("c p t -> p c t"),
            in_=mixCR[:].rearrange("p (c t) -> p c t", t=TT)),
            waits=ee.wv("v_done"), inc=("g_dma", 16))

        sc_base = ee.now("sc_done")
        kk_items = []
        for fc in range(NFC):
            pp = fc % 2

            def tkk(t, fc=fc, pp=pp):
                mm = None
                for dck in range(NC8):
                    mm = t.matmul(psA2[:, pp * TT:(pp + 1) * TT],
                                  cwkb[:, dck * FCM + fc * 128:dck * FCM + (fc + 1) * 128],
                                  mixCK[:, dck * TT:(dck + 1) * TT],
                                  start=(dck == 0), stop=(dck == NC8 - 1))
                return mm
            ee.Tn(tkk, waits=[("v_done", mix_done),
                              ("sc_done", evac_done[pp])], inc=("t_done", 1))
            tk_cnt = ee.now("t_done")
            if t_i == 0:
                ee.Sc(lambda s, fc=fc, pp=pp: s.activation(
                    out=krel[:, pp * TT:(pp + 1) * TT],
                    in_=psA2[:, pp * TT:(pp + 1) * TT], func=AF.Identity,
                    bias=dcol2[:, fc:fc + 1]),
                    waits=[("t_done", tk_cnt)])

                def vfixkk(v, fc=fc, pp=pp):
                    v.tensor_tensor(out=stcb[:, 5:6], in0=dcol2[:, 32 + fc:32 + fc + 1],
                                    in1=ee.t["ims_col"][:, 0:1], op=ALU.mult)
                    return v.tensor_tensor(out=krel[:, pp * TT:pp * TT + 1],
                                           in0=krel[:, pp * TT:pp * TT + 1],
                                           in1=stcb[:, 5:6], op=ALU.subtract)
                ee.V(vfixkk, waits=ee.wv("sc_done"))
                ee.Sc(lambda s, pp=pp: s.activation(
                    out=krel[:, pp * TT:(pp + 1) * TT],
                    in_=krel[:, pp * TT:(pp + 1) * TT], func=AF.Relu),
                    waits=ee.wv("v_done"))
                evac_done[pp] = ee.now("sc_done")
                ev_cnt = evac_done[pp]
            else:
                kk_items.append((
                    lambda s, fc=fc, pp=pp: s.activation(
                        out=krel[:, pp * TT:(pp + 1) * TT],
                        in_=psA2[:, pp * TT:(pp + 1) * TT], func=AF.Relu,
                        bias=dcol2[:, fc:fc + 1]),
                    [("t_done", tk_cnt)], None))
                evac_done[pp] = sc_base + len(kk_items)
                ev_cnt = evac_done[pp]
            ee.V(lambda v, fc=fc, pp=pp: v.tensor_tensor(
                out=kkbuf[:, fc * TT:(fc + 1) * TT],
                in0=krel[:, pp * TT:(pp + 1) * TT],
                in1=krel[:, pp * TT:(pp + 1) * TT],
                op=ALU.mult),
                waits=[("sc_done", ev_cnt)])
        if kk_items:
            ee.Sc_batch(kk_items)
        ee.G(lambda g, t_i=t_i: g.dma_start(
            out=scr["kk2"][:, :, t_i * TT:(t_i + 1) * TT]
            .rearrange("c p t -> p c t"),
            in_=kkbuf[:].rearrange("p (c t) -> p c t", t=TT)),
            waits=ee.wv("v_done"), inc=("g_dma", 16))
    ee.G(lambda g, v=ee.now("g_dma"): g.wait_ge(sems["g_dma"], v))
    ee.Sc(lambda s: s.copy(out=ee.t["dcol"][:, 72:88], in_=dcol2[:, 64:80]),
          waits=ee.wv("sc_done"))



def emit_p1b2(ee, stack, inp, x_in, scr, psh):
    nc = ee.nc
    sems = ee.sems
    NFC = FCM // 128
    NG = HALF // TT

    def sb(name, shape, dt=F32):
        h = stack.enter_context(nc.sbuf_tensor(name, shape, dt))
        ee.t[name] = h
        return h

    def ps(name, shape, dt=F32):
        h = stack.enter_context(nc.psum_tensor(name, shape, dt))
        ee.t[name] = h
        return h

    cwvb = sb("cwvb", [128, NFC * D], BF16)
    cwrb2 = sb("cwrb2", [128, NC8 * D], BF16)
    wld3 = sb("wld3", [128, D])
    kk2g = sb("kk2g", [128, NFC * TT], BF16)
    mcrg = sb("mcrg", [128, NC8 * TT], BF16)
    x0g = sb("x0g", [128, (TT // 128) * D])
    inng = sb("inng", [128, (TT // 128) * D])
    rrg = sb("rrg", [128, (TT // 128) * D])
    bcrb = sb("bcrb", [128, D])
    ccrr = sb("ccrr", [1, D])
    onebf = sb("onebf", [1, 128], BF16)
    rowbf = sb("rowbf", [1, D], BF16)
    psM = [ps("psM%d" % i, [128, D]) for i in range(4)]

    for c in range(NFC):
        ee.S(lambda e, c=c: e.dma_start(
            out=wld3[:], in_=inp["L0_attn_a_cwv"][c * 128:(c + 1) * 128, :]),
            waits=ee.wv("v_done"), inc=("s_dma", 16))
        ee.V(lambda v, c=c: v.tensor_copy(out=cwvb[:, c * D:(c + 1) * D],
                                          in_=wld3[:]),
             waits=ee.wv("s_dma"))
    for c in range(NC8):
        ee.S(lambda e, c=c: e.dma_start(
            out=wld3[:], in_=inp["L0_attn_a_cwr"][c * 128:(c + 1) * 128, :]),
            waits=ee.wv("v_done"), inc=("s_dma", 16))
        ee.V(lambda v, c=c: v.tensor_scalar(
            out=cwrb2[:, c * D:(c + 1) * D], in0=wld3[:],
            scalar1=cslot(ee, "g2", c), scalar2=None, op0=ALU.mult),
            waits=ee.wv("s_dma"))
    ee.V(lambda v: v.memset(onebf[:], 1.0))
    colbf = sb("colbf", [128, 16], BF16)
    ee.V(lambda v: v.tensor_copy(out=colbf[:], in_=ee.t["dcol"][:, 72:88]),
         waits=ee.wv("sc_done"))
    for ec in range(NC8):
        ee.Tn(lambda t, ec=ec: t.matmul(psM[0][0:1, 0:128], colbf[:, ec:ec + 1],
                                        ee.t["eye_bf"][:], start=True, stop=True),
              waits=ee.wv("v_done"), inc=("t_done", 1))
        ee.V(lambda v, ec=ec: v.tensor_copy(out=rowbf[:, ec * 128:(ec + 1) * 128],
                                            in_=psM[0][0:1, 0:128]),
             waits=ee.wv("t_done"))
    for ec in range(NC8):
        ee.Tn(lambda t, ec=ec: t.matmul(psM[0][0:1, 0:128], colbf[:, 8 + ec:8 + ec + 1],
                                        ee.t["eye_bf"][:], start=True, stop=True),
              waits=ee.wv("v_done"), inc=("t_done", 1))
        ee.V(lambda v, ec=ec: v.tensor_copy(out=ccrr[:, ec * 128:(ec + 1) * 128],
                                            in_=psM[0][0:1, 0:128]),
             waits=ee.wv("t_done"))
    for nh in range(2):
        ee.Tn(lambda t, nh=nh: t.matmul(psM[0][:, nh * 512:(nh + 1) * 512],
                                        onebf[:, 0:128],
                                        rowbf[:, nh * 512:(nh + 1) * 512],
                                        start=True, stop=True),
              waits=ee.wv("v_done"), inc=("t_done", 1))
    ee.V(lambda v: v.tensor_copy(out=bcrb[:], in_=psM[0][:]),
         waits=ee.wv("t_done"))

    for gi in range(NG):
        ee.S(lambda e, gi=gi: e.dma_start(
            out=kk2g[:].rearrange("p (c t) -> p c t", t=TT),
            in_=scr["kk2"][:, :, gi * TT:(gi + 1) * TT]
            .rearrange("c p t -> p c t")),
            waits=ee.wv("v_done", "t_done"), inc=("s_dma", 16))
        ee.S(lambda e, gi=gi: e.dma_start(
            out=mcrg[:].rearrange("p (c t) -> p c t", t=TT),
            in_=scr["mcr"][:, :, gi * TT:(gi + 1) * TT]
            .rearrange("c p t -> p c t")),
            inc=("s_dma", 16))
        ee.S(lambda e, gi=gi: e.dma_start(
            out=inng[:].rearrange("p (c d) -> p c d", d=D),
            in_=scr["inner"][gi * TT:(gi + 1) * TT, :]
            .rearrange("(c p) d -> p c d", p=128)),
            inc=("s_dma", 16))

        def gx0(g, pnc, gi=gi):
            batch, s, h2 = _role(pnc)
            xb = batch * T + s * HALF + gi * TT
            return g.dma_start(
                out=x0g[:].rearrange("p (c d) -> p c d", d=D),
                in_=x_in[xb:xb + TT, :].rearrange("(c p) d -> p c d", p=128))
        ee.Gsw(gx0, waits=ee.wv("v_done"), inc=("g_dma", 16))
        ld = [("s_dma", ee.now("s_dma")), ("g_dma", ee.now("g_dma"))]

        for ts in range(TT // 128):
            def trr(t, ts=ts):
                mm = None
                for dck in range(NC8):
                    lhs = mcrg[:, dck * TT + ts * 128:dck * TT + (ts + 1) * 128]
                    for nh in range(2):
                        mm = t.matmul(psM[ts][:, nh * 512:(nh + 1) * 512], lhs,
                                      cwrb2[:, dck * D + nh * 512:dck * D + (nh + 1) * 512],
                                      start=(dck == 0), stop=(dck == NC8 - 1))
                return mm
            ee.Tn(trr, waits=ld + ee.wv("v_done"), inc=("t_done", 1))

            def vrr(v, ts=ts, gi=gi):
                v.tensor_tensor(out=rrg[:, ts * D:(ts + 1) * D], in0=psM[ts][:],
                                in1=bcrb[:], op=ALU.add)
                if gi == 0 and ts == 0:
                    v.tensor_scalar(out=ccrr[:, :], in0=ccrr[:, :],
                                    scalar1=ee.t["ims_col"][0:1, 0:1],
                                    scalar2=None, op0=ALU.mult)
                    return v.tensor_tensor(out=rrg[0:1, 0:D],
                                           in0=rrg[0:1, 0:D], in1=ccrr[:, :],
                                           op=ALU.subtract)
                return None
            wl = ee.wv("t_done")

            def vrr2(v, vrr=vrr):
                r = vrr(v)
                return r
            ee.V(vrr2, waits=wl)
        ee.Sc(lambda s: s.activation(out=rrg[:], in_=rrg[:], func=AF.Tanh,
                                     scale=0.5),
              waits=ee.wv("v_done"))
        ee.V(lambda v: v.tensor_scalar(out=rrg[:], in0=rrg[:], scalar1=0.5,
                                       scalar2=0.5, op0=ALU.mult, op1=ALU.add),
             waits=ee.wv("sc_done"))

        for ts in range(TT // 128):
            def tcm(t, ts=ts):
                mm = None
                for fc in range(NFC):
                    lhs = kk2g[:, fc * TT + ts * 128:fc * TT + (ts + 1) * 128]
                    for nh in range(2):
                        mm = t.matmul(psM[ts][:, nh * 512:(nh + 1) * 512], lhs,
                                      cwvb[:, fc * D + nh * 512:fc * D + (nh + 1) * 512],
                                      start=(fc == 0), stop=(fc == NFC - 1))
                return mm
            ee.Tn(tcm, waits=ee.wv("sc_done") + [("v_done", ee.now("v_done"))],
                  inc=("t_done", 1))

            def vx1(v, ts=ts):
                sl = slice(ts * D, (ts + 1) * D)
                v.tensor_tensor(out=x0g[:, sl], in0=x0g[:, sl], in1=inng[:, sl],
                                op=ALU.add)
                v.scalar_tensor_tensor(out=rrg[:, sl], in0=rrg[:, sl],
                                       scalar=1.0, in1=psM[ts][:],
                                       op0=ALU.mult, op1=ALU.mult)
                return v.tensor_tensor(out=inng[:, sl], in0=x0g[:, sl],
                                       in1=rrg[:, sl], op=ALU.add)
            ee.V(vx1, waits=ee.wv("t_done"))
        ee.G(lambda g, gi=gi: g.dma_start(
            out=scr["x1"][gi * TT:(gi + 1) * TT, :]
            .rearrange("(c p) d -> p c d", p=128),
            in_=inng[:].rearrange("p (c d) -> p c d", d=D)),
            waits=ee.wv("v_done"), inc=("g_dma", 16))
    ee.G(lambda g, v=ee.now("g_dma"): g.wait_ge(sems["g_dma"], v))



def emit_ffn(ee, stack, inp, scr, src_name, dst, ntok, dbgtag=""):
    nc = ee.nc
    sems = ee.sems
    NH = FFN // 128
    NG = ntok // TT

    def sb(name, shape, dt=F32):
        h = stack.enter_context(nc.sbuf_tensor(dbgtag + name, shape, dt))
        ee.t[dbgtag + name] = h
        return h

    def ps(name, shape, dt=F32):
        h = stack.enter_context(nc.psum_tensor(dbgtag + name, shape, dt))
        ee.t[dbgtag + name] = h
        return h

    inner1 = contextlib.ExitStack()

    def sb1(name, shape, dt=F32):
        h = inner1.enter_context(nc.sbuf_tensor(dbgtag + name, shape, dt))
        ee.t[dbgtag + name] = h
        return h
    wgTb = sb1("wgTb", [128, NC8 * FFN], BF16)
    wuTb = sb1("wuTb", [128, NC8 * FFN], BF16)
    xsg = sb1("xsg", [128, (TT // 128) * D])
    xbu = sb1("xbu", [128, D], BF16)
    xbT = sb1("xbT", [128, NC8 * TT], BF16)
    hhb = sb1("hhb", [128, NH * TT], BF16)
    gtmp = sb1("gtmp", [128, TT], BF16)
    stat6f = sb1("stat6f", [128, 12])
    stat2f = sb1("stat2f", [128, 2])
    stcf = sb1("stcf", [128, 12])
    psF = ee.t[dbgtag + "psF"] = inner1.enter_context(
        nc.psum_tensor(dbgtag + "psF", [128, TT], F32))
    psG = ee.t[dbgtag + "psG"] = inner1.enter_context(
        nc.psum_tensor(dbgtag + "psG", [128, TT], F32))
    psTf = ee.t[dbgtag + "psTf"] = inner1.enter_context(
        nc.psum_tensor(dbgtag + "psTf", [128, 128], BF16))

    for dc in range(NC8):
        ee.S(lambda e, dc=dc: e.dma_start(out=wgTb[:, dc * FFN:(dc + 1) * FFN],
                                          in_=scr["wgT"][dc, :, :]),
             inc=("s_dma", 16))
        ee.S(lambda e, dc=dc: e.dma_start(out=wuTb[:, dc * FFN:(dc + 1) * FFN],
                                          in_=scr["wuT"][dc, :, :]),
             inc=("s_dma", 16))
    wload = ee.now("s_dma")

    hh_parts = []
    for gi in range(NG):
        ee.S(lambda e, gi=gi: e.dma_start(
            out=xsg[:].rearrange("p (c d) -> p c d", d=D),
            in_=scr[src_name][gi * TT:(gi + 1) * TT, :]
            .rearrange("(c p) d -> p c d", p=128)),
            waits=ee.wv("v_done"), inc=("s_dma", 16))
        sld = ee.now("s_dma")
        for rc in range(TT // 128):
            srcc = xsg[:, rc * D:(rc + 1) * D]

            def vstf(v, srcc=srcc, rc=rc):
                v.bn_stats(out=stat6f[:, 0:6], in_=srcc[:, 0:512])
                v.bn_stats(out=stat6f[:, 6:12], in_=srcc[:, 512:1024])
                return v.bn_aggr(out=stcf[:, 2 * rc:2 * rc + 2], in_=stat6f[:])
            ee.V(vstf, waits=[("s_dma", sld), ("sc_done", ee.now("sc_done"))])

        def scstf_all(s):
            s.activation(out=stcf[:, 8:12], in_=stcf[:, 1:8:2], func=AF.Ln,
                         bias=ee.t["cbias"][:, 1:2])
            return s.activation(out=stcf[:, 8:12], in_=stcf[:, 8:12],
                                func=AF.Exp, scale=-0.5)
        ee.Sc(scstf_all, waits=ee.wv("v_done"))
        fstats_done = ee.now("sc_done")
        for rc in range(TT // 128):
            srcc = xsg[:, rc * D:(rc + 1) * D]
            ee.V(lambda v, srcc=srcc, rc=rc: v.tensor_scalar(
                out=xbu[:], in0=srcc, scalar1=stcf[:, 2 * rc:2 * rc + 1],
                scalar2=stcf[:, 8 + rc:9 + rc], op0=ALU.subtract, op1=ALU.mult),
                waits=[("sc_done", fstats_done), ("t_done", ee.now("t_done"))])
            vdn = ee.now("v_done")
            for dck in range(NC8):
                ee.Tn(lambda t, dck=dck:
                      t.transpose(psTf[:], xbu[:, dck * 128:(dck + 1) * 128],
                                  ee.t["eye_bf"][:]),
                      waits=[("v_done", max(vdn, ee.now("v_done")))],
                      inc=("t_done", 1))
                dstc = xbT[:, dck * TT + rc * 128:dck * TT + (rc + 1) * 128]
                ee.V(lambda v, dstc=dstc: v.tensor_copy(out=dstc, in_=psTf[:]),
                     waits=ee.wv("t_done"))
        xT_done = ee.now("v_done")
        for hc in range(NH):
            def tgate(t, hc=hc):
                mm = None
                for dck in range(NC8):
                    mm = t.matmul(psF[:],
                                  wgTb[:, dck * FFN + hc * 128:dck * FFN + (hc + 1) * 128],
                                  xbT[:, dck * TT:(dck + 1) * TT],
                                  start=(dck == 0), stop=(dck == NC8 - 1))
                return mm
            ee.Tn(tgate, waits=[("v_done", max(xT_done, ee.now("v_done"))),
                                ("s_dma", wload),
                                ("sc_done", ee.now("sc_done"))],
                  inc=("t_done", 1))
            ee.Sc(lambda s: s.activation(out=gtmp[:], in_=psF[:], func=AF.Silu,
                                         scale=1.0 / 127.0),
                  waits=ee.wv("t_done"))
            g_ev = ee.now("sc_done")

            def tup(t, hc=hc):
                mm = None
                for dck in range(NC8):
                    mm = t.matmul(psG[:],
                                  wuTb[:, dck * FFN + hc * 128:dck * FFN + (hc + 1) * 128],
                                  xbT[:, dck * TT:(dck + 1) * TT],
                                  start=(dck == 0), stop=(dck == NC8 - 1))
                return mm
            ee.Tn(tup, waits=[("v_done", ee.now("v_done"))], inc=("t_done", 1))
            ee.V(lambda v, hc=hc: v.scalar_tensor_tensor(
                out=hhb[:, hc * TT:(hc + 1) * TT], in0=gtmp[:], scalar=1.0,
                in1=psG[:], op0=ALU.mult, op1=ALU.mult),
                waits=[("sc_done", g_ev), ("t_done", ee.now("t_done"))])
        ee.G(lambda g, gi=gi: g.dma_start(
            out=scr["hh"][:, :, gi * TT:(gi + 1) * TT]
            .rearrange("c p t -> p c t"),
            in_=hhb[:].rearrange("p (c t) -> p c t", t=TT)),
            waits=ee.wv("v_done"), inc=("g_dma", 16))
    ee.G(lambda g, v=ee.now("g_dma"): g.wait_ge(sems["g_dma"], v))
    ee.barrier()
    inner1.close()

    wdTb = sb("wdTb", [128, NH * D], BF16)
    hhg = sb("hhg", [128, NH * TT], BF16)
    xrg = sb("xrg", [128, (TT // 128) * D])
    psD = [ps("psD%d" % i, [128, D]) for i in range(4)]
    for fc in range(NH):
        ee.S(lambda e, fc=fc: e.dma_start(out=wdTb[:, fc * D:(fc + 1) * D],
                                          in_=scr["wdT"][fc, :, :]),
             inc=("s_dma", 16))
    wdl = ee.now("s_dma")
    for gi in range(NG):
        ee.S(lambda e, gi=gi: e.dma_start(
            out=hhg[:].rearrange("p (c t) -> p c t", t=TT),
            in_=scr["hh"][:, :, gi * TT:(gi + 1) * TT]
            .rearrange("c p t -> p c t")),
            waits=ee.wv("v_done"), inc=("s_dma", 16))
        ee.S(lambda e, gi=gi: e.dma_start(
            out=xrg[:].rearrange("p (c d) -> p c d", d=D),
            in_=scr[src_name][gi * TT:(gi + 1) * TT, :]
            .rearrange("(c p) d -> p c d", p=128)),
            inc=("s_dma", 16))
        sld = ee.now("s_dma")
        for ts in range(TT // 128):
            def tdn(t, ts=ts):
                mm = None
                for fc in range(NH):
                    lhs = hhg[:, fc * TT + ts * 128:fc * TT + (ts + 1) * 128]
                    for nh in range(2):
                        mm = t.matmul(psD[ts][:, nh * 512:(nh + 1) * 512], lhs,
                                      wdTb[:, fc * D + nh * 512:fc * D + (nh + 1) * 512],
                                      start=(fc == 0), stop=(fc == NH - 1))
                return mm
            ee.Tn(tdn, waits=[("s_dma", max(sld, wdl)),
                              ("v_done", ee.now("v_done"))],
                  inc=("t_done", 1))

            def vx2(v, ts=ts):
                sl = slice(ts * D, (ts + 1) * D)
                v.tensor_scalar(out=xrg[:, sl], in0=xrg[:, sl], scalar1=1.0,
                                scalar2=None, op0=ALU.mult)
                return v.scalar_tensor_tensor(
                    out=xrg[:, sl], in0=psD[ts][:],
                    scalar=1.0 / (127.0 * 127.0), in1=xrg[:, sl],
                    op0=ALU.mult, op1=ALU.add)
            ee.V(vx2, waits=ee.wv("t_done"))
        ee.G(lambda g, gi=gi: g.dma_start(
            out=dst[gi * TT:(gi + 1) * TT, :].rearrange("(c p) d -> p c d", p=128),
            in_=xrg[:].rearrange("p (c d) -> p c d", d=D)),
            waits=ee.wv("v_done"), inc=("g_dma", 16))
    ee.G(lambda g, v=ee.now("g_dma"): g.wait_ge(sems["g_dma"], v))



def emit_p3(ee, inp, scr, psh):
    nc = ee.nc
    sems = ee.sems
    import contextlib as _ctx
    SH = 16

    with _ctx.ExitStack() as ost:
        def osb(name, shape, dt=F32):
            h = ost.enter_context(nc.sbuf_tensor(name, shape, dt))
            ee.t[name] = h
            return h
        qTb = None
        attoT = None

        with _ctx.ExitStack() as ast_:
            def asb(name, shape, dt=F32):
                h = ast_.enter_context(nc.sbuf_tensor(name, shape, dt))
                ee.t[name] = h
                return h
            def aps(name, shape, dt=F32):
                h = ast_.enter_context(nc.psum_tensor(name, shape, dt))
                ee.t[name] = h
                return h
            xcT = asb("xcT", [128, NC8 * HALF], BF16)
            x2g = asb("x2g", [128, (TT // 128) * D])
            xcu = asb("xcu", [128, D], BF16)
            wkb = asb("wkb", [128, NC8 * NKV], BF16)
            wvb = asb("wvb", [128, NC8 * NKV], BF16)
            wldp = asb("wldp", [128, NKV])
            ctab = asb("ctab", [128, 1024])
            stab = asb("stab", [128, 1024])
            ktmp = asb("ktmp", [128, 1024])
            kshu = asb("kshu", [128, 1024])
            krop = asb("krop", [128, 1024])
            kvst = asb("kvst", [128, 4 * HALF], BF16)
            st6p = asb("st6p", [128, 12])
            st2p = asb("st2p", [128, 2])
            stcp = asb("stcp", [128, 12])
            psP = aps("psP", [128, 1024])
            psTp = aps("psTp", [128, 128], BF16)

            for c in range(NC8):
                ee.S(lambda e, c=c: e.dma_start(
                    out=wldp[:], in_=inp["L0_attn_b_Wk"][c * 128:(c + 1) * 128, :]),
                    waits=ee.wv("v_done"), inc=("s_dma", 16))
                ee.V(lambda v, c=c: v.tensor_copy(out=wkb[:, c * NKV:(c + 1) * NKV],
                                                  in_=wldp[:]),
                     waits=ee.wv("s_dma"))
                ee.S(lambda e, c=c: e.dma_start(
                    out=wldp[:], in_=inp["L0_attn_b_Wv"][c * 128:(c + 1) * 128, :]),
                    waits=ee.wv("v_done"), inc=("s_dma", 16))
                ee.V(lambda v, c=c: v.tensor_copy(out=wvb[:, c * NKV:(c + 1) * NKV],
                                                  in_=wldp[:]),
                     waits=ee.wv("s_dma"))

            for gi in range(HALF // TT):
                ee.S(lambda e, gi=gi: e.dma_start(
                    out=x2g[:].rearrange("p (c d) -> p c d", d=D),
                    in_=scr["x2"][gi * TT:(gi + 1) * TT, :]
                    .rearrange("(c p) d -> p c d", p=128)),
                    waits=ee.wv("v_done"), inc=("s_dma", 16))
                sld = ee.now("s_dma")
                for rc in range(TT // 128):
                    srcc = x2g[:, rc * D:(rc + 1) * D]

                    def vstp(v, srcc=srcc, rc=rc):
                        v.bn_stats(out=st6p[:, 0:6], in_=srcc[:, 0:512])
                        v.bn_stats(out=st6p[:, 6:12], in_=srcc[:, 512:1024])
                        return v.bn_aggr(out=stcp[:, 2 * rc:2 * rc + 2],
                                         in_=st6p[:])
                    ee.V(vstp, waits=[("s_dma", sld),
                                      ("sc_done", ee.now("sc_done"))])

                def scp_all(s):
                    s.activation(out=stcp[:, 8:12], in_=stcp[:, 1:8:2],
                                 func=AF.Ln, bias=ee.t["cbias"][:, 1:2])
                    return s.activation(out=stcp[:, 8:12], in_=stcp[:, 8:12],
                                        func=AF.Exp, scale=-0.5)
                ee.Sc(scp_all, waits=ee.wv("v_done"))
                pstats_done = ee.now("sc_done")
                for rc in range(TT // 128):
                    srcc = x2g[:, rc * D:(rc + 1) * D]
                    ee.V(lambda v, srcc=srcc, rc=rc: v.tensor_scalar(
                        out=xcu[:], in0=srcc, scalar1=stcp[:, 2 * rc:2 * rc + 1],
                        scalar2=stcp[:, 8 + rc:9 + rc],
                        op0=ALU.subtract, op1=ALU.mult),
                        waits=[("sc_done", pstats_done),
                               ("t_done", ee.now("t_done"))])
                    vdn = ee.now("v_done")
                    for dck in range(NC8):
                        ee.Tn(lambda t, dck=dck: t.transpose(
                            psTp[:, 0:128], xcu[:, dck * 128:(dck + 1) * 128],
                            ee.t["eye_bf"][:]),
                            waits=[("v_done", max(vdn, ee.now("v_done")))],
                            inc=("t_done", 1))
                        dstc = xcT[:, dck * HALF + gi * TT + rc * 128:
                                   dck * HALF + gi * TT + (rc + 1) * 128]
                        ee.V(lambda v, dstc=dstc: v.tensor_copy(out=dstc,
                                                                in_=psTp[:, 0:128]),
                             waits=ee.wv("t_done"))
            xcT_done = ee.now("v_done")

            def gtab(g, pnc, tb=None, which=None):
                pass
            for tb in range(HALF // 1024):
                def gct(g, pnc, tb=tb):
                    batch, s, h2 = _role(pnc)
                    spos = s * HALF
                    g.dma_start(out=ctab[:],
                                in_=inp["rope_cos"][:, spos + tb * 1024:
                                                    spos + (tb + 1) * 1024]
                                ).then_inc(sems["g_dma"], 16)
                    return g.dma_start(out=stab[:],
                                       in_=inp["rope_sin"][:, spos + tb * 1024:
                                                           spos + (tb + 1) * 1024])
                ee.Gsw(gct, waits=ee.wv("v_done"), inc=("g_dma", 16))
                ee.cnt["g_dma"] += 16
                tabld = ee.now("g_dma")
                for kc4 in range(4):
                    def tkp(t, tb=tb, kc4=kc4):
                        mm = None
                        for dck in range(NC8):
                            for nn in range(2):
                                mm = t.matmul(
                                    psP[:, nn * 512:(nn + 1) * 512],
                                    wkb[:, dck * NKV + kc4 * 128:dck * NKV + (kc4 + 1) * 128],
                                    xcT[:, dck * HALF + tb * 1024 + nn * 512:
                                        dck * HALF + tb * 1024 + (nn + 1) * 512],
                                    start=(dck == 0), stop=(dck == NC8 - 1))
                        return mm
                    ee.Tn(tkp, waits=[("v_done", max(xcT_done, ee.now("v_done"))),
                                      ("sc_done", ee.now("sc_done"))],
                          inc=("t_done", 1))
                    ee.Sc(lambda s: s.copy(out=ktmp[:], in_=psP[:]),
                          waits=ee.wv("t_done"))

                    def vrope(v, tb=tb, kc4=kc4):
                        mask = []
                        for i in range(0, 32, 2):
                            mask += [i + 1, i]
                        v.stream_shuffle(out=kshu[:], in_=ktmp[:], mask=mask)
                        v.tensor_tensor(out=krop[:], in0=ktmp[:], in1=ctab[:],
                                        op=ALU.mult)
                        v.scalar_tensor_tensor(out=kshu[:], in0=kshu[:],
                                               scalar=1.0, in1=stab[:],
                                               op0=ALU.mult, op1=ALU.mult)
                        return v.tensor_tensor(
                            out=kvst[:, kc4 * HALF + tb * 1024:
                                     kc4 * HALF + (tb + 1) * 1024],
                            in0=krop[:], in1=kshu[:], op=ALU.add)
                    ee.V(vrope, waits=[("g_dma", tabld),
                                       ("sc_done", ee.now("sc_done"))])
            def gkw(g, pnc):
                batch, s, h2 = _role(pnc)
                return g.dma_start(
                    out=psh["kT"][s * NKV:(s + 1) * NKV, :]
                    .rearrange("(c p) t -> p c t", p=128),
                    in_=kvst[:].rearrange("p (c t) -> p c t", t=HALF))
            ee.Gsw(gkw, waits=ee.wv("v_done"), inc=("g_dma", 16))
            ee.G(lambda g, v=ee.now("g_dma"): g.wait_ge(sems["g_dma"], v))

            for tc in range(HALF // 128):
                def tvp(t, tc=tc):
                    mm = None
                    for dck in range(NC8):
                        mm = t.matmul(
                            psP[:, 0:NKV],
                            xcT[:, dck * HALF + tc * 128:dck * HALF + (tc + 1) * 128],
                            wvb[:, dck * NKV:(dck + 1) * NKV],
                            start=(dck == 0), stop=(dck == NC8 - 1))
                    return mm
                ee.Tn(tvp, waits=[("sc_done", ee.now("sc_done")),
                                  ("v_done", ee.now("v_done"))],
                      inc=("t_done", 1))
                ee.Sc(lambda s, tc=tc: s.copy(
                    out=kvst[:, (tc % 32) * NKV:(tc % 32) * NKV + NKV]
                    .rearrange("p t -> p t") if False else
                    kvst[:, (tc % 32) * NKV:(tc % 32 + 1) * NKV],
                    in_=psP[:, 0:NKV]),
                    waits=ee.wv("t_done"))
                if tc % 32 == 31:
                    blk = tc // 32

                    def gvw(g, pnc, blk=blk):
                        batch, s, h2 = _role(pnc)
                        return g.dma_start(
                            out=psh["v"][s * HALF + blk * 4096:
                                         s * HALF + (blk + 1) * 4096, :]
                            .rearrange("(c p) t -> p c t", p=128),
                            in_=kvst[:, 0:32 * NKV]
                            .rearrange("p (c t) -> p c t", t=NKV))
                    ee.Gsw(gvw, waits=ee.wv("sc_done"), inc=("g_dma", 16))
                    ee.G(lambda g, v=ee.now("g_dma"): g.wait_ge(sems["g_dma"], v))
            _send_flag(ee, psh, FL_KV)

        ee.barrier()
        qTb = osb("qTb", [128, NC8 * QB], BF16)
        attoT = osb("attoT", [128, NC8 * QB], BF16)

        with _ctx.ExitStack() as bst:
            def bsb(name, shape, dt=F32):
                h = bst.enter_context(nc.sbuf_tensor(name, shape, dt))
                ee.t[name] = h
                return h
            def bps(name, shape, dt=F32):
                h = bst.enter_context(nc.psum_tensor(name, shape, dt))
                ee.t[name] = h
                return h
            xcq = bsb("xcq", [128, NC8 * QB], BF16)
            x2g = bsb("x2gB", [128, (TT // 128) * D])
            xcu = bsb("xcuB", [128, D], BF16)
            wqb1 = bsb("wqb1", [128, NC8 * 512], BF16)
            wqb2 = bsb("wqb2", [128, NC8 * 512], BF16)
            wldp = bsb("wldpB", [128, 512])
            ctab = bsb("ctabB", [128, 1024])
            stab = bsb("stabB", [128, 1024])
            ktmp = bsb("ktmpB", [128, 1024])
            kshu = bsb("kshuB", [128, 1024])
            krop = bsb("kropB", [128, 1024])
            st6p = bsb("st6pB", [128, 12])
            st2p = bsb("st2pB", [128, 2])
            stcp = bsb("stcpB", [128, 12])
            psP = bps("psPB", [128, 1024])
            psTp = bps("psTpB", [128, 128], BF16)

            for c in range(NC8):
                for half_i, wdst in ((0, wqb1), (1, wqb2)):
                    ee.S(lambda e, c=c, half_i=half_i: e.dma_start(
                        out=wldp[:],
                        in_=inp["L0_attn_b_Wq"][c * 128:(c + 1) * 128,
                                                half_i * 512:(half_i + 1) * 512]),
                        waits=ee.wv("v_done"), inc=("s_dma", 16))
                    ee.V(lambda v, c=c, wdst=wdst: v.tensor_copy(
                        out=wdst[:, c * 512:(c + 1) * 512], in_=wldp[:]),
                        waits=ee.wv("s_dma"))

            for gi in range(QB // TT):
                def gx2qb(g, pnc, gi=gi):
                    batch, s, h2 = _role(pnc)
                    off = h2 * QB + gi * TT
                    return g.dma_start(
                        out=x2g[:].rearrange("p (c d) -> p c d", d=D),
                        in_=scr["x2"][off:off + TT, :]
                        .rearrange("(c p) d -> p c d", p=128))
                ee.Gsw(gx2qb, waits=ee.wv("v_done"), inc=("g_dma", 16))
                sld = ee.now("g_dma")
                for rc in range(TT // 128):
                    srcc = x2g[:, rc * D:(rc + 1) * D]

                    def vstq(v, srcc=srcc, rc=rc):
                        v.bn_stats(out=st6p[:, 0:6], in_=srcc[:, 0:512])
                        v.bn_stats(out=st6p[:, 6:12], in_=srcc[:, 512:1024])
                        return v.bn_aggr(out=stcp[:, 2 * rc:2 * rc + 2],
                                         in_=st6p[:])
                    ee.V(vstq, waits=[("g_dma", sld),
                                      ("sc_done", ee.now("sc_done"))])

                def scq_all(s):
                    s.activation(out=stcp[:, 8:12], in_=stcp[:, 1:8:2],
                                 func=AF.Ln, bias=ee.t["cbias"][:, 1:2])
                    return s.activation(out=stcp[:, 8:12], in_=stcp[:, 8:12],
                                        func=AF.Exp, scale=-0.5)
                ee.Sc(scq_all, waits=ee.wv("v_done"))
                qstats_done = ee.now("sc_done")
                for rc in range(TT // 128):
                    srcc = x2g[:, rc * D:(rc + 1) * D]
                    ee.V(lambda v, srcc=srcc, rc=rc: v.tensor_scalar(
                        out=xcu[:], in0=srcc, scalar1=stcp[:, 2 * rc:2 * rc + 1],
                        scalar2=stcp[:, 8 + rc:9 + rc],
                        op0=ALU.subtract, op1=ALU.mult),
                        waits=[("sc_done", qstats_done),
                               ("t_done", ee.now("t_done"))])
                    vdn = ee.now("v_done")
                    for dck in range(NC8):
                        ee.Tn(lambda t, dck=dck: t.transpose(
                            psTp[:, 0:128], xcu[:, dck * 128:(dck + 1) * 128],
                            ee.t["eye_bf"][:]),
                            waits=[("v_done", max(vdn, ee.now("v_done")))],
                            inc=("t_done", 1))
                        dstc = xcq[:, dck * QB + gi * TT + rc * 128:
                                   dck * QB + gi * TT + (rc + 1) * 128]
                        ee.V(lambda v, dstc=dstc: v.tensor_copy(
                            out=dstc, in_=psTp[:, 0:128]),
                            waits=ee.wv("t_done"))
            xcq_done = ee.now("v_done")

            for tb in range(QB // 1024):
                def gqtab(g, pnc, tb=tb):
                    batch, s, h2 = _role(pnc)
                    qpos = s * HALF + h2 * QB
                    g.dma_start(out=ctab[:],
                                in_=inp["rope_cos"][:, qpos + tb * 1024:
                                                    qpos + (tb + 1) * 1024]
                                ).then_inc(sems["g_dma"], 16)
                    return g.dma_start(out=stab[:],
                                       in_=inp["rope_sin"][:, qpos + tb * 1024:
                                                           qpos + (tb + 1) * 1024])
                ee.Gsw(gqtab, waits=ee.wv("v_done"), inc=("g_dma", 16))
                ee.cnt["g_dma"] += 16
                tabld = ee.now("g_dma")
                for qc4 in range(NC8):
                    wsrc = wqb1 if qc4 < 4 else wqb2
                    qoff = (qc4 % 4) * 128

                    def tqp(t, tb=tb, qc4=qc4, wsrc=wsrc, qoff=qoff):
                        mm = None
                        for dck in range(NC8):
                            for nn in range(2):
                                mm = t.matmul(
                                    psP[:, nn * 512:(nn + 1) * 512],
                                    wsrc[:, dck * 512 + qoff:dck * 512 + qoff + 128],
                                    xcq[:, dck * QB + tb * 1024 + nn * 512:
                                        dck * QB + tb * 1024 + (nn + 1) * 512],
                                    start=(dck == 0), stop=(dck == NC8 - 1))
                        return mm
                    ee.Tn(tqp, waits=[("v_done", max(xcq_done, ee.now("v_done"))),
                                      ("sc_done", ee.now("sc_done")),
                                      ("g_dma", ee.now("g_dma"))],
                          inc=("t_done", 1))
                    ee.Sc(lambda s: s.activation(out=ktmp[:], in_=psP[:],
                                                 func=AF.Identity, scale=0.125),
                          waits=ee.wv("t_done"))

                    def vqrope(v, tb=tb, qc4=qc4):
                        mask = []
                        for i in range(0, 32, 2):
                            mask += [i + 1, i]
                        v.stream_shuffle(out=kshu[:], in_=ktmp[:], mask=mask)
                        v.tensor_tensor(out=krop[:], in0=ktmp[:], in1=ctab[:],
                                        op=ALU.mult)
                        v.scalar_tensor_tensor(out=kshu[:], in0=kshu[:],
                                               scalar=1.0, in1=stab[:],
                                               op0=ALU.mult, op1=ALU.mult)
                        return v.tensor_tensor(
                            out=qTb[:, qc4 * QB + tb * 1024:
                                    qc4 * QB + (tb + 1) * 1024],
                            in0=krop[:], in1=kshu[:], op=ALU.add)
                    ee.V(vqrope, waits=[("g_dma", tabld),
                                        ("sc_done", ee.now("sc_done"))])
            ee.G(lambda g, v=ee.now("g_dma"): g.wait_ge(sems["g_dma"], v))

        ee.barrier()

        with _ctx.ExitStack() as cst:
            def csb(name, shape, dt=F32):
                h = cst.enter_context(nc.sbuf_tensor(name, shape, dt))
                ee.t[name] = h
                return h
            def cps(name, shape, dt=F32):
                h = cst.enter_context(nc.psum_tensor(name, shape, dt))
                ee.t[name] = h
                return h
            khalf = csb("khalf", [128, 2 * T], BF16)
            vhalf = csb("vhalf", [128, 64 * 260], BF16)
            Ebuf = csb("Ebuf", [128, 2 * 1024], BF16)
            q64 = csb("q64", [128, 2 * 512], BF16)
            denb = csb("denb", [128, 16])
            avs = csb("avs", [64, 512])
            denr = csb("denr", [1, 512])
            denrb = csb("denrb", [1, 512], BF16)
            ones64 = csb("ones64", [1, 64], BF16)
            psS = cps("psS", [128, 2 * 1024])
            psAV = cps("psAV", [128, 2 * 512])
            psBC = cps("psBC", [64, 2 * 512])

            vev_done = {0: 0, 1: 0}
            vsc_done = {0: 0, 1: 0}
            q_staged = {0: 0, 1: 0}
            q_last_sc = {0: 0, 1: 0}
            pending = [None]
            prev_exp = [None]

            ee.V(lambda v: v.memset(ones64[:], 1.0))
            def gpkv(g, pnc):
                s = pnc % 2
                _emit_poll(ee, g, psh, FL_KV, want_side=1 - s)
                return None
            ee.Gsw(gpkv, done_sem="g_done")

            for hs in range(2):
                def gkh(g, hs=hs):
                    for lc in range(2):
                        for sd in range(2):
                            g.dma_start(
                                out=khalf[:, lc * T + sd * HALF:
                                          lc * T + (sd + 1) * HALF],
                                in_=psh["kT"][sd * NKV + (2 * hs + lc) * 128:
                                              sd * NKV + (2 * hs + lc + 1) * 128, :]
                                ).then_inc(sems["g_dma"], 16)
                    return None
                ee.G(lambda g, hs=hs: gkh(None if False else g, hs),
                     waits=ee.wv("g_done", "t_done", "sc_done"))
                ee.cnt["g_dma"] += 64
                def gvh(g, hs=hs):
                    for sd in range(2):
                        for hh in range(4):
                            g.dma_start(
                                out=vhalf[:, sd * 32 * 260:(sd + 1) * 32 * 260]
                                .rearrange("p (g r) -> p g r", r=260)
                                [:, :, hh * 65:hh * 65 + 64],
                                in_=psh["v"][sd * HALF:(sd + 1) * HALF,
                                             hs * 256 + hh * 64:
                                             hs * 256 + (hh + 1) * 64]
                                .rearrange("(g p) d -> p g d", p=128)
                                ).then_inc(sems["g_dma"], 16)
                    return None
                ee.G(lambda g, hs=hs: gvh(g, hs))
                ee.cnt["g_dma"] += 128
                ee.V(lambda v: v.memset(
                    vhalf[:].rearrange("p (g h d) -> p g h d", h=4, d=65)
                    [:, :, :, 64:65], 1.0),
                    waits=ee.wv("g_dma"))
                kvld = ee.now("g_dma")

                groups = [(qg, h8) for qg in range(4) for h8 in range(8)]

                def gparm(h8, hs=hs):
                    qh = hs * 8 + h8
                    kvh = qh // 2
                    lci = (kvh - hs * 4) // 2
                    rbase = ((kvh - hs * 4) % 2) * 64
                    qrow = (qh % 2) * 64
                    qchunk = qh // 2
                    return qh, kvh, lci, rbase, qrow, qchunk

                def stage_q(gi, hs=hs):
                    qg, h8 = groups[gi]
                    qh, kvh, lci, rbase, qrow, qchunk = gparm(h8)
                    gpn = (hs * 32 + gi) % 2

                    def vq64(v, qrow=qrow, rbase=rbase, qchunk=qchunk,
                             qg=qg, gpn=gpn):
                        return v.tensor_copy(
                            out=q64[rbase:rbase + 64, gpn * 512:(gpn + 1) * 512],
                            in_=qTb[qrow:qrow + 64,
                                    qchunk * QB + qg * 512:
                                    qchunk * QB + (qg + 1) * 512])
                    ee.V(vq64, waits=[("t_done", q_last_sc[gpn])])
                    q_staged[gpn] = ee.now("v_done")

                stage_q(0)
                for gi, (qg, h8) in enumerate(groups):
                    qh, kvh, lci, rbase, qrow, qchunk = gparm(h8)
                    gp = (hs * 32 + gi) % 2
                    exp_of = {}
                    NU = 32
                    exp_base = ee.now("sc_done")
                    sc_items = []

                    def emit_sc2(u, gp=gp, lci=lci, rbase=rbase,
                                 prev_exp=prev_exp[0]):
                        def tsc2(t, u=u, lci=lci, rbase=rbase, gp=gp):
                            mm = None
                            for j in range(2):
                                kt = u * 2 + j
                                mm = t.matmul(
                                    psS[:, (u % 2) * 1024 + j * 512:
                                        (u % 2) * 1024 + (j + 1) * 512],
                                    khalf[:, lci * T + kt * 128:
                                          lci * T + (kt + 1) * 128]
                                    [rbase:rbase + 64, :],
                                    q64[rbase:rbase + 64,
                                        gp * 512:(gp + 1) * 512],
                                    start=True, stop=True)
                            return mm
                        w = [("g_dma", kvld), ("v_done", q_staged[gp])]
                        if u >= 2:
                            w.append(("sc_done", exp_of[u - 2]))
                        elif prev_exp is not None:
                            w.append(("sc_done", prev_exp[NU - 2 + u]))
                        ee.Tn(tsc2, waits=w, inc=("t_done", 1))

                    def emit_exp2(u):
                        def sexp2(s, u=u):
                            return s.activation(
                                out=Ebuf[:, (u % 2) * 1024:(u % 2 + 1) * 1024],
                                in_=psS[:, (u % 2) * 1024:(u % 2 + 1) * 1024],
                                func=AF.Exp, bias=ee.t["cbias"][:, 2:3])
                        sc_items.append(
                            (sexp2, [("t_done", ee.now("t_done"))], None))
                        exp_of[u] = exp_base + u + 1

                    def emit_av2(u, gp=gp, kvh=kvh):
                        def tav2(t, u=u, kvh=kvh, hs=hs, gp=gp):
                            mm = None
                            hh = kvh - hs * 4
                            for j in range(2):
                                kt = u * 2 + j
                                mm = t.matmul(
                                    psAV[0:65, gp * 512:(gp + 1) * 512],
                                    vhalf[:, kt * 260 + hh * 65:
                                          kt * 260 + (hh + 1) * 65],
                                    Ebuf[:, (u % 2) * 1024 + j * 512:
                                         (u % 2) * 1024 + (j + 1) * 512],
                                    start=(kt == 0), stop=(kt == 2 * NU - 1))
                            return mm
                        w = [("sc_done", exp_of[u])]
                        if u == 0:
                            w.append(("v_done", vev_done[gp]))
                        ee.Tn(tav2, waits=w, inc=("t_done", 1))

                    emit_sc2(0)
                    emit_exp2(0)
                    emit_sc2(1)
                    emit_av2(0)
                    emit_exp2(1)
                    if pending[0] is not None:
                        pending[0]()
                        pending[0] = None
                    if gi + 1 < len(groups):
                        stage_q(gi + 1)
                    for u in range(2, NU):
                        emit_sc2(u)
                        emit_av2(u - 1)
                        emit_exp2(u)
                    q_last_sc[gp] = ee.now("t_done")
                    emit_av2(NU - 1)
                    av31_cnt = ee.now("t_done")
                    ee.Sc_batch(sc_items)
                    prev_exp[0] = exp_of

                    def vev(v, gp=gp):
                        v.tensor_copy(out=denr[:],
                                      in_=psAV[64:65, gp * 512:gp * 512 + 512])
                        v.reciprocal(out=denr[:], in_=denr[:])
                        v.tensor_copy(out=denrb[:], in_=denr[:])
                        return v.tensor_copy(out=avs[:],
                                             in_=psAV[0:64,
                                                      gp * 512:gp * 512 + 512])
                    ee.V(vev, waits=[("t_done", av31_cnt)])
                    vev_done[gp] = ee.now("v_done")

                    def mk_tail(gp=gp, qh=qh, qg=qg, vevd=vev_done[gp],
                                vscd=vsc_done[gp]):
                        def tail():
                            ee.Tn(lambda t: t.matmul(
                                psBC[:, gp * 512:(gp + 1) * 512], ones64[:],
                                denrb[:], start=True, stop=True),
                                waits=[("v_done", max(vevd, vscd))],
                                inc=("t_done", 1))
                            bc_cnt = ee.now("t_done")

                            def vsc(v, qh=qh, qg=qg, gp=gp):
                                qchunk = qh // 2
                                qrow = (qh % 2) * 64
                                return v.scalar_tensor_tensor(
                                    out=attoT[qrow:qrow + 64,
                                              qchunk * QB + qg * 512:
                                              qchunk * QB + (qg + 1) * 512],
                                    in0=avs[:], scalar=1.0,
                                    in1=psBC[:, gp * 512:(gp + 1) * 512],
                                    op0=ALU.mult, op1=ALU.mult)
                            ee.V(vsc, waits=[("t_done", bc_cnt)])
                            vsc_done[gp] = ee.now("v_done")
                        return tail
                    pending[0] = mk_tail()
                if pending[0] is not None:
                    pending[0]()
                    pending[0] = None
        ee.barrier()

        with _ctx.ExitStack() as dst_:
            def dsb(name, shape, dt=F32):
                h = dst_.enter_context(nc.sbuf_tensor(name, shape, dt))
                ee.t[name] = h
                return h
            def dps(name, shape, dt=F32):
                h = dst_.enter_context(nc.psum_tensor(name, shape, dt))
                ee.t[name] = h
                return h
            wob = dsb("wob", [128, NC8 * D], BF16)
            wldo = dsb("wldo", [128, D])
            bobf = dsb("bobf", [1, D], BF16)
            bob = dsb("bob", [128, D])
            ones128 = dsb("ones128", [1, 128], BF16)
            x2q = dsb("x2q", [128, (TT // 128) * D])
            x3st = dsb("x3st", [128, (TT // 128) * D])
            psW = [dps("psW%d" % i, [128, D]) for i in range(4)]

            for c in range(NC8):
                ee.S(lambda e, c=c: e.dma_start(
                    out=wldo[:], in_=inp["L0_attn_b_Wo"][c * 128:(c + 1) * 128, :]),
                    waits=ee.wv("v_done"), inc=("s_dma", 16))
                ee.V(lambda v, c=c: v.tensor_copy(out=wob[:, c * D:(c + 1) * D],
                                                  in_=wldo[:]),
                     waits=ee.wv("s_dma"))
            ee.S(lambda e: e.dma_start(out=wldo[0:1, :], in_=inp["L0_attn_b_bo"][:, :]),
                 inc=("s_dma", 16))

            def vbo(v):
                v.memset(ones128[:], 1.0)
                return v.tensor_copy(out=bobf[:], in_=wldo[0:1, :])
            ee.V(vbo, waits=ee.wv("s_dma"))
            for nh in range(2):
                ee.Tn(lambda t, nh=nh: t.matmul(
                    psW[0][:, nh * 512:(nh + 1) * 512], ones128[:],
                    bobf[:, nh * 512:(nh + 1) * 512], start=True, stop=True),
                    waits=ee.wv("v_done"), inc=("t_done", 1))
            ee.V(lambda v: v.tensor_copy(out=bob[:], in_=psW[0][:]),
                 waits=ee.wv("t_done"))

            for qg in range(4):
                def gx2q(g, pnc, qg=qg):
                    batch, s, h2 = _role(pnc)
                    off = h2 * QB + qg * TT
                    return g.dma_start(
                        out=x2q[:].rearrange("p (c d) -> p c d", d=D),
                        in_=scr["x2"][off:off + TT, :]
                        .rearrange("(c p) d -> p c d", p=128))
                ee.Gsw(gx2q, waits=ee.wv("v_done"), inc=("g_dma", 16))
                x2ld = ee.now("g_dma")
                for ts in range(TT // 128):
                    def tow(t, qg=qg, ts=ts):
                        mm = None
                        for dck in range(NC8):
                            lhs = attoT[:, dck * QB + qg * TT + ts * 128:
                                        dck * QB + qg * TT + (ts + 1) * 128]
                            for nh in range(2):
                                mm = t.matmul(
                                    psW[ts][:, nh * 512:(nh + 1) * 512], lhs,
                                    wob[:, dck * D + nh * 512:dck * D + (nh + 1) * 512],
                                    start=(dck == 0), stop=(dck == NC8 - 1))
                        return mm
                    ee.Tn(tow, waits=[("v_done", ee.now("v_done")),
                                      ("g_dma", x2ld)], inc=("t_done", 1))

                    def vx3(v, ts=ts):
                        sl = slice(ts * D, (ts + 1) * D)
                        v.tensor_tensor(out=x3st[:, sl], in0=psW[ts][:],
                                        in1=bob[:], op=ALU.add)
                        return v.tensor_tensor(out=x3st[:, sl], in0=x3st[:, sl],
                                               in1=x2q[:, sl], op=ALU.add)
                    ee.V(vx3, waits=ee.wv("t_done"))
                ee.G(lambda g, qg=qg: g.dma_start(
                    out=scr["x3"][qg * TT:(qg + 1) * TT, :]
                    .rearrange("(c p) d -> p c d", p=128),
                    in_=x3st[:].rearrange("p (c d) -> p c d", d=D)),
                    waits=ee.wv("v_done"), inc=("g_dma", 16))
            ee.G(lambda g, v=ee.now("g_dma"): g.wait_ge(sems["g_dma"], v))


def emit_p2(ee, stack, inp, scr):
    emit_ffn(ee, stack, inp, scr, "x1", scr["x2"], HALF, dbgtag="f1")


def emit_p4(ee, stack, inp, scr, out):
    emit_ffn(ee, stack, inp, scr, "x3", out, QB, dbgtag="f2")




def _colpack(inputs):
    pk = np.zeros((128, 16 * 8), np.float32)
    src = {"mk": "L0_attn_a_tm_mix_k", "mv": "L0_attn_a_tm_mix_v",
           "mr": "L0_attn_a_tm_mix_r", "cmk": "L0_attn_a_cm_mix_k",
           "cmr": "L0_attn_a_cm_mix_r", "td": "L0_attn_a_time_decay",
           "tf": "L0_attn_a_time_first", "g1": "L0_attn_a_ln1_g",
           "b1": "L0_attn_a_ln1_b", "g2": "L0_attn_a_ln2_g",
           "b2": "L0_attn_a_ln2_b"}
    for k, nm in src.items():
        v = np.asarray(inputs[nm], np.float32).reshape(-1)
        pk[:, CSLOT[k] * 8:CSLOT[k] * 8 + 8] = v.reshape(8, 128).T
    return pk


def _derived_inputs(inputs):
    pos = np.arange(T, dtype=np.float64)
    rows = np.arange(128)
    j = rows % 64
    inv_freq = 1.0 / (10000.0 ** ((2 * (j // 2)) / ROT))
    freqs = pos[None, :] * inv_freq[:, None]
    cos = np.cos(freqs)
    sin = np.sin(freqs) * np.where(j % 2 == 0, -1.0, 1.0)[:, None]
    rot = (j < ROT)[:, None]
    nonce = int(np.random.randint(1 << 20, 1 << 30))
    return {
        "rope_cos": np.where(rot, cos, 1.0).astype(np.float32),
        "rope_sin": np.where(rot, sin, 0.0).astype(np.float32),
        "colpack": _colpack(inputs),
        "eye_bf16": np.eye(128, dtype=ml_dtypes.bfloat16),
        "nonce_row": (nonce + np.arange(16)).astype(np.int32).reshape(1, 16),
    }


def _make_in_map(inputs):
    in_map = {}
    x = np.asarray(inputs["L0_attn_a_x"], dtype=np.float32)
    in_map["L0_attn_a_x"] = np.ascontiguousarray(x.reshape(B * T, D))
    for k, v in inputs.items():
        if k == "L0_attn_a_x":
            continue
        a = np.asarray(v)
        if a.ndim == 1:
            a = a.reshape(1, -1)
        in_map[k] = np.ascontiguousarray(a)
    in_map.update(_derived_inputs(inputs))
    return in_map


def kernel(**inputs):
    in_map = _make_in_map(inputs)
    nc = build_program()
    res = run_bass_kernel_spmd(nc, [in_map] * 8, core_ids=list(range(8)))
    full = np.zeros((B, T, D), dtype=np.float32)
    for rr in res.results:
        pnc = int(round(float(rr["whoami"][0, 0])))
        batch, s, h2 = _role(pnc)
        qstart = s * HALF + h2 * QB
        full[batch, qstart:qstart + QB, :] = rr["out"]
    return full


# version 12
# speedup vs baseline: 1.2744x; 1.0326x over previous
import contextlib
import numpy as np
import ml_dtypes
import concourse.bass as bass
import concourse.mybir as mybir
from concourse.bass_utils import run_bass_kernel_spmd

F32 = mybir.dt.float32
BF16 = mybir.dt.bfloat16
I32 = mybir.dt.int32
AF = mybir.ActivationFunctionType
ALU = mybir.AluOpType
DS = bass.ds

B, T, D = 2, 8192, 1024
HALF = T // 2
QB = T // 4
FCM = 4 * D
FFN = 3072
EPS = 1e-5
NC8 = D // 128
TT = 512
ROT = 32
NKV = 512

CSLOT = {"mk": 0, "mv": 1, "mr": 2, "cmk": 3, "cmr": 4, "td": 5, "tf": 6,
         "g1": 7, "b1": 8, "g2": 9, "b2": 10}
DSLOT = {"lam": 0, "eu": 1, "bk": 2, "bv": 3, "br": 4, "bcr": 5,
         "ck": 6, "cv": 7, "cr": 8}

FL_BND = 1
FL_KV = 2
FL_INA = 3
FL_INB = 4


def _role(pnc):
    return pnc // 4, pnc % 2, (pnc // 2) % 2


CHAIN_OPS = {
    "activation", "copy", "add", "mul", "square", "sqrt", "sign",
    "tensor_tensor", "tensor_scalar", "scalar_tensor_tensor",
    "tensor_tensor_scan", "tensor_copy", "tensor_add", "tensor_sub",
    "tensor_mul", "tensor_max", "tensor_relu", "tensor_scalar_add",
    "tensor_scalar_mul", "tensor_scalar_sub", "tensor_scalar_max",
    "tensor_reduce", "reduce_sum", "reduce_max", "bn_stats", "bn_aggr",
    "memset", "memzero", "reciprocal", "select", "stream_shuffle",
    "copy_predicated", "iota", "transpose", "max",
}


class _Chained:

    def __init__(self, eng, ee, key):
        self._eng = eng
        self._ee = ee
        self._key = key
        self._sem = ee.sems[key]

    def __getattr__(self, name):
        fn = getattr(self._eng, name)
        if name not in CHAIN_OPS:
            return fn
        ee, eng, sem, key = self._ee, self._eng, self._sem, self._key

        def wrapped(*a, **k):
            if ee.cnt[key] > 0:
                eng.wait_ge(sem, ee.cnt[key])
            ins = fn(*a, **k)
            ins.then_inc(sem, 1)
            ee.cnt[key] += 1
            return ins
        return wrapped


class E:

    def __init__(self, nc, block, stack):
        self.nc = nc
        self.block = block
        self.stack = stack
        self.sems = {}
        self.cnt = {}
        self.t = {}
        self.regs = {}
        self.bar_target = 0

    def sem(self, *names):
        for n in names:
            self.sems[n] = self.stack.enter_context(self.nc.semaphore(n))
            self.cnt[n] = 0

    CHAIN_KEY = {"vector": "v_done", "scalar": "sc_done"}

    def _wrap(self, emit, waits, inc, engname):
        wl = [(self.sems[s], v) for (s, v) in waits]
        it = None
        if inc:
            it = (self.sems[inc[0]], inc[1])
        ck = self.CHAIN_KEY.get(engname)

        def body(e):
            for sh, v in wl:
                if v > 0:
                    e.wait_ge(sh, v)
            ins = emit(_Chained(e, self, ck) if ck else e)
            if it is not None:
                assert ins is not None, f"{engname} op with inc returned None"
                ins.then_inc(it[0], it[1])
        getattr(self.block, engname)(body)
        if inc:
            self.cnt[inc[0]] += inc[1]

    def S(self, emit, waits=(), inc=None):
        self._wrap(emit, waits, inc, "sync")

    def V(self, emit, waits=(), inc=None):
        self._wrap(emit, waits, inc, "vector")

    def Tn(self, emit, waits=(), inc=None):
        self._wrap(emit, waits, inc, "tensor")

    def Sc(self, emit, waits=(), inc=None):
        self._wrap(emit, waits, inc, "scalar")

    def G(self, emit, waits=(), inc=None):
        self._wrap(emit, waits, inc, "gpsimd")

    def Gsw(self, emit, waits=(), inc=None, done_sem=None):
        wl = [(self.sems[s], v) for (s, v) in waits]
        it = (self.sems[inc[0]], inc[1]) if inc else None
        nc = self.nc

        def body(g):
            for sh, v in wl:
                if v > 0:
                    g.wait_ge(sh, v)
            for arm in nc.switch_core_id(engines=[g], n=128):
                ins = emit(g, int(arm) % 8)
                if it is not None and ins is not None:
                    ins.then_inc(it[0], it[1])
                if done_sem is not None:
                    g.sem_inc(self.sems[done_sem], 1)
        self.block.gpsimd(body)
        if inc:
            self.cnt[inc[0]] += inc[1]
        if done_sem is not None:
            self.cnt[done_sem] += 1

    def Sc_batch(self, items):
        resolved = []
        for emit, waits, inc in items:
            wl = [(self.sems[s], v) for (s, v) in waits]
            it = (self.sems[inc[0]], inc[1]) if inc else None
            resolved.append((emit, wl, it))

        def body(e):
            ch = _Chained(e, self, "sc_done")
            for emit, wl, it in resolved:
                for sh, v in wl:
                    if v > 0:
                        e.wait_ge(sh, v)
                ins = emit(ch)
                if it is not None:
                    ins.then_inc(it[0], it[1])
                    self.cnt[it[0]] += it[1]
        self.block.scalar(body)

    def now(self, sem):
        return self.cnt[sem]

    def wv(self, *sems):
        return [(s, self.cnt[s]) for s in sems]

    def barrier(self):
        bar = self.sems["bar"]
        self.bar_target += 5
        tgt = self.bar_target

        def f(e):
            e.sem_inc(bar, 1)
            e.wait_ge(bar, tgt)
        for nm in ("sync", "scalar", "vector", "tensor", "gpsimd"):
            getattr(self.block, nm)(f)

    def sb(self, name, shape, dt=F32):
        h = self.stack.enter_context(self.nc.sbuf_tensor(name, shape, dt))
        self.t[name] = h
        return h

    def ps(self, name, shape, dt=F32):
        h = self.stack.enter_context(self.nc.psum_tensor(name, shape, dt))
        self.t[name] = h
        return h


def cslot(ee, name, c):
    i = CSLOT[name] * 8 + c
    return ee.t["colpk"][:, i:i + 1]


def dslot(ee, name, c):
    i = DSLOT[name] * 8 + c
    return ee.t["dcol"][:, i:i + 1]



def build_program(dbg=(), race_checks=False, poll_backoff=True):
    nc = bass.Bass("TRN2", target_bir_lowering=False,
                   detect_race_conditions=race_checks)
    nc.disable_value_cache = True
    inp = {}

    def di(name, shape, dt=F32):
        inp[name] = nc.dram_tensor(name, shape, dt, kind="ExternalInput").ap()
        return inp[name]

    x_in = di("L0_attn_a_x", [B * T, D])
    for nm in ["ln1_g", "ln1_b", "ln2_g", "ln2_b", "tm_mix_k", "tm_mix_v",
               "tm_mix_r", "time_decay", "time_first", "cm_mix_k", "cm_mix_r"]:
        di("L0_attn_a_" + nm, [1, D])
    di("L0_attn_a_wk", [D, D]); di("L0_attn_a_wv", [D, D])
    di("L0_attn_a_wr", [D, D]); di("L0_attn_a_wo", [D, D])
    di("L0_attn_a_cwk", [D, FCM]); di("L0_attn_a_cwv", [FCM, D])
    di("L0_attn_a_cwr", [D, D])
    di("L0_attn_b_Wq", [D, D]); di("L0_attn_b_Wk", [D, NKV])
    di("L0_attn_b_Wv", [D, NKV]); di("L0_attn_b_Wo", [D, D])
    di("L0_attn_b_bo", [1, D])
    di("L0_ffn_w_gate", [FFN, D], I32); di("L0_ffn_w_up", [FFN, D], I32)
    di("L0_ffn_w_down", [D, FFN], I32)
    di("colpack", [128, 16 * 8])
    di("rope_cos", [128, T]); di("rope_sin", [128, T])
    di("eye_bf16", [128, 128], BF16)
    di("nonce_row", [1, 16], I32)

    out = nc.dram_tensor("out", [QB, D], F32, kind="ExternalOutput").ap()
    whoami = nc.dram_tensor("whoami", [1, 1], F32, kind="ExternalOutput").ap()
    dbg_t = {}
    for name, shape in [("dbg_inner", [HALF, D]), ("dbg_x1", [HALF, D]),
                        ("dbg_x2", [HALF, D]), ("dbg_x3", [QB, D])]:
        if name in dbg:
            dbg_t[name] = nc.dram_tensor(name, shape, F32,
                                         kind="ExternalOutput").ap()

    scr = {
        "inner": nc.dram_tensor("inner_scr", [HALF, D], F32).ap(),
        "x1": nc.dram_tensor("x1_scr", [HALF, D], F32).ap(),
        "x2": nc.dram_tensor("x2_scr", [HALF, D], F32).ap(),
        "x3": nc.dram_tensor("x3_scr", [QB, D], F32).ap(),
        "kk2": nc.dram_tensor("kk2_scr", [FCM // 128, 128, HALF], BF16).ap(),
        "mcr": nc.dram_tensor("mcr_scr", [NC8, 128, HALF], BF16).ap(),
        "hh": nc.dram_tensor("hh_scr", [FFN // 128, 128, HALF], BF16).ap(),
        "wgT": nc.dram_tensor("wgT_scr", [NC8, 128, FFN], BF16).ap(),
        "wuT": nc.dram_tensor("wuT_scr", [NC8, 128, FFN], BF16).ap(),
        "wdT": nc.dram_tensor("wdT_scr", [FFN // 128, 128, D], BF16).ap(),
    }
    psh = {
        "inner": nc.dram_tensor("psh_inner", [T, D], F32, addr_space="Shared").ap(),
        "bnd": nc.dram_tensor("psh_bnd", [2, D], F32, addr_space="Shared").ap(),
        "kT": nc.dram_tensor("psh_kT", [2 * NKV, HALF], BF16, addr_space="Shared").ap(),
        "v": nc.dram_tensor("psh_v", [2 * HALF, NKV], BF16, addr_space="Shared").ap(),
        "flag": nc.dram_tensor("psh_flag", [32, 1], I32, addr_space="Shared").ap(),
    }

    for name, shape, dt in [("dbg_x0t", [128, (TT // 128) * D], F32),
                            ("dbg_h1T", [128, NC8 * (TT + 128)], BF16),
                            ("dbg_mixK", [128, NC8 * TT], BF16),
                            ("dbg_kvr", [128, 3 * TT], F32),
                            ("dbg_rwkv", [128, NC8 * TT], BF16)]:
        if name in dbg:
            dbg_t[name] = nc.dram_tensor(name, shape, dt,
                                         kind="ExternalOutput").ap()
    with contextlib.ExitStack() as stack:
        block = stack.enter_context(nc.Block())
        ee = E(nc, block, stack)
        ee.poll_backoff = poll_backoff
        ee.sem("bar", "g_dma", "s_dma", "c_dma", "v_done", "t_done",
               "sc_done", "g_done", "poll", "scan")

        ee.sb("colpk", [128, 16 * 8])
        ee.sb("dcol", [128, 16 * 8])
        ee.sb("onesT", [128, TT])
        ee.sb("id_t", [128, 1])
        ee.sb("s_col", [128, 1])
        ee.sb("ims_col", [128, 1])
        ee.sb("nonce_t", [1, 16], I32)
        ee.sb("flrd_t", [1, 16], I32)
        ee.sb("carryA", [128, NC8])
        ee.sb("carryB", [128, NC8])
        ee.sb("eye_bf", [128, 128], BF16)
        ee.sb("b1bf", [128, 8], BF16)
        ee.sb("b1m", [128, 24], BF16)
        ee.sb("b1mf", [128, 8])
        ee.sb("cbias", [128, 4])
        ee.sb("bnd_row", [1, D])
        ee.sb("bnd_loc", [1, D])
        ee.sb("tdtf", [128, 8])

        emit_init(ee, inp)
        with nc.named_scope("p0w"):
            with contextlib.ExitStack() as pstack:
                emit_p0w(ee, pstack, inp, scr)
            ee.barrier()
        with nc.named_scope("p1a"):
            with contextlib.ExitStack() as pstack:
                emit_p1a(ee, pstack, inp, x_in, scr, psh, dbg_t)
            ee.barrier()
        if "dbg_inner" in dbg_t:
            ee.G(lambda g: g.dma_start(out=dbg_t["dbg_inner"][:, :],
                                       in_=scr["inner"][:, :]),
                 inc=("g_dma", 16))
            ee.G(lambda g, v=ee.now("g_dma"): g.wait_ge(ee.sems["g_dma"], v))
            ee.barrier()
        with nc.named_scope("p1b1"):
            with contextlib.ExitStack() as pstack:
                emit_p1b1(ee, pstack, inp, x_in, scr, psh)
            ee.barrier()
        with nc.named_scope("p1b2"):
            with contextlib.ExitStack() as pstack:
                emit_p1b2(ee, pstack, inp, x_in, scr, psh)
            ee.barrier()
        if "dbg_x1" in dbg_t:
            ee.G(lambda g: g.dma_start(out=dbg_t["dbg_x1"][:, :],
                                       in_=scr["x1"][:, :]),
                 inc=("g_dma", 16))
            ee.G(lambda g, v=ee.now("g_dma"): g.wait_ge(ee.sems["g_dma"], v))
            ee.barrier()
        with nc.named_scope("p2"):
            with contextlib.ExitStack() as pstack:
                emit_p2(ee, pstack, inp, scr)
            ee.barrier()
        if "dbg_x2" in dbg_t:
            ee.G(lambda g: g.dma_start(out=dbg_t["dbg_x2"][:, :],
                                       in_=scr["x2"][:, :]),
                 inc=("g_dma", 16))
            ee.G(lambda g, v=ee.now("g_dma"): g.wait_ge(ee.sems["g_dma"], v))
            ee.barrier()
        with nc.named_scope("p3"):
            emit_p3(ee, inp, scr, psh)
            ee.barrier()
        if "dbg_x3" in dbg_t:
            ee.G(lambda g: g.dma_start(out=dbg_t["dbg_x3"][:, :],
                                       in_=scr["x3"][:, :]),
                 inc=("g_dma", 16))
            ee.G(lambda g, v=ee.now("g_dma"): g.wait_ge(ee.sems["g_dma"], v))
            ee.barrier()
        with nc.named_scope("p4"):
            with contextlib.ExitStack() as pstack:
                emit_p4(ee, pstack, inp, scr, out)
            ee.barrier()
        emit_whoami(ee, whoami)
    return nc



def emit_init(ee, inp):
    nc = ee.nc

    def g_init(g):
        r = {}
        for nm in ["r_tgt", "r_poll", "r_pollcnt"]:
            r[nm] = g.alloc_register(nm)
        g.reg_mov(r["r_pollcnt"], 0)
        ee.regs.update(r)
        for arm in nc.switch_core_id(engines=[g], n=128):
            pnc = int(arm) % 8
            batch, s, h2 = _role(pnc)
            g.memset(ee.t["id_t"][:], float(pnc))
            g.memset(ee.t["s_col"][:], float(s))
            g.memset(ee.t["ims_col"][:], float(1 - s))
    ee.block.gpsimd(g_init)

    ee.G(lambda g: g.dma_start(out=ee.t["colpk"][:], in_=inp["colpack"][:, :]),
         inc=("g_dma", 16))
    ee.G(lambda g: g.dma_start(out=ee.t["nonce_t"][:], in_=inp["nonce_row"][:, :]),
         inc=("g_dma", 16))

    def gtdtf(g, pnc):
        cg = pnc % 2
        g.dma_start(
            out=ee.t["tdtf"][:, 0:4],
            in_=inp["colpack"][:, CSLOT["td"] * 8 + cg * 4:
                               CSLOT["td"] * 8 + cg * 4 + 4]).then_inc(
            ee.sems["g_dma"], 16)
        return g.dma_start(
            out=ee.t["tdtf"][:, 4:8],
            in_=inp["colpack"][:, CSLOT["tf"] * 8 + cg * 4:
                               CSLOT["tf"] * 8 + cg * 4 + 4])
    ee.Gsw(gtdtf, inc=("g_dma", 16))
    ee.cnt["g_dma"] += 16
    ee.S(lambda e: e.dma_start(out=ee.t["eye_bf"][:], in_=inp["eye_bf16"][:, :]),
         inc=("s_dma", 16))

    def v0(v):
        v.memset(ee.t["onesT"][:], 1.0)
        v.memset(ee.t["cbias"][:, 0:1], EPS * EPS)
        v.memset(ee.t["cbias"][:, 1:2], EPS)
        v.memset(ee.t["cbias"][:, 2:3], -16.0)
        v.memset(ee.t["bnd_row"][:], 0.0)
        v.memset(ee.t["carryA"][:], 0.0)
        v.memset(ee.t["carryB"][:], 0.0)
        v.memset(ee.t["bnd_loc"][:], 0.0)
        v.memset(ee.t["dcol"][:], 0.0)
        v.tensor_copy(out=ee.t["b1bf"][:],
                      in_=ee.t["colpk"][:, CSLOT["b1"] * 8:CSLOT["b1"] * 8 + 8])
        for mi, mname in enumerate(("mk", "mv", "mr")):
            v.tensor_tensor(out=ee.t["b1mf"][:],
                            in0=ee.t["colpk"][:, CSLOT["b1"] * 8:CSLOT["b1"] * 8 + 8],
                            in1=ee.t["colpk"][:, CSLOT[mname] * 8:CSLOT[mname] * 8 + 8],
                            op=ALU.mult)
            v.tensor_tensor(out=ee.t["b1mf"][:],
                            in0=ee.t["colpk"][:, CSLOT["b1"] * 8:CSLOT["b1"] * 8 + 8],
                            in1=ee.t["b1mf"][:], op=ALU.subtract)
            v.tensor_copy(out=ee.t["b1m"][:, mi * 8:(mi + 1) * 8], in_=ee.t["b1mf"][:])
        return v.memset(ee.t["lowbuf"][:], 0.0) if False else v.memset(ee.t["b1mf"][:], 0.0)
    ee.V(v0, waits=ee.wv("g_dma", "s_dma"))

    def sc0(sc):
        dc = ee.t["dcol"]
        sc.activation(out=dc[:, 0:4], in_=ee.t["tdtf"][:, 0:4], func=AF.Exp)
        sc.activation(out=dc[:, 0:4], in_=dc[:, 0:4], func=AF.Exp, scale=-1.0)
        return sc.activation(out=dc[:, 8:12], in_=ee.t["tdtf"][:, 4:8],
                             func=AF.Exp)
    ee.Sc(sc0, waits=ee.wv("v_done", "g_dma"))
    ee.barrier()


def _emit_poll(ee, g, psh, col, want_side):
    r = ee.regs
    sems = ee.sems
    g.reg_load(r["r_tgt"], ee.t["nonce_t"][0:1, col:col + 1])
    g.reg_mov(r["r_poll"], 1)
    with g.While(r["r_poll"]):
        if ee.poll_backoff:
            g.nop(cycle_cnt=6000)
        g.reg_alu(r["r_pollcnt"], r["r_pollcnt"], 16, ALU.add)
        g.dma_start(out=ee.t["flrd_t"][:, col:col + 1],
                    in_=psh["flag"][want_side * 16 + col:want_side * 16 + col + 1,
                                    :]).then_inc(sems["poll"], 16)
        g.wait_ge(sems["poll"], r["r_pollcnt"])
        g.reg_load(r["r_poll"], ee.t["flrd_t"][0:1, col:col + 1])
        g.reg_alu(r["r_poll"], r["r_poll"], r["r_tgt"], ALU.not_equal)


def _send_flag(ee, psh, col):
    def gf(g, pnc):
        s = pnc % 2
        return g.dma_start(out=psh["flag"][s * 16 + col:s * 16 + col + 1, :],
                           in_=ee.t["nonce_t"][0:1, col:col + 1])
    ee.Gsw(gf, inc=("g_dma", 16))
    ee.G(lambda g, v=ee.now("g_dma"): g.wait_ge(ee.sems["g_dma"], v))



def emit_p1a(ee, stack, inp, x_in, scr, psh, dbg_t=()):
    nc = ee.nc
    sems = ee.sems
    r = ee.regs
    NTILE = T // TT
    HTW = TT + 128
    NOC = 4
    W512 = 512
    WO_OFF = 3 * NC8 * W512

    def sb(name, shape, dt=F32):
        h = stack.enter_context(nc.sbuf_tensor(name, shape, dt))
        ee.t[name] = h
        return h

    def ps(name, shape, dt=F32):
        h = stack.enter_context(nc.psum_tensor(name, shape, dt))
        ee.t[name] = h
        return h

    w4 = sb("w4", [128, WO_OFF + NOC * D], BF16)
    wld = sb("wld", [128, D])
    rawbf = sb("rawbf", [128, W512], BF16)
    h1T = sb("h1T", [128, NC8 * HTW], BF16)
    mixK = sb("mixK", [128, NC8 * TT], BF16)
    mixV = sb("mixV", [128, NC8 * TT], BF16)
    mixR = sb("mixR", [128, NC8 * TT], BF16)
    rwkv = sb("rwkv", [128, NOC * TT], BF16)
    x0t = sb("x0t", [128, (TT // 128) * D])
    h1tm = sb("h1tm", [128, D], BF16)
    ddsc = sb("ddsc", [128, TT], BF16)
    lam_s = sb("lam_s", [128, TT])
    kc = sb("kc", [128, TT]); vcx = sb("vcx", [128, TT])
    rpre = sb("rpre", [128, NOC * TT], BF16)
    ekc = sb("ekc", [128, TT]); ekvc = sb("ekvc", [128, TT])
    t1c = sb("t1c", [128, TT])
    denb = sb("denb1", [128, NOC * TT])
    At = sb("At", [128, TT + 1])
    Bt = sb("Bt", [128, TT + 1])
    inner_t = sb("inner_t", [128, (TT // 128) * D])
    stat6 = sb("stat6", [128, 12])
    stat2 = sb("stat2", [128, 2])
    stc = sb("stc", [128, 8])
    stb = sb("stb", [128, 26])
    psA = ps("psA", [128, TT])
    psB = ps("psB", [128, TT])
    psT = ps("psT", [128, 128], BF16)
    psO = ps("psO", [128, D])
    psbias = ps("psbias", [128, 48])

    ee.V(lambda v: v.memset(h1T[:], 0.0))

    WN = [("L0_attn_a_wk", "g1"), ("L0_attn_a_wv", "g1"),
          ("L0_attn_a_wr", "g1")]
    for wi, (iname, gname) in enumerate(WN):
        for c in range(NC8):
            def gwld(g, pnc, iname=iname, c=c):
                cg = pnc % 2
                return g.dma_start(
                    out=wld[:, 0:W512],
                    in_=inp[iname][c * 128:(c + 1) * 128,
                                   cg * W512:(cg + 1) * W512])
            ee.Gsw(gwld, waits=ee.wv("v_done"), inc=("g_dma", 16))
            ee.V(lambda v: v.tensor_copy(out=rawbf[:], in_=wld[:, 0:W512]),
                 waits=ee.wv("g_dma", "t_done"))

            def tbias(t, wi=wi, c=c):
                mm = None
                for ec in range(NOC):
                    first = wi == 0 and c == 0 and ec == 0
                    last = wi == 2 and c == NC8 - 1 and ec == NOC - 1
                    mm = t.matmul(psbias[:, wi * 8 + ec:wi * 8 + ec + 1],
                                  rawbf[:, ec * 128:(ec + 1) * 128],
                                  ee.t["b1bf"][:, c:c + 1],
                                  start=first, stop=last,
                                  skip_group_check=True)
                    mm = t.matmul(psbias[:, 24 + wi * 8 + ec:24 + wi * 8 + ec + 1],
                                  rawbf[:, ec * 128:(ec + 1) * 128],
                                  ee.t["b1m"][:, wi * 8 + c:wi * 8 + c + 1],
                                  start=False, stop=last,
                                  skip_group_check=True)
                return mm
            ee.Tn(tbias, waits=ee.wv("v_done"), inc=("t_done", 1))

            def vcast(v, wi=wi, c=c, gname=gname):
                dst = w4[:, (wi * NC8 + c) * W512:(wi * NC8 + c + 1) * W512]
                return v.tensor_scalar(out=dst, in0=wld[:, 0:W512],
                                       scalar1=cslot(ee, gname, c),
                                       scalar2=None, op0=ALU.mult)
            ee.V(vcast, waits=ee.wv("g_dma"))
    for rc in range(NOC):
        def gwo(g, pnc, rc=rc):
            cg = pnc % 2
            rb = cg * W512 + rc * 128
            return g.dma_start(out=wld[:],
                               in_=inp["L0_attn_a_wo"][rb:rb + 128, :])
        ee.Gsw(gwo, waits=ee.wv("v_done"), inc=("g_dma", 16))
        ee.V(lambda v, rc=rc: v.tensor_copy(
            out=w4[:, WO_OFF + rc * D:WO_OFF + (rc + 1) * D], in_=wld[:]),
            waits=ee.wv("g_dma"))
    def sc_bias_ev(s):
        last = None
        for wi in range(3):
            s.copy(out=ee.t["dcol"][:, 16 + wi * 8:16 + wi * 8 + NOC],
                   in_=psbias[:, wi * 8:wi * 8 + NOC])
            last = s.copy(out=ee.t["dcol"][:, 48 + wi * 8:48 + wi * 8 + NOC],
                          in_=psbias[:, 24 + wi * 8:24 + wi * 8 + NOC])
        return last
    ee.Sc(sc_bias_ev, waits=ee.wv("t_done"))


    ee.V(lambda v: v.tensor_copy(out=lam_s[:], in_=ee.t["onesT"][:]),
         waits=ee.wv("sc_done"))

    for t_i in range(NTILE):
        last_tile = t_i == NTILE - 1

        def gx(g, pnc, t_i=t_i):
            batch, s, h2 = _role(pnc)
            xb = batch * T + t_i * TT
            return g.dma_start(
                out=x0t[:].rearrange("p (c d) -> p c d", d=D),
                in_=x_in[xb:xb + TT, :].rearrange("(c p) d -> p c d", p=128))
        ee.Gsw(gx, waits=ee.wv("v_done"), inc=("g_dma", 16))
        x_loaded = ee.now("g_dma")

        if t_i > 0:
            def vbcol(v):
                last = None
                for dck in range(NC8):
                    last = v.tensor_copy(
                        out=h1T[:, dck * HTW + 127:dck * HTW + 128],
                        in_=h1T[:, dck * HTW + 128 + TT - 1:dck * HTW + 128 + TT])
                return last
            ee.V(vbcol)

        rowchunks = list(range(TT // 128))
        nrc = len(rowchunks)
        off = 0
        for i, rcix in enumerate(rowchunks):
            src = x0t[:, rcix * D:(rcix + 1) * D]

            def vstat(v, src=src, i=i):
                v.bn_stats(out=stat6[:, 0:6], in_=src[:, 0:512])
                v.bn_stats(out=stat6[:, 6:12], in_=src[:, 512:1024])
                return v.bn_aggr(out=stb[:, 2 * i:2 * i + 2], in_=stat6[:])
            ee.V(vstat, waits=[("g_dma", x_loaded),
                               ("sc_done", ee.now("sc_done"))])

        def scst_all(s, nrc=nrc, off=off):
            s.activation(out=stb[:, 12:12 + nrc], in_=stb[:, 1:2 * nrc:2],
                         func=AF.Ln, scale=(1.0 + EPS), bias=ee.t["cbias"][:, 0:1])
            s.activation(out=stb[:, 12:12 + nrc], in_=stb[:, 12:12 + nrc],
                         func=AF.Exp, scale=-0.5)
            s.activation(out=stb[:, 18:22], in_=stb[:, 2 * off + 1:2 * off + 8:2],
                         func=AF.Ln, bias=ee.t["cbias"][:, 1:2])
            return s.activation(out=stb[:, 18:22], in_=stb[:, 18:22],
                                func=AF.Exp, scale=-0.5)
        ee.Sc(scst_all, waits=ee.wv("v_done"))
        gname0 = "ims_col" if t_i < NTILE // 2 else "s_col"
        ee.V(lambda v, gname0=gname0: v.tensor_scalar(
            out=stb[:, 22:26], in0=stb[:, 18:22],
            scalar1=ee.t[gname0][:, 0:1], scalar2=None, op0=ALU.mult),
            waits=ee.wv("sc_done"))
        stats_done = ee.now("sc_done")

        for i, rcix in enumerate(rowchunks):
            src = x0t[:, rcix * D:(rcix + 1) * D]
            ee.V(lambda v, src=src, i=i: v.tensor_scalar(
                out=h1tm[:], in0=src, scalar1=stb[:, 2 * i:2 * i + 1],
                scalar2=stb[:, 12 + i:13 + i],
                op0=ALU.subtract, op1=ALU.mult),
                waits=[("sc_done", stats_done), ("t_done", ee.now("t_done"))])
            vdn = ee.now("v_done")
            for dck in range(NC8):
                ee.Tn(lambda t, dck=dck:
                      t.transpose(psT[:], h1tm[:, dck * 128:(dck + 1) * 128],
                                  ee.t["eye_bf"][:]),
                      waits=[("v_done", max(vdn, ee.now("v_done")))],
                      inc=("t_done", 1))
                dst = h1T[:, dck * HTW + 128 + rcix * 128:
                          dck * HTW + 128 + (rcix + 1) * 128]
                ee.V(lambda v, dst=dst: v.tensor_copy(out=dst, in_=psT[:]),
                     waits=ee.wv("t_done"))

        if t_i == 0 and "dbg_x0t" in dbg_t:
            ee.G(lambda g: g.dma_start(out=dbg_t["dbg_x0t"][:, :], in_=x0t[:]),
                 waits=[("g_dma", x_loaded)], inc=("g_dma", 16))
            ee.G(lambda g, v=ee.now("g_dma"): g.wait_ge(sems["g_dma"], v))
            ee.barrier()
        if t_i == 0 and "dbg_h1T" in dbg_t:
            ee.G(lambda g: g.dma_start(out=dbg_t["dbg_h1T"][:, :], in_=h1T[:]),
                 waits=ee.wv("v_done"), inc=("g_dma", 16))
            ee.G(lambda g, v=ee.now("g_dma"): g.wait_ge(sems["g_dma"], v))
            ee.barrier()

        def vmix(v):
            last = None
            for dck in range(NC8):
                h = h1T[:, dck * HTW + 128:dck * HTW + 128 + TT]
                hx = h1T[:, dck * HTW + 127:dck * HTW + 127 + TT]
                v.tensor_tensor(out=ddsc[:], in0=h, in1=hx, op=ALU.subtract)
                for mname, mt in (("mk", mixK), ("mv", mixV), ("mr", mixR)):
                    last = v.scalar_tensor_tensor(
                        out=mt[:, dck * TT:(dck + 1) * TT], in0=ddsc[:],
                        scalar=cslot(ee, mname, dck), in1=hx,
                        op0=ALU.mult, op1=ALU.add)
            return last
        ee.V(vmix, waits=ee.wv("t_done"))
        mix_done = ee.now("v_done")

        if t_i == 0 and "dbg_mixK" in dbg_t:
            ee.G(lambda g: g.dma_start(out=dbg_t["dbg_mixK"][:, :], in_=mixK[:]),
                 waits=ee.wv("v_done"), inc=("g_dma", 16))
            ee.G(lambda g, v=ee.now("g_dma"): g.wait_ge(sems["g_dma"], v))
            ee.barrier()

        prev_r_evac = ee.now("sc_done")
        for oc in range(NOC):
            def tk(t, oc=oc):
                mm = None
                for dck in range(NC8):
                    mm = t.matmul(psA[:], w4[:, (0 * NC8 + dck) * W512 + oc * 128:
                                              (0 * NC8 + dck) * W512 + (oc + 1) * 128],
                                  mixK[:, dck * TT:(dck + 1) * TT],
                                  start=(dck == 0), stop=(dck == NC8 - 1))
                return mm
            ee.Tn(tk, waits=[("v_done", mix_done), ("sc_done", prev_r_evac)],
                  inc=("t_done", 1))
            ee.Sc(lambda s, oc=oc: s.activation(out=kc[:], in_=psA[:],
                                                func=AF.Identity,
                                                bias=dslot(ee, "bk", oc)),
                  waits=ee.wv("t_done", "v_done"))
            if t_i == 0:
                ee.V(lambda v, oc=oc: v.tensor_tensor(
                    out=kc[:, 0:1], in0=kc[:, 0:1], in1=dslot(ee, "ck", oc),
                    op=ALU.subtract),
                    waits=ee.wv("sc_done"))
            k_ev = ee.now("sc_done")

            def tv(t, oc=oc):
                mm = None
                for dck in range(NC8):
                    mm = t.matmul(psB[:], w4[:, (1 * NC8 + dck) * W512 + oc * 128:
                                              (1 * NC8 + dck) * W512 + (oc + 1) * 128],
                                  mixV[:, dck * TT:(dck + 1) * TT],
                                  start=(dck == 0), stop=(dck == NC8 - 1))
                return mm
            ee.Tn(tv, waits=[("sc_done", k_ev)], inc=("t_done", 1))
            ee.Sc(lambda s, oc=oc: s.activation(out=vcx[:], in_=psB[:],
                                                func=AF.Identity,
                                                bias=dslot(ee, "bv", oc)),
                  waits=ee.wv("t_done", "v_done"))
            if t_i == 0:
                ee.V(lambda v, oc=oc: v.tensor_tensor(
                    out=vcx[:, 0:1], in0=vcx[:, 0:1], in1=dslot(ee, "cv", oc),
                    op=ALU.subtract),
                    waits=ee.wv("sc_done"))
            v_ev = ee.now("sc_done")

            def tr(t, oc=oc):
                mm = None
                for dck in range(NC8):
                    mm = t.matmul(psA[:], w4[:, (2 * NC8 + dck) * W512 + oc * 128:
                                              (2 * NC8 + dck) * W512 + (oc + 1) * 128],
                                  mixR[:, dck * TT:(dck + 1) * TT],
                                  start=(dck == 0), stop=(dck == NC8 - 1))
                return mm
            ee.Tn(tr, waits=[("sc_done", k_ev)], inc=("t_done", 1))
            if t_i == 0:
                ee.Sc(lambda s, oc=oc: s.activation(out=t1c[:], in_=psA[:],
                                                    func=AF.Identity,
                                                    bias=dslot(ee, "br", oc)),
                      waits=ee.wv("t_done", "v_done"))
                def vfixr(v, oc=oc):
                    v.tensor_tensor(out=t1c[:, 0:1], in0=t1c[:, 0:1],
                                    in1=dslot(ee, "cr", oc), op=ALU.subtract)
                    return v.tensor_copy(out=rpre[:, oc * TT:(oc + 1) * TT],
                                         in_=t1c[:])
                ee.V(vfixr, waits=ee.wv("sc_done"))
                prev_r_evac = ee.now("sc_done")
            else:
                ee.Sc(lambda s, oc=oc: s.activation(
                    out=rpre[:, oc * TT:(oc + 1) * TT], in_=psA[:],
                    func=AF.Identity, bias=dslot(ee, "br", oc)),
                      waits=ee.wv("t_done"))
                prev_r_evac = ee.now("sc_done")

            ee.V(lambda v, oc=oc: v.tensor_scalar(
                out=lam_s[:], in0=ee.t["onesT"][:], scalar1=dslot(ee, "lam", oc),
                scalar2=None, op0=ALU.mult))
            ee.Sc(lambda s: s.activation(out=ekc[:], in_=kc[:], func=AF.Exp),
                  waits=[("sc_done", k_ev), ("v_done", ee.now("v_done"))])
            ek_done = ee.now("sc_done")
            ee.V(lambda v: v.tensor_tensor(out=ekvc[:], in0=ekc[:], in1=vcx[:],
                                           op=ALU.mult),
                 waits=[("sc_done", max(ek_done, v_ev))])

            def vscanA(v, oc=oc):
                return v.tensor_tensor_scan(
                    out=At[:, 1:1 + TT], data0=lam_s[:],
                    data1=ekvc[:], initial=ee.t["carryA"][:, oc:oc + 1],
                    op0=ALU.mult, op1=ALU.add)
            ee.V(vscanA, waits=ee.wv("v_done"))

            def vscanB(v, oc=oc):
                return v.tensor_tensor_scan(
                    out=Bt[:, 1:1 + TT], data0=lam_s[:],
                    data1=ekc[:], initial=ee.t["carryB"][:, oc:oc + 1],
                    op0=ALU.mult, op1=ALU.add)
            ee.V(vscanB, waits=[("sc_done", ek_done)])

            def vcarry(v, oc=oc):
                v.tensor_copy(out=At[:, 0:1], in_=ee.t["carryA"][:, oc:oc + 1])
                v.tensor_copy(out=Bt[:, 0:1], in_=ee.t["carryB"][:, oc:oc + 1])
                v.tensor_copy(out=ee.t["carryA"][:, oc:oc + 1],
                              in_=At[:, TT:TT + 1])
                return v.tensor_copy(out=ee.t["carryB"][:, oc:oc + 1],
                                     in_=Bt[:, TT:TT + 1])
            ee.V(vcarry)

            def vnum(v, oc=oc):
                v.scalar_tensor_tensor(out=t1c[:], in0=ekc[:],
                                       scalar=dslot(ee, "eu", oc), in1=vcx[:],
                                       op0=ALU.mult, op1=ALU.mult)
                v.tensor_tensor(out=rwkv[:, oc * TT:(oc + 1) * TT], in0=t1c[:],
                                in1=At[:, 0:TT], op=ALU.add)
                return v.scalar_tensor_tensor(
                    out=denb[:, oc * TT:(oc + 1) * TT], in0=ekc[:],
                    scalar=dslot(ee, "eu", oc),
                    in1=Bt[:, 0:TT], op0=ALU.mult, op1=ALU.add)
            ee.V(vnum, waits=ee.wv("v_done"))
        def scpost(s):
            s.activation(out=denb[:], in_=denb[:], func=AF.Ln)
            s.activation(out=denb[:], in_=denb[:], func=AF.Exp, scale=-1.0)
            return s.activation(out=rpre[:], in_=rpre[:], func=AF.Tanh,
                                scale=0.5)
        ee.Sc(scpost, waits=ee.wv("v_done"))

        def vrfin(v):
            v.tensor_scalar(out=rpre[:], in0=rpre[:], scalar1=0.5, scalar2=0.5,
                            op0=ALU.mult, op1=ALU.add)
            last = None
            for oc2 in range(NOC):
                sl = slice(oc2 * TT, (oc2 + 1) * TT)
                v.tensor_tensor(out=rwkv[:, sl], in0=rwkv[:, sl],
                                in1=denb[:, sl], op=ALU.mult)
                last = v.tensor_tensor(out=rwkv[:, sl], in0=rwkv[:, sl],
                                       in1=rpre[:, sl], op=ALU.mult)
            return last
        ee.V(vrfin, waits=ee.wv("sc_done"))
        rwkv_done = ee.now("v_done")
        if t_i == 0 and "dbg_rwkv" in dbg_t:
            ee.G(lambda g: g.dma_start(out=dbg_t["dbg_rwkv"][:, :], in_=rwkv[:]),
                 waits=ee.wv("v_done"), inc=("g_dma", 16))
            ee.G(lambda g, v=ee.now("g_dma"): g.wait_ge(sems["g_dma"], v))
            ee.barrier()

        for ts in range(TT // 128):
            def two(t, ts=ts):
                mm = None
                for dck in range(NOC):
                    lhs = rwkv[:, dck * TT + ts * 128:dck * TT + (ts + 1) * 128]
                    for nh in range(2):
                        mm = t.matmul(psO[:, nh * 512:(nh + 1) * 512], lhs,
                                      w4[:, WO_OFF + dck * D + nh * 512:
                                         WO_OFF + dck * D + (nh + 1) * 512],
                                      start=(dck == 0), stop=(dck == NOC - 1))
                return mm
            ee.Tn(two, waits=[("v_done", max(rwkv_done, ee.now("v_done")))],
                  inc=("t_done", 1))

            def vinner(v, ts=ts):
                v.tensor_scalar(out=inner_t[:, ts * D:(ts + 1) * D],
                                in0=x0t[:, ts * D:(ts + 1) * D],
                                scalar1=stb[:, 2 * ts:2 * ts + 1],
                                scalar2=stb[:, 22 + ts:23 + ts],
                                op0=ALU.subtract, op1=ALU.mult)
                return v.tensor_tensor(out=inner_t[:, ts * D:(ts + 1) * D],
                                       in0=inner_t[:, ts * D:(ts + 1) * D],
                                       in1=psO[:], op=ALU.add)
            ee.V(vinner, waits=ee.wv("t_done", "g_dma"))

        def ginner(g, pnc, t_i=t_i):
            batch, s, h2 = _role(pnc)
            if (t_i // (NTILE // 2)) == s:
                dst = scr["inner"][(t_i - s * (NTILE // 2)) * TT:
                                   (t_i - s * (NTILE // 2) + 1) * TT, :]
            else:
                dst = psh["inner"][t_i * TT:(t_i + 1) * TT, :]
            return g.dma_start(
                out=dst.rearrange("(c p) d -> p c d", p=128),
                in_=inner_t[:].rearrange("p (c d) -> p c d", d=D))
        ee.Gsw(ginner, waits=ee.wv("v_done"), inc=("g_dma", 16))

        if t_i == NTILE // 2 - 1:
            ee.G(lambda g, v=ee.now("g_dma"): g.wait_ge(sems["g_dma"], v))

            def gbnd(g, pnc):
                s = pnc % 2
                return g.dma_start(
                    out=psh["bnd"][s:s + 1, :],
                    in_=inner_t[127:128, (TT // 128 - 1) * D:(TT // 128) * D])
            ee.Gsw(gbnd, waits=ee.wv("v_done"), inc=("g_dma", 16))
            ee.G(lambda g, v=ee.now("g_dma"): g.wait_ge(sems["g_dma"], v))
            _send_flag(ee, psh, FL_BND)
            _send_flag(ee, psh, FL_INA)

        if last_tile:
            ee.G(lambda g, v=ee.now("g_dma"): g.wait_ge(sems["g_dma"], v))
            _send_flag(ee, psh, FL_INB)

    ee.G(lambda g, v=ee.now("g_dma"): g.wait_ge(sems["g_dma"], v))


def emit_whoami(ee, whoami):
    sems = ee.sems
    ee.G(lambda g: g.dma_start(out=whoami[:, :], in_=ee.t["id_t"][0:1, 0:1]),
         inc=("g_dma", 16))
    ee.G(lambda g, v=ee.now("g_dma"): g.wait_ge(sems["g_dma"], v))



def emit_p0w(ee, stack, inp, scr):
    nc = ee.nc

    def sb(name, shape, dt=F32):
        h = stack.enter_context(nc.sbuf_tensor(name, shape, dt))
        ee.t[name] = h
        return h

    def ps(name, shape, dt=F32):
        h = stack.enter_context(nc.psum_tensor(name, shape, dt))
        ee.t[name] = h
        return h

    wi32 = [sb("wi32%d" % p, [128, FFN], I32) for p in range(2)]
    wbfc = [sb("wbfc%d" % p, [128, FFN], BF16) for p in range(2)]
    tstage = [sb("tstage%d" % p, [128, FFN], BF16) for p in range(2)]
    psTw = [ps("psTw%d" % p, [128, FFN], BF16) for p in range(2)]

    items = []
    for wname, sname in (("L0_ffn_w_gate", "wgT"), ("L0_ffn_w_up", "wuT")):
        for hc in range(FFN // 128):
            items.append((wname, hc, NC8, sname))
    for ec in range(NC8):
        items.append(("L0_ffn_w_down", ec, FFN // 128, "wdT"))

    cast_done = [0, 0]
    store_done = [0, 0]
    stage_done = [0, 0]
    for idx, (wname, rc, nblk, sname) in enumerate(items):
        p = idx % 2
        ncol = nblk * 128
        ee.S(lambda e, wname=wname, rc=rc, ncol=ncol, p=p: e.dma_start(
            out=wi32[p][:, 0:ncol],
            in_=inp[wname][rc * 128:(rc + 1) * 128, :]),
            waits=[("v_done", cast_done[p])], inc=("s_dma", 16))
        ee.V(lambda v, p=p, ncol=ncol: v.tensor_copy(
            out=wbfc[p][:, 0:ncol], in_=wi32[p][:, 0:ncol]),
            waits=ee.wv("s_dma"))
        cast_done[p] = ee.now("v_done")

        def ttr(t, p=p, nblk=nblk):
            last = None
            for j in range(nblk):
                last = t.transpose(psTw[p][:, j * 128:(j + 1) * 128],
                                   wbfc[p][:, j * 128:(j + 1) * 128],
                                   ee.t["eye_bf"][:])
            return last
        ee.Tn(ttr, waits=[("v_done", cast_done[p]),
                          ("v_done", stage_done[p])], inc=("t_done", 1))
        ee.V(lambda v, p=p, ncol=ncol: v.tensor_copy(
            out=tstage[p][:, 0:ncol], in_=psTw[p][:, 0:ncol]),
            waits=ee.wv("t_done") + [("g_dma", store_done[p])])
        stage_done[p] = ee.now("v_done")

        ee.G(lambda g, sname=sname, rc=rc, p=p, ncol=ncol: g.dma_start(
            out=scr[sname][:, :, rc * 128:(rc + 1) * 128]
            .rearrange("c p h -> p c h"),
            in_=tstage[p][:, 0:ncol].rearrange("p (c h) -> p c h", h=128)),
            waits=ee.wv("v_done"), inc=("g_dma", 16))
        store_done[p] = ee.now("g_dma")
    ee.G(lambda g, v=ee.now("g_dma"): g.wait_ge(ee.sems["g_dma"], v))



def emit_p1b1(ee, stack, inp, x_in, scr, psh):
    nc = ee.nc
    sems = ee.sems
    NTILE = HALF // TT
    HTW = TT + 128
    NFC = FCM // 128

    def sb(name, shape, dt=F32):
        h = stack.enter_context(nc.sbuf_tensor(name, shape, dt))
        ee.t[name] = h
        return h

    def ps(name, shape, dt=F32):
        h = stack.enter_context(nc.psum_tensor(name, shape, dt))
        ee.t[name] = h
        return h

    cwkb = sb("cwkb", [128, NC8 * FCM], BF16)
    wld2 = sb("wld2", [128, FCM])
    rawb2 = sb("rawb2", [128, FCM], BF16)
    dcol2 = sb("dcol2", [128, 128])
    b2bf = sb("b2bf", [128, 8], BF16)
    b2m = sb("b2m", [128, 16], BF16)
    h2T = sb("h2T", [128, NC8 * HTW], BF16)
    mixCK = sb("mixCK", [128, NC8 * TT], BF16)
    mixCR = sb("mixCR", [128, NC8 * TT], BF16)
    inn_t = sb("inn_t", [128, (TT // 128) * D])
    h2tm = sb("h2tm", [128, D], BF16)
    ddsc2 = sb("ddsc2", [128, TT], BF16)
    kkbuf = sb("kkbuf", [128, NFC * TT], BF16)
    krel = sb("krel", [128, 2 * TT])
    stat6b = sb("stat6b", [128, 12])
    stat2b = sb("stat2b", [128, 2])
    stcb = sb("stcb", [128, 8])
    stcb2 = sb("stcb2", [128, 12])
    bndh2 = sb("bndh2", [1, D], BF16)
    bndst = sb("bndst", [1, 16])
    psA2 = ps("psA2", [128, 2 * TT])
    psT2 = ps("psT2", [128, 128], BF16)
    psb2 = ps("psb2", [128, 80])

    def vb2(v):
        v.memset(h2T[:], 0.0)
        v.tensor_copy(out=b2bf[:],
                      in_=ee.t["colpk"][:, CSLOT["b2"] * 8:CSLOT["b2"] * 8 + 8])
        for mi, mname in enumerate(("cmk", "cmr")):
            v.tensor_tensor(out=ee.t["b1mf"][:],
                            in0=ee.t["colpk"][:, CSLOT["b2"] * 8:CSLOT["b2"] * 8 + 8],
                            in1=ee.t["colpk"][:, CSLOT[mname] * 8:CSLOT[mname] * 8 + 8],
                            op=ALU.mult)
            v.tensor_tensor(out=ee.t["b1mf"][:],
                            in0=ee.t["colpk"][:, CSLOT["b2"] * 8:CSLOT["b2"] * 8 + 8],
                            in1=ee.t["b1mf"][:], op=ALU.subtract)
            v.tensor_copy(out=b2m[:, mi * 8:(mi + 1) * 8], in_=ee.t["b1mf"][:])
        return v.memset(bndh2[:], 0.0)
    ee.V(vb2)

    for c in range(NC8):
        ee.S(lambda e, c=c: e.dma_start(
            out=wld2[:], in_=inp["L0_attn_a_cwk"][c * 128:(c + 1) * 128, :]),
            waits=ee.wv("v_done"), inc=("s_dma", 16))
        ee.V(lambda v: v.tensor_copy(out=rawb2[:], in_=wld2[:]),
             waits=ee.wv("s_dma", "t_done"))

        def tb2(t, c=c):
            mm = None
            for fc in range(NFC):
                mm = t.matmul(psb2[:, fc:fc + 1],
                              rawb2[:, fc * 128:(fc + 1) * 128],
                              b2bf[:, c:c + 1],
                              start=(c == 0 and fc == 0), stop=False,
                              skip_group_check=True)
                mm = t.matmul(psb2[:, 32 + fc:32 + fc + 1],
                              rawb2[:, fc * 128:(fc + 1) * 128],
                              b2m[:, c:c + 1],
                              start=False, stop=False,
                              skip_group_check=True)
            return mm
        ee.Tn(tb2, waits=ee.wv("v_done"), inc=("t_done", 1))
        ee.V(lambda v, c=c: v.tensor_scalar(
            out=cwkb[:, c * FCM:(c + 1) * FCM], in0=wld2[:],
            scalar1=cslot(ee, "g2", c), scalar2=None, op0=ALU.mult),
            waits=ee.wv("s_dma"))
    cwrb = sb("cwrb", [128, NC8 * D], BF16)
    for c in range(NC8):
        ee.S(lambda e, c=c: e.dma_start(
            out=wld2[:, 0:D], in_=inp["L0_attn_a_cwr"][c * 128:(c + 1) * 128, :]),
            waits=ee.wv("v_done"), inc=("s_dma", 16))
        ee.V(lambda v: v.tensor_copy(out=rawb2[:, 0:D], in_=wld2[:, 0:D]),
             waits=ee.wv("s_dma", "t_done"))

        def tb3(t, c=c):
            mm = None
            for ec in range(NC8):
                mm = t.matmul(psb2[:, 64 + ec:64 + ec + 1],
                              rawb2[:, ec * 128:(ec + 1) * 128],
                              b2bf[:, c:c + 1],
                              start=False,
                              stop=False, skip_group_check=True)
                mm = t.matmul(psb2[:, 72 + ec:72 + ec + 1],
                              rawb2[:, ec * 128:(ec + 1) * 128],
                              b2m[:, 8 + c:8 + c + 1],
                              start=False,
                              stop=(c == NC8 - 1 and ec == NC8 - 1),
                              skip_group_check=True)
            return mm
        ee.Tn(tb3, waits=ee.wv("v_done"), inc=("t_done", 1))
        ee.V(lambda v, c=c: v.tensor_scalar(
            out=cwrb[:, c * D:(c + 1) * D], in0=wld2[:, 0:D],
            scalar1=cslot(ee, "g2", c), scalar2=None, op0=ALU.mult),
            waits=ee.wv("s_dma"))
    ee.Sc(lambda s: s.copy(out=dcol2[:, 0:80], in_=psb2[:, 0:80]),
          waits=ee.wv("t_done"))

    def gbndr(g, pnc):
        s = pnc % 2
        if s == 1:
            _emit_poll(ee, g, psh, FL_BND, want_side=0)
            g.reg_alu(ee.regs["r_pollcnt"], ee.regs["r_pollcnt"], 32, ALU.add)
            g.dma_start(out=ee.t["bnd_row"][:, :],
                        in_=psh["bnd"][0:1, :]).then_inc(sems["poll"], 16)
            g.dma_start(out=ee.t["bnd_loc"][:, :],
                        in_=psh["bnd"][1:2, :]).then_inc(sems["poll"], 16)
            g.wait_ge(sems["poll"], ee.regs["r_pollcnt"])
        return None
    ee.Gsw(gbndr, done_sem="g_done")

    def ginnp(g, pnc):
        s = pnc % 2
        if s == 0:
            _emit_poll(ee, g, psh, FL_INA, want_side=1)
        else:
            _emit_poll(ee, g, psh, FL_INB, want_side=0)
        return None
    ee.Gsw(ginnp, done_sem="g_done")

    ee.V(lambda v: v.scalar_tensor_tensor(
        out=ee.t["bnd_row"][:], in0=ee.t["bnd_loc"][:],
        scalar=ee.t["s_col"][0:1, 0:1], in1=ee.t["bnd_row"][:],
        op0=ALU.mult, op1=ALU.add),
        waits=ee.wv("g_done"))

    def vbndstat(v):
        v.bn_stats(out=bndst[:, 0:6], in_=ee.t["bnd_row"][:, 0:512])
        v.bn_stats(out=bndst[:, 6:12], in_=ee.t["bnd_row"][:, 512:1024])
        return v.bn_aggr(out=bndst[:, 12:14], in_=bndst[:, 0:12])
    ee.V(vbndstat, waits=ee.wv("g_done"))

    def scbnd(s):
        s.activation(out=bndst[:, 14:15], in_=bndst[:, 13:14], func=AF.Ln,
                     bias=ee.t["cbias"][0:1, 1:2])
        return s.activation(out=bndst[:, 14:15], in_=bndst[:, 14:15],
                            func=AF.Exp, scale=-0.5)
    ee.Sc(scbnd, waits=ee.wv("v_done"))
    ee.V(lambda v: v.tensor_scalar(out=bndh2[:, :], in0=ee.t["bnd_row"][:, :],
                                   scalar1=bndst[:, 12:13],
                                   scalar2=bndst[:, 14:15],
                                   op0=ALU.subtract, op1=ALU.mult),
         waits=ee.wv("sc_done"))
    vdn0 = ee.now("v_done")
    for dck in range(NC8):
        ee.Tn(lambda t, dck=dck: t.transpose(
            psT2[:, 0:1], bndh2[:, dck * 128:(dck + 1) * 128],
            ee.t["eye_bf"][0:1, 0:1]),
            waits=[("v_done", max(vdn0, ee.now("v_done")))], inc=("t_done", 1))
        ee.V(lambda v, dck=dck: v.tensor_copy(
            out=h2T[:, dck * HTW + 127:dck * HTW + 128],
            in_=psT2[:, 0:1]),
            waits=ee.wv("t_done"))

    prev_wb = 0
    evac_done = {0: 0, 1: 0}
    for t_i in range(NTILE):
        ee.S(lambda e, t_i=t_i: e.dma_start(
            out=inn_t[:].rearrange("p (c d) -> p c d", d=D),
            in_=scr["inner"][t_i * TT:(t_i + 1) * TT, :]
            .rearrange("(c p) d -> p c d", p=128)),
            waits=ee.wv("v_done") + [("g_dma", prev_wb)], inc=("s_dma", 16))

        inn2 = wld2

        def ginn2(g, pnc, t_i=t_i):
            batch, s, h2 = _role(pnc)
            rb = s * HALF + t_i * TT
            return g.dma_start(
                out=inn2[:].rearrange("p (c d) -> p c d", d=D),
                in_=psh["inner"][rb:rb + TT, :]
                .rearrange("(c p) d -> p c d", p=128))
        ee.Gsw(ginn2, waits=ee.wv("v_done") + [("g_done", ee.now("g_done"))],
               inc=("g_dma", 16))
        s_loaded = ee.now("s_dma")
        ee.V(lambda v: v.tensor_tensor(out=inn_t[:], in0=inn_t[:], in1=inn2[:],
                                       op=ALU.add),
             waits=[("s_dma", s_loaded), ("g_dma", ee.now("g_dma"))])
        ee.G(lambda g, t_i=t_i: g.dma_start(
            out=scr["inner"][t_i * TT:(t_i + 1) * TT, :]
            .rearrange("(c p) d -> p c d", p=128),
            in_=inn_t[:].rearrange("p (c d) -> p c d", d=D)),
            waits=ee.wv("v_done"), inc=("g_dma", 16))
        prev_wb = ee.now("g_dma")
        if t_i > 0:
            def vbcol2(v):
                last = None
                for dck in range(NC8):
                    last = v.tensor_copy(
                        out=h2T[:, dck * HTW + 127:dck * HTW + 128],
                        in_=h2T[:, dck * HTW + 128 + TT - 1:dck * HTW + 128 + TT])
                return last
            ee.V(vbcol2)
        for rcix in range(TT // 128):
            srcc = inn_t[:, rcix * D:(rcix + 1) * D]

            def vstatb(v, srcc=srcc, rcix=rcix):
                v.bn_stats(out=stat6b[:, 0:6], in_=srcc[:, 0:512])
                v.bn_stats(out=stat6b[:, 6:12], in_=srcc[:, 512:1024])
                return v.bn_aggr(out=stcb2[:, 2 * rcix:2 * rcix + 2],
                                 in_=stat6b[:])
            ee.V(vstatb, waits=[("s_dma", s_loaded),
                                ("sc_done", ee.now("sc_done"))])

        def scstb_all(s):
            s.activation(out=stcb2[:, 8:12], in_=stcb2[:, 1:8:2], func=AF.Ln,
                         bias=ee.t["cbias"][:, 1:2])
            return s.activation(out=stcb2[:, 8:12], in_=stcb2[:, 8:12],
                                func=AF.Exp, scale=-0.5)
        ee.Sc(scstb_all, waits=ee.wv("v_done"))
        bstats_done = ee.now("sc_done")
        for rcix in range(TT // 128):
            srcc = inn_t[:, rcix * D:(rcix + 1) * D]
            ee.V(lambda v, srcc=srcc, rcix=rcix: v.tensor_scalar(
                out=h2tm[:], in0=srcc, scalar1=stcb2[:, 2 * rcix:2 * rcix + 1],
                scalar2=stcb2[:, 8 + rcix:9 + rcix],
                op0=ALU.subtract, op1=ALU.mult),
                waits=[("sc_done", bstats_done), ("t_done", ee.now("t_done"))])
            vdn = ee.now("v_done")
            for dck in range(NC8):
                ee.Tn(lambda t, dck=dck:
                      t.transpose(psT2[:], h2tm[:, dck * 128:(dck + 1) * 128],
                                  ee.t["eye_bf"][:]),
                      waits=[("v_done", max(vdn, ee.now("v_done")))],
                      inc=("t_done", 1))
                dst = h2T[:, dck * HTW + 128 + rcix * 128:
                          dck * HTW + 128 + (rcix + 1) * 128]
                ee.V(lambda v, dst=dst: v.tensor_copy(out=dst, in_=psT2[:]),
                     waits=ee.wv("t_done"))

        def vmix2(v):
            last = None
            for dck in range(NC8):
                h = h2T[:, dck * HTW + 128:dck * HTW + 128 + TT]
                hx = h2T[:, dck * HTW + 127:dck * HTW + 127 + TT]
                v.tensor_tensor(out=ddsc2[:], in0=h, in1=hx, op=ALU.subtract)
                for mname, mt in (("cmk", mixCK), ("cmr", mixCR)):
                    last = v.scalar_tensor_tensor(
                        out=mt[:, dck * TT:(dck + 1) * TT], in0=ddsc2[:],
                        scalar=cslot(ee, mname, dck), in1=hx,
                        op0=ALU.mult, op1=ALU.add)
            return last
        ee.V(vmix2, waits=ee.wv("t_done"))
        mix_done = ee.now("v_done")
        ee.G(lambda g, t_i=t_i: g.dma_start(
            out=scr["mcr"][:, :, t_i * TT:(t_i + 1) * TT]
            .rearrange("c p t -> p c t"),
            in_=mixCR[:].rearrange("p (c t) -> p c t", t=TT)),
            waits=ee.wv("v_done"), inc=("g_dma", 16))

        sc_base = ee.now("sc_done")
        kk_items = []
        for fc in range(NFC):
            pp = fc % 2

            def tkk(t, fc=fc, pp=pp):
                mm = None
                for dck in range(NC8):
                    mm = t.matmul(psA2[:, pp * TT:(pp + 1) * TT],
                                  cwkb[:, dck * FCM + fc * 128:dck * FCM + (fc + 1) * 128],
                                  mixCK[:, dck * TT:(dck + 1) * TT],
                                  start=(dck == 0), stop=(dck == NC8 - 1))
                return mm
            ee.Tn(tkk, waits=[("v_done", mix_done),
                              ("sc_done", evac_done[pp])], inc=("t_done", 1))
            tk_cnt = ee.now("t_done")
            if t_i == 0:
                ee.Sc(lambda s, fc=fc, pp=pp: s.activation(
                    out=krel[:, pp * TT:(pp + 1) * TT],
                    in_=psA2[:, pp * TT:(pp + 1) * TT], func=AF.Identity,
                    bias=dcol2[:, fc:fc + 1]),
                    waits=[("t_done", tk_cnt)])

                def vfixkk(v, fc=fc, pp=pp):
                    v.tensor_tensor(out=stcb[:, 5:6], in0=dcol2[:, 32 + fc:32 + fc + 1],
                                    in1=ee.t["ims_col"][:, 0:1], op=ALU.mult)
                    return v.tensor_tensor(out=krel[:, pp * TT:pp * TT + 1],
                                           in0=krel[:, pp * TT:pp * TT + 1],
                                           in1=stcb[:, 5:6], op=ALU.subtract)
                ee.V(vfixkk, waits=ee.wv("sc_done"))
                ee.Sc(lambda s, pp=pp: s.activation(
                    out=krel[:, pp * TT:(pp + 1) * TT],
                    in_=krel[:, pp * TT:(pp + 1) * TT], func=AF.Relu),
                    waits=ee.wv("v_done"))
                evac_done[pp] = ee.now("sc_done")
                ev_cnt = evac_done[pp]
            else:
                kk_items.append((
                    lambda s, fc=fc, pp=pp: s.activation(
                        out=krel[:, pp * TT:(pp + 1) * TT],
                        in_=psA2[:, pp * TT:(pp + 1) * TT], func=AF.Relu,
                        bias=dcol2[:, fc:fc + 1]),
                    [("t_done", tk_cnt)], None))
                evac_done[pp] = sc_base + len(kk_items)
                ev_cnt = evac_done[pp]
            ee.V(lambda v, fc=fc, pp=pp: v.tensor_tensor(
                out=kkbuf[:, fc * TT:(fc + 1) * TT],
                in0=krel[:, pp * TT:(pp + 1) * TT],
                in1=krel[:, pp * TT:(pp + 1) * TT],
                op=ALU.mult),
                waits=[("sc_done", ev_cnt)])
        if kk_items:
            ee.Sc_batch(kk_items)
        ee.G(lambda g, t_i=t_i: g.dma_start(
            out=scr["kk2"][:, :, t_i * TT:(t_i + 1) * TT]
            .rearrange("c p t -> p c t"),
            in_=kkbuf[:].rearrange("p (c t) -> p c t", t=TT)),
            waits=ee.wv("v_done"), inc=("g_dma", 16))
    ee.G(lambda g, v=ee.now("g_dma"): g.wait_ge(sems["g_dma"], v))
    ee.Sc(lambda s: s.copy(out=ee.t["dcol"][:, 72:88], in_=dcol2[:, 64:80]),
          waits=ee.wv("sc_done"))



def emit_p1b2(ee, stack, inp, x_in, scr, psh):
    nc = ee.nc
    sems = ee.sems
    NFC = FCM // 128
    NG = HALF // TT

    def sb(name, shape, dt=F32):
        h = stack.enter_context(nc.sbuf_tensor(name, shape, dt))
        ee.t[name] = h
        return h

    def ps(name, shape, dt=F32):
        h = stack.enter_context(nc.psum_tensor(name, shape, dt))
        ee.t[name] = h
        return h

    cwvb = sb("cwvb", [128, NFC * D], BF16)
    cwrb2 = sb("cwrb2", [128, NC8 * D], BF16)
    wld3 = sb("wld3", [128, D])
    kk2g = sb("kk2g", [128, NFC * TT], BF16)
    mcrg = sb("mcrg", [128, NC8 * TT], BF16)
    x0g = sb("x0g", [128, (TT // 128) * D])
    inng = sb("inng", [128, (TT // 128) * D])
    rrg = sb("rrg", [128, (TT // 128) * D])
    bcrb = sb("bcrb", [128, D])
    ccrr = sb("ccrr", [1, D])
    onebf = sb("onebf", [1, 128], BF16)
    rowbf = sb("rowbf", [1, D], BF16)
    psM = [ps("psM%d" % i, [128, D]) for i in range(4)]

    for c in range(NFC):
        ee.S(lambda e, c=c: e.dma_start(
            out=wld3[:], in_=inp["L0_attn_a_cwv"][c * 128:(c + 1) * 128, :]),
            waits=ee.wv("v_done"), inc=("s_dma", 16))
        ee.V(lambda v, c=c: v.tensor_copy(out=cwvb[:, c * D:(c + 1) * D],
                                          in_=wld3[:]),
             waits=ee.wv("s_dma"))
    for c in range(NC8):
        ee.S(lambda e, c=c: e.dma_start(
            out=wld3[:], in_=inp["L0_attn_a_cwr"][c * 128:(c + 1) * 128, :]),
            waits=ee.wv("v_done"), inc=("s_dma", 16))
        ee.V(lambda v, c=c: v.tensor_scalar(
            out=cwrb2[:, c * D:(c + 1) * D], in0=wld3[:],
            scalar1=cslot(ee, "g2", c), scalar2=None, op0=ALU.mult),
            waits=ee.wv("s_dma"))
    ee.V(lambda v: v.memset(onebf[:], 1.0))
    colbf = sb("colbf", [128, 16], BF16)
    ee.V(lambda v: v.tensor_copy(out=colbf[:], in_=ee.t["dcol"][:, 72:88]),
         waits=ee.wv("sc_done"))
    for ec in range(NC8):
        ee.Tn(lambda t, ec=ec: t.matmul(psM[0][0:1, 0:128], colbf[:, ec:ec + 1],
                                        ee.t["eye_bf"][:], start=True, stop=True),
              waits=ee.wv("v_done"), inc=("t_done", 1))
        ee.V(lambda v, ec=ec: v.tensor_copy(out=rowbf[:, ec * 128:(ec + 1) * 128],
                                            in_=psM[0][0:1, 0:128]),
             waits=ee.wv("t_done"))
    for ec in range(NC8):
        ee.Tn(lambda t, ec=ec: t.matmul(psM[0][0:1, 0:128], colbf[:, 8 + ec:8 + ec + 1],
                                        ee.t["eye_bf"][:], start=True, stop=True),
              waits=ee.wv("v_done"), inc=("t_done", 1))
        ee.V(lambda v, ec=ec: v.tensor_copy(out=ccrr[:, ec * 128:(ec + 1) * 128],
                                            in_=psM[0][0:1, 0:128]),
             waits=ee.wv("t_done"))
    for nh in range(2):
        ee.Tn(lambda t, nh=nh: t.matmul(psM[0][:, nh * 512:(nh + 1) * 512],
                                        onebf[:, 0:128],
                                        rowbf[:, nh * 512:(nh + 1) * 512],
                                        start=True, stop=True),
              waits=ee.wv("v_done"), inc=("t_done", 1))
    ee.V(lambda v: v.tensor_copy(out=bcrb[:], in_=psM[0][:]),
         waits=ee.wv("t_done"))

    for gi in range(NG):
        ee.S(lambda e, gi=gi: e.dma_start(
            out=kk2g[:].rearrange("p (c t) -> p c t", t=TT),
            in_=scr["kk2"][:, :, gi * TT:(gi + 1) * TT]
            .rearrange("c p t -> p c t")),
            waits=ee.wv("v_done", "t_done"), inc=("s_dma", 16))
        ee.S(lambda e, gi=gi: e.dma_start(
            out=mcrg[:].rearrange("p (c t) -> p c t", t=TT),
            in_=scr["mcr"][:, :, gi * TT:(gi + 1) * TT]
            .rearrange("c p t -> p c t")),
            inc=("s_dma", 16))
        ee.S(lambda e, gi=gi: e.dma_start(
            out=inng[:].rearrange("p (c d) -> p c d", d=D),
            in_=scr["inner"][gi * TT:(gi + 1) * TT, :]
            .rearrange("(c p) d -> p c d", p=128)),
            inc=("s_dma", 16))

        def gx0(g, pnc, gi=gi):
            batch, s, h2 = _role(pnc)
            xb = batch * T + s * HALF + gi * TT
            return g.dma_start(
                out=x0g[:].rearrange("p (c d) -> p c d", d=D),
                in_=x_in[xb:xb + TT, :].rearrange("(c p) d -> p c d", p=128))
        ee.Gsw(gx0, waits=ee.wv("v_done"), inc=("g_dma", 16))
        ld = [("s_dma", ee.now("s_dma")), ("g_dma", ee.now("g_dma"))]

        for ts in range(TT // 128):
            def trr(t, ts=ts):
                mm = None
                for dck in range(NC8):
                    lhs = mcrg[:, dck * TT + ts * 128:dck * TT + (ts + 1) * 128]
                    for nh in range(2):
                        mm = t.matmul(psM[ts][:, nh * 512:(nh + 1) * 512], lhs,
                                      cwrb2[:, dck * D + nh * 512:dck * D + (nh + 1) * 512],
                                      start=(dck == 0), stop=(dck == NC8 - 1))
                return mm
            ee.Tn(trr, waits=ld + ee.wv("v_done"), inc=("t_done", 1))

            def vrr(v, ts=ts, gi=gi):
                v.tensor_tensor(out=rrg[:, ts * D:(ts + 1) * D], in0=psM[ts][:],
                                in1=bcrb[:], op=ALU.add)
                if gi == 0 and ts == 0:
                    v.tensor_scalar(out=ccrr[:, :], in0=ccrr[:, :],
                                    scalar1=ee.t["ims_col"][0:1, 0:1],
                                    scalar2=None, op0=ALU.mult)
                    return v.tensor_tensor(out=rrg[0:1, 0:D],
                                           in0=rrg[0:1, 0:D], in1=ccrr[:, :],
                                           op=ALU.subtract)
                return None
            wl = ee.wv("t_done")

            def vrr2(v, vrr=vrr):
                r = vrr(v)
                return r
            ee.V(vrr2, waits=wl)
        ee.Sc(lambda s: s.activation(out=rrg[:], in_=rrg[:], func=AF.Tanh,
                                     scale=0.5),
              waits=ee.wv("v_done"))
        ee.V(lambda v: v.tensor_scalar(out=rrg[:], in0=rrg[:], scalar1=0.5,
                                       scalar2=0.5, op0=ALU.mult, op1=ALU.add),
             waits=ee.wv("sc_done"))

        for ts in range(TT // 128):
            def tcm(t, ts=ts):
                mm = None
                for fc in range(NFC):
                    lhs = kk2g[:, fc * TT + ts * 128:fc * TT + (ts + 1) * 128]
                    for nh in range(2):
                        mm = t.matmul(psM[ts][:, nh * 512:(nh + 1) * 512], lhs,
                                      cwvb[:, fc * D + nh * 512:fc * D + (nh + 1) * 512],
                                      start=(fc == 0), stop=(fc == NFC - 1))
                return mm
            ee.Tn(tcm, waits=ee.wv("sc_done") + [("v_done", ee.now("v_done"))],
                  inc=("t_done", 1))

            def vx1(v, ts=ts):
                sl = slice(ts * D, (ts + 1) * D)
                v.tensor_tensor(out=x0g[:, sl], in0=x0g[:, sl], in1=inng[:, sl],
                                op=ALU.add)
                v.scalar_tensor_tensor(out=rrg[:, sl], in0=rrg[:, sl],
                                       scalar=1.0, in1=psM[ts][:],
                                       op0=ALU.mult, op1=ALU.mult)
                return v.tensor_tensor(out=inng[:, sl], in0=x0g[:, sl],
                                       in1=rrg[:, sl], op=ALU.add)
            ee.V(vx1, waits=ee.wv("t_done"))
        ee.G(lambda g, gi=gi: g.dma_start(
            out=scr["x1"][gi * TT:(gi + 1) * TT, :]
            .rearrange("(c p) d -> p c d", p=128),
            in_=inng[:].rearrange("p (c d) -> p c d", d=D)),
            waits=ee.wv("v_done"), inc=("g_dma", 16))
    ee.G(lambda g, v=ee.now("g_dma"): g.wait_ge(sems["g_dma"], v))



def emit_ffn(ee, stack, inp, scr, src_name, dst, ntok, dbgtag=""):
    nc = ee.nc
    sems = ee.sems
    NH = FFN // 128
    NG = ntok // TT

    def sb(name, shape, dt=F32):
        h = stack.enter_context(nc.sbuf_tensor(dbgtag + name, shape, dt))
        ee.t[dbgtag + name] = h
        return h

    def ps(name, shape, dt=F32):
        h = stack.enter_context(nc.psum_tensor(dbgtag + name, shape, dt))
        ee.t[dbgtag + name] = h
        return h

    inner1 = contextlib.ExitStack()

    def sb1(name, shape, dt=F32):
        h = inner1.enter_context(nc.sbuf_tensor(dbgtag + name, shape, dt))
        ee.t[dbgtag + name] = h
        return h
    wgTb = sb1("wgTb", [128, NC8 * FFN], BF16)
    wuTb = sb1("wuTb", [128, NC8 * FFN], BF16)
    xsg = sb1("xsg", [128, (TT // 128) * D])
    xbu = sb1("xbu", [128, D], BF16)
    xbT = sb1("xbT", [128, NC8 * TT], BF16)
    hhb = sb1("hhb", [128, NH * TT], BF16)
    gtmp = sb1("gtmp", [128, 2 * TT], BF16)
    stat6f = sb1("stat6f", [128, 12])
    stat2f = sb1("stat2f", [128, 2])
    stcf = sb1("stcf", [128, 12])
    psF = ee.t[dbgtag + "psF"] = inner1.enter_context(
        nc.psum_tensor(dbgtag + "psF", [128, 2 * TT], F32))
    psG = ee.t[dbgtag + "psG"] = inner1.enter_context(
        nc.psum_tensor(dbgtag + "psG", [128, 2 * TT], F32))
    psTf = ee.t[dbgtag + "psTf"] = inner1.enter_context(
        nc.psum_tensor(dbgtag + "psTf", [128, 128], BF16))

    for dc in range(NC8):
        ee.S(lambda e, dc=dc: e.dma_start(out=wgTb[:, dc * FFN:(dc + 1) * FFN],
                                          in_=scr["wgT"][dc, :, :]),
             inc=("s_dma", 16))
        ee.S(lambda e, dc=dc: e.dma_start(out=wuTb[:, dc * FFN:(dc + 1) * FFN],
                                          in_=scr["wuT"][dc, :, :]),
             inc=("s_dma", 16))
    wload = ee.now("s_dma")

    hh_parts = []
    silu_done = {0: 0, 1: 0}
    stt_done = {0: 0, 1: 0}
    for gi in range(NG):
        ee.S(lambda e, gi=gi: e.dma_start(
            out=xsg[:].rearrange("p (c d) -> p c d", d=D),
            in_=scr[src_name][gi * TT:(gi + 1) * TT, :]
            .rearrange("(c p) d -> p c d", p=128)),
            waits=ee.wv("v_done"), inc=("s_dma", 16))
        sld = ee.now("s_dma")
        for rc in range(TT // 128):
            srcc = xsg[:, rc * D:(rc + 1) * D]

            def vstf(v, srcc=srcc, rc=rc):
                v.bn_stats(out=stat6f[:, 0:6], in_=srcc[:, 0:512])
                v.bn_stats(out=stat6f[:, 6:12], in_=srcc[:, 512:1024])
                return v.bn_aggr(out=stcf[:, 2 * rc:2 * rc + 2], in_=stat6f[:])
            ee.V(vstf, waits=[("s_dma", sld), ("sc_done", ee.now("sc_done"))])

        def scstf_all(s):
            s.activation(out=stcf[:, 8:12], in_=stcf[:, 1:8:2], func=AF.Ln,
                         bias=ee.t["cbias"][:, 1:2])
            return s.activation(out=stcf[:, 8:12], in_=stcf[:, 8:12],
                                func=AF.Exp, scale=-0.5)
        ee.Sc(scstf_all, waits=ee.wv("v_done"))
        fstats_done = ee.now("sc_done")
        for rc in range(TT // 128):
            srcc = xsg[:, rc * D:(rc + 1) * D]
            ee.V(lambda v, srcc=srcc, rc=rc: v.tensor_scalar(
                out=xbu[:], in0=srcc, scalar1=stcf[:, 2 * rc:2 * rc + 1],
                scalar2=stcf[:, 8 + rc:9 + rc], op0=ALU.subtract, op1=ALU.mult),
                waits=[("sc_done", fstats_done), ("t_done", ee.now("t_done"))])
            vdn = ee.now("v_done")
            for dck in range(NC8):
                ee.Tn(lambda t, dck=dck:
                      t.transpose(psTf[:], xbu[:, dck * 128:(dck + 1) * 128],
                                  ee.t["eye_bf"][:]),
                      waits=[("v_done", max(vdn, ee.now("v_done")))],
                      inc=("t_done", 1))
                dstc = xbT[:, dck * TT + rc * 128:dck * TT + (rc + 1) * 128]
                ee.V(lambda v, dstc=dstc: v.tensor_copy(out=dstc, in_=psTf[:]),
                     waits=ee.wv("t_done"))
        xT_done = ee.now("v_done")
        sc_base2 = ee.now("sc_done")
        silu_items = []
        for hc in range(NH):
            pp = hc % 2

            def tgate(t, hc=hc, pp=pp):
                mm = None
                for dck in range(NC8):
                    mm = t.matmul(psF[:, pp * TT:(pp + 1) * TT],
                                  wgTb[:, dck * FFN + hc * 128:dck * FFN + (hc + 1) * 128],
                                  xbT[:, dck * TT:(dck + 1) * TT],
                                  start=(dck == 0), stop=(dck == NC8 - 1))
                return mm
            ee.Tn(tgate, waits=[("v_done", xT_done), ("s_dma", wload),
                                ("sc_done", silu_done[pp])],
                  inc=("t_done", 1))
            tg_cnt = ee.now("t_done")
            silu_items.append((
                lambda s, pp=pp: s.activation(
                    out=gtmp[:, pp * TT:(pp + 1) * TT],
                    in_=psF[:, pp * TT:(pp + 1) * TT], func=AF.Silu,
                    scale=1.0 / 127.0),
                [("t_done", tg_cnt), ("v_done", stt_done[pp])], None))
            silu_done[pp] = sc_base2 + len(silu_items)
            g_ev = silu_done[pp]

            def tup(t, hc=hc, pp=pp):
                mm = None
                for dck in range(NC8):
                    mm = t.matmul(psG[:, pp * TT:(pp + 1) * TT],
                                  wuTb[:, dck * FFN + hc * 128:dck * FFN + (hc + 1) * 128],
                                  xbT[:, dck * TT:(dck + 1) * TT],
                                  start=(dck == 0), stop=(dck == NC8 - 1))
                return mm
            ee.Tn(tup, waits=[("v_done", stt_done[pp])], inc=("t_done", 1))
            ee.V(lambda v, hc=hc, pp=pp: v.scalar_tensor_tensor(
                out=hhb[:, hc * TT:(hc + 1) * TT],
                in0=gtmp[:, pp * TT:(pp + 1) * TT], scalar=1.0,
                in1=psG[:, pp * TT:(pp + 1) * TT], op0=ALU.mult, op1=ALU.mult),
                waits=[("sc_done", g_ev), ("t_done", ee.now("t_done"))])
            stt_done[pp] = ee.now("v_done")
        ee.Sc_batch(silu_items)
        ee.G(lambda g, gi=gi: g.dma_start(
            out=scr["hh"][:, :, gi * TT:(gi + 1) * TT]
            .rearrange("c p t -> p c t"),
            in_=hhb[:].rearrange("p (c t) -> p c t", t=TT)),
            waits=ee.wv("v_done"), inc=("g_dma", 16))
    ee.G(lambda g, v=ee.now("g_dma"): g.wait_ge(sems["g_dma"], v))
    ee.barrier()
    inner1.close()

    wdTb = sb("wdTb", [128, NH * D], BF16)
    hhg = sb("hhg", [128, NH * TT], BF16)
    xrg = sb("xrg", [128, (TT // 128) * D])
    psD = [ps("psD%d" % i, [128, D]) for i in range(4)]
    for fc in range(NH):
        ee.S(lambda e, fc=fc: e.dma_start(out=wdTb[:, fc * D:(fc + 1) * D],
                                          in_=scr["wdT"][fc, :, :]),
             inc=("s_dma", 16))
    wdl = ee.now("s_dma")
    for gi in range(NG):
        ee.S(lambda e, gi=gi: e.dma_start(
            out=hhg[:].rearrange("p (c t) -> p c t", t=TT),
            in_=scr["hh"][:, :, gi * TT:(gi + 1) * TT]
            .rearrange("c p t -> p c t")),
            waits=ee.wv("v_done"), inc=("s_dma", 16))
        ee.S(lambda e, gi=gi: e.dma_start(
            out=xrg[:].rearrange("p (c d) -> p c d", d=D),
            in_=scr[src_name][gi * TT:(gi + 1) * TT, :]
            .rearrange("(c p) d -> p c d", p=128)),
            inc=("s_dma", 16))
        sld = ee.now("s_dma")
        for ts in range(TT // 128):
            def tdn(t, ts=ts):
                mm = None
                for fc in range(NH):
                    lhs = hhg[:, fc * TT + ts * 128:fc * TT + (ts + 1) * 128]
                    for nh in range(2):
                        mm = t.matmul(psD[ts][:, nh * 512:(nh + 1) * 512], lhs,
                                      wdTb[:, fc * D + nh * 512:fc * D + (nh + 1) * 512],
                                      start=(fc == 0), stop=(fc == NH - 1))
                return mm
            ee.Tn(tdn, waits=[("s_dma", max(sld, wdl)),
                              ("v_done", ee.now("v_done"))],
                  inc=("t_done", 1))

            def vx2(v, ts=ts):
                sl = slice(ts * D, (ts + 1) * D)
                v.tensor_scalar(out=xrg[:, sl], in0=xrg[:, sl], scalar1=1.0,
                                scalar2=None, op0=ALU.mult)
                return v.scalar_tensor_tensor(
                    out=xrg[:, sl], in0=psD[ts][:],
                    scalar=1.0 / (127.0 * 127.0), in1=xrg[:, sl],
                    op0=ALU.mult, op1=ALU.add)
            ee.V(vx2, waits=ee.wv("t_done"))
        ee.G(lambda g, gi=gi: g.dma_start(
            out=dst[gi * TT:(gi + 1) * TT, :].rearrange("(c p) d -> p c d", p=128),
            in_=xrg[:].rearrange("p (c d) -> p c d", d=D)),
            waits=ee.wv("v_done"), inc=("g_dma", 16))
    ee.G(lambda g, v=ee.now("g_dma"): g.wait_ge(sems["g_dma"], v))



def emit_p3(ee, inp, scr, psh):
    nc = ee.nc
    sems = ee.sems
    import contextlib as _ctx
    SH = 16

    with _ctx.ExitStack() as ost:
        def osb(name, shape, dt=F32):
            h = ost.enter_context(nc.sbuf_tensor(name, shape, dt))
            ee.t[name] = h
            return h
        qTb = None
        attoT = None

        with _ctx.ExitStack() as ast_:
            def asb(name, shape, dt=F32):
                h = ast_.enter_context(nc.sbuf_tensor(name, shape, dt))
                ee.t[name] = h
                return h
            def aps(name, shape, dt=F32):
                h = ast_.enter_context(nc.psum_tensor(name, shape, dt))
                ee.t[name] = h
                return h
            xcT = asb("xcT", [128, NC8 * HALF], BF16)
            x2g = asb("x2g", [128, (TT // 128) * D])
            xcu = asb("xcu", [128, D], BF16)
            wkb = asb("wkb", [128, NC8 * NKV], BF16)
            wvb = asb("wvb", [128, NC8 * NKV], BF16)
            wldp = asb("wldp", [128, NKV])
            ctab = asb("ctab", [128, 1024])
            stab = asb("stab", [128, 1024])
            ktmp = asb("ktmp", [128, 1024])
            kshu = asb("kshu", [128, 1024])
            krop = asb("krop", [128, 1024])
            kvst = asb("kvst", [128, 4 * HALF], BF16)
            st6p = asb("st6p", [128, 12])
            st2p = asb("st2p", [128, 2])
            stcp = asb("stcp", [128, 12])
            psP = aps("psP", [128, 1024])
            psTp = aps("psTp", [128, 128], BF16)

            for c in range(NC8):
                ee.S(lambda e, c=c: e.dma_start(
                    out=wldp[:], in_=inp["L0_attn_b_Wk"][c * 128:(c + 1) * 128, :]),
                    waits=ee.wv("v_done"), inc=("s_dma", 16))
                ee.V(lambda v, c=c: v.tensor_copy(out=wkb[:, c * NKV:(c + 1) * NKV],
                                                  in_=wldp[:]),
                     waits=ee.wv("s_dma"))
                ee.S(lambda e, c=c: e.dma_start(
                    out=wldp[:], in_=inp["L0_attn_b_Wv"][c * 128:(c + 1) * 128, :]),
                    waits=ee.wv("v_done"), inc=("s_dma", 16))
                ee.V(lambda v, c=c: v.tensor_copy(out=wvb[:, c * NKV:(c + 1) * NKV],
                                                  in_=wldp[:]),
                     waits=ee.wv("s_dma"))

            for gi in range(HALF // TT):
                ee.S(lambda e, gi=gi: e.dma_start(
                    out=x2g[:].rearrange("p (c d) -> p c d", d=D),
                    in_=scr["x2"][gi * TT:(gi + 1) * TT, :]
                    .rearrange("(c p) d -> p c d", p=128)),
                    waits=ee.wv("v_done"), inc=("s_dma", 16))
                sld = ee.now("s_dma")
                for rc in range(TT // 128):
                    srcc = x2g[:, rc * D:(rc + 1) * D]

                    def vstp(v, srcc=srcc, rc=rc):
                        v.bn_stats(out=st6p[:, 0:6], in_=srcc[:, 0:512])
                        v.bn_stats(out=st6p[:, 6:12], in_=srcc[:, 512:1024])
                        return v.bn_aggr(out=stcp[:, 2 * rc:2 * rc + 2],
                                         in_=st6p[:])
                    ee.V(vstp, waits=[("s_dma", sld),
                                      ("sc_done", ee.now("sc_done"))])

                def scp_all(s):
                    s.activation(out=stcp[:, 8:12], in_=stcp[:, 1:8:2],
                                 func=AF.Ln, bias=ee.t["cbias"][:, 1:2])
                    return s.activation(out=stcp[:, 8:12], in_=stcp[:, 8:12],
                                        func=AF.Exp, scale=-0.5)
                ee.Sc(scp_all, waits=ee.wv("v_done"))
                pstats_done = ee.now("sc_done")
                for rc in range(TT // 128):
                    srcc = x2g[:, rc * D:(rc + 1) * D]
                    ee.V(lambda v, srcc=srcc, rc=rc: v.tensor_scalar(
                        out=xcu[:], in0=srcc, scalar1=stcp[:, 2 * rc:2 * rc + 1],
                        scalar2=stcp[:, 8 + rc:9 + rc],
                        op0=ALU.subtract, op1=ALU.mult),
                        waits=[("sc_done", pstats_done),
                               ("t_done", ee.now("t_done"))])
                    vdn = ee.now("v_done")
                    for dck in range(NC8):
                        ee.Tn(lambda t, dck=dck: t.transpose(
                            psTp[:, 0:128], xcu[:, dck * 128:(dck + 1) * 128],
                            ee.t["eye_bf"][:]),
                            waits=[("v_done", max(vdn, ee.now("v_done")))],
                            inc=("t_done", 1))
                        dstc = xcT[:, dck * HALF + gi * TT + rc * 128:
                                   dck * HALF + gi * TT + (rc + 1) * 128]
                        ee.V(lambda v, dstc=dstc: v.tensor_copy(out=dstc,
                                                                in_=psTp[:, 0:128]),
                             waits=ee.wv("t_done"))
            xcT_done = ee.now("v_done")

            def gtab(g, pnc, tb=None, which=None):
                pass
            for tb in range(HALF // 1024):
                def gct(g, pnc, tb=tb):
                    batch, s, h2 = _role(pnc)
                    spos = s * HALF
                    g.dma_start(out=ctab[:],
                                in_=inp["rope_cos"][:, spos + tb * 1024:
                                                    spos + (tb + 1) * 1024]
                                ).then_inc(sems["g_dma"], 16)
                    return g.dma_start(out=stab[:],
                                       in_=inp["rope_sin"][:, spos + tb * 1024:
                                                           spos + (tb + 1) * 1024])
                ee.Gsw(gct, waits=ee.wv("v_done"), inc=("g_dma", 16))
                ee.cnt["g_dma"] += 16
                tabld = ee.now("g_dma")
                for kc4 in range(4):
                    def tkp(t, tb=tb, kc4=kc4):
                        mm = None
                        for dck in range(NC8):
                            for nn in range(2):
                                mm = t.matmul(
                                    psP[:, nn * 512:(nn + 1) * 512],
                                    wkb[:, dck * NKV + kc4 * 128:dck * NKV + (kc4 + 1) * 128],
                                    xcT[:, dck * HALF + tb * 1024 + nn * 512:
                                        dck * HALF + tb * 1024 + (nn + 1) * 512],
                                    start=(dck == 0), stop=(dck == NC8 - 1))
                        return mm
                    ee.Tn(tkp, waits=[("v_done", max(xcT_done, ee.now("v_done"))),
                                      ("sc_done", ee.now("sc_done"))],
                          inc=("t_done", 1))
                    ee.Sc(lambda s: s.copy(out=ktmp[:], in_=psP[:]),
                          waits=ee.wv("t_done"))

                    def vrope(v, tb=tb, kc4=kc4):
                        mask = []
                        for i in range(0, 32, 2):
                            mask += [i + 1, i]
                        v.stream_shuffle(out=kshu[:], in_=ktmp[:], mask=mask)
                        v.tensor_tensor(out=krop[:], in0=ktmp[:], in1=ctab[:],
                                        op=ALU.mult)
                        v.scalar_tensor_tensor(out=kshu[:], in0=kshu[:],
                                               scalar=1.0, in1=stab[:],
                                               op0=ALU.mult, op1=ALU.mult)
                        return v.tensor_tensor(
                            out=kvst[:, kc4 * HALF + tb * 1024:
                                     kc4 * HALF + (tb + 1) * 1024],
                            in0=krop[:], in1=kshu[:], op=ALU.add)
                    ee.V(vrope, waits=[("g_dma", tabld),
                                       ("sc_done", ee.now("sc_done"))])
            def gkw(g, pnc):
                batch, s, h2 = _role(pnc)
                return g.dma_start(
                    out=psh["kT"][s * NKV:(s + 1) * NKV, :]
                    .rearrange("(c p) t -> p c t", p=128),
                    in_=kvst[:].rearrange("p (c t) -> p c t", t=HALF))
            ee.Gsw(gkw, waits=ee.wv("v_done"), inc=("g_dma", 16))
            ee.G(lambda g, v=ee.now("g_dma"): g.wait_ge(sems["g_dma"], v))

            for tc in range(HALF // 128):
                def tvp(t, tc=tc):
                    mm = None
                    for dck in range(NC8):
                        mm = t.matmul(
                            psP[:, 0:NKV],
                            xcT[:, dck * HALF + tc * 128:dck * HALF + (tc + 1) * 128],
                            wvb[:, dck * NKV:(dck + 1) * NKV],
                            start=(dck == 0), stop=(dck == NC8 - 1))
                    return mm
                ee.Tn(tvp, waits=[("sc_done", ee.now("sc_done")),
                                  ("v_done", ee.now("v_done"))],
                      inc=("t_done", 1))
                ee.Sc(lambda s, tc=tc: s.copy(
                    out=kvst[:, (tc % 32) * NKV:(tc % 32) * NKV + NKV]
                    .rearrange("p t -> p t") if False else
                    kvst[:, (tc % 32) * NKV:(tc % 32 + 1) * NKV],
                    in_=psP[:, 0:NKV]),
                    waits=ee.wv("t_done"))
                if tc % 32 == 31:
                    blk = tc // 32

                    def gvw(g, pnc, blk=blk):
                        batch, s, h2 = _role(pnc)
                        return g.dma_start(
                            out=psh["v"][s * HALF + blk * 4096:
                                         s * HALF + (blk + 1) * 4096, :]
                            .rearrange("(c p) t -> p c t", p=128),
                            in_=kvst[:, 0:32 * NKV]
                            .rearrange("p (c t) -> p c t", t=NKV))
                    ee.Gsw(gvw, waits=ee.wv("sc_done"), inc=("g_dma", 16))
                    ee.G(lambda g, v=ee.now("g_dma"): g.wait_ge(sems["g_dma"], v))
            _send_flag(ee, psh, FL_KV)

        ee.barrier()
        qTb = osb("qTb", [128, NC8 * QB], BF16)
        attoT = osb("attoT", [128, NC8 * QB], BF16)

        with _ctx.ExitStack() as bst:
            def bsb(name, shape, dt=F32):
                h = bst.enter_context(nc.sbuf_tensor(name, shape, dt))
                ee.t[name] = h
                return h
            def bps(name, shape, dt=F32):
                h = bst.enter_context(nc.psum_tensor(name, shape, dt))
                ee.t[name] = h
                return h
            xcq = bsb("xcq", [128, NC8 * QB], BF16)
            x2g = bsb("x2gB", [128, (TT // 128) * D])
            xcu = bsb("xcuB", [128, D], BF16)
            wqb1 = bsb("wqb1", [128, NC8 * 512], BF16)
            wqb2 = bsb("wqb2", [128, NC8 * 512], BF16)
            wldp = bsb("wldpB", [128, 512])
            ctab = bsb("ctabB", [128, 1024])
            stab = bsb("stabB", [128, 1024])
            ktmp = bsb("ktmpB", [128, 1024])
            kshu = bsb("kshuB", [128, 1024])
            krop = bsb("kropB", [128, 1024])
            st6p = bsb("st6pB", [128, 12])
            st2p = bsb("st2pB", [128, 2])
            stcp = bsb("stcpB", [128, 12])
            psP = bps("psPB", [128, 1024])
            psTp = bps("psTpB", [128, 128], BF16)

            for c in range(NC8):
                for half_i, wdst in ((0, wqb1), (1, wqb2)):
                    ee.S(lambda e, c=c, half_i=half_i: e.dma_start(
                        out=wldp[:],
                        in_=inp["L0_attn_b_Wq"][c * 128:(c + 1) * 128,
                                                half_i * 512:(half_i + 1) * 512]),
                        waits=ee.wv("v_done"), inc=("s_dma", 16))
                    ee.V(lambda v, c=c, wdst=wdst: v.tensor_copy(
                        out=wdst[:, c * 512:(c + 1) * 512], in_=wldp[:]),
                        waits=ee.wv("s_dma"))

            for gi in range(QB // TT):
                def gx2qb(g, pnc, gi=gi):
                    batch, s, h2 = _role(pnc)
                    off = h2 * QB + gi * TT
                    return g.dma_start(
                        out=x2g[:].rearrange("p (c d) -> p c d", d=D),
                        in_=scr["x2"][off:off + TT, :]
                        .rearrange("(c p) d -> p c d", p=128))
                ee.Gsw(gx2qb, waits=ee.wv("v_done"), inc=("g_dma", 16))
                sld = ee.now("g_dma")
                for rc in range(TT // 128):
                    srcc = x2g[:, rc * D:(rc + 1) * D]

                    def vstq(v, srcc=srcc, rc=rc):
                        v.bn_stats(out=st6p[:, 0:6], in_=srcc[:, 0:512])
                        v.bn_stats(out=st6p[:, 6:12], in_=srcc[:, 512:1024])
                        return v.bn_aggr(out=stcp[:, 2 * rc:2 * rc + 2],
                                         in_=st6p[:])
                    ee.V(vstq, waits=[("g_dma", sld),
                                      ("sc_done", ee.now("sc_done"))])

                def scq_all(s):
                    s.activation(out=stcp[:, 8:12], in_=stcp[:, 1:8:2],
                                 func=AF.Ln, bias=ee.t["cbias"][:, 1:2])
                    return s.activation(out=stcp[:, 8:12], in_=stcp[:, 8:12],
                                        func=AF.Exp, scale=-0.5)
                ee.Sc(scq_all, waits=ee.wv("v_done"))
                qstats_done = ee.now("sc_done")
                for rc in range(TT // 128):
                    srcc = x2g[:, rc * D:(rc + 1) * D]
                    ee.V(lambda v, srcc=srcc, rc=rc: v.tensor_scalar(
                        out=xcu[:], in0=srcc, scalar1=stcp[:, 2 * rc:2 * rc + 1],
                        scalar2=stcp[:, 8 + rc:9 + rc],
                        op0=ALU.subtract, op1=ALU.mult),
                        waits=[("sc_done", qstats_done),
                               ("t_done", ee.now("t_done"))])
                    vdn = ee.now("v_done")
                    for dck in range(NC8):
                        ee.Tn(lambda t, dck=dck: t.transpose(
                            psTp[:, 0:128], xcu[:, dck * 128:(dck + 1) * 128],
                            ee.t["eye_bf"][:]),
                            waits=[("v_done", max(vdn, ee.now("v_done")))],
                            inc=("t_done", 1))
                        dstc = xcq[:, dck * QB + gi * TT + rc * 128:
                                   dck * QB + gi * TT + (rc + 1) * 128]
                        ee.V(lambda v, dstc=dstc: v.tensor_copy(
                            out=dstc, in_=psTp[:, 0:128]),
                            waits=ee.wv("t_done"))
            xcq_done = ee.now("v_done")

            for tb in range(QB // 1024):
                def gqtab(g, pnc, tb=tb):
                    batch, s, h2 = _role(pnc)
                    qpos = s * HALF + h2 * QB
                    g.dma_start(out=ctab[:],
                                in_=inp["rope_cos"][:, qpos + tb * 1024:
                                                    qpos + (tb + 1) * 1024]
                                ).then_inc(sems["g_dma"], 16)
                    return g.dma_start(out=stab[:],
                                       in_=inp["rope_sin"][:, qpos + tb * 1024:
                                                           qpos + (tb + 1) * 1024])
                ee.Gsw(gqtab, waits=ee.wv("v_done"), inc=("g_dma", 16))
                ee.cnt["g_dma"] += 16
                tabld = ee.now("g_dma")
                for qc4 in range(NC8):
                    wsrc = wqb1 if qc4 < 4 else wqb2
                    qoff = (qc4 % 4) * 128

                    def tqp(t, tb=tb, qc4=qc4, wsrc=wsrc, qoff=qoff):
                        mm = None
                        for dck in range(NC8):
                            for nn in range(2):
                                mm = t.matmul(
                                    psP[:, nn * 512:(nn + 1) * 512],
                                    wsrc[:, dck * 512 + qoff:dck * 512 + qoff + 128],
                                    xcq[:, dck * QB + tb * 1024 + nn * 512:
                                        dck * QB + tb * 1024 + (nn + 1) * 512],
                                    start=(dck == 0), stop=(dck == NC8 - 1))
                        return mm
                    ee.Tn(tqp, waits=[("v_done", max(xcq_done, ee.now("v_done"))),
                                      ("sc_done", ee.now("sc_done")),
                                      ("g_dma", ee.now("g_dma"))],
                          inc=("t_done", 1))
                    ee.Sc(lambda s: s.activation(out=ktmp[:], in_=psP[:],
                                                 func=AF.Identity, scale=0.125),
                          waits=ee.wv("t_done"))

                    def vqrope(v, tb=tb, qc4=qc4):
                        mask = []
                        for i in range(0, 32, 2):
                            mask += [i + 1, i]
                        v.stream_shuffle(out=kshu[:], in_=ktmp[:], mask=mask)
                        v.tensor_tensor(out=krop[:], in0=ktmp[:], in1=ctab[:],
                                        op=ALU.mult)
                        v.scalar_tensor_tensor(out=kshu[:], in0=kshu[:],
                                               scalar=1.0, in1=stab[:],
                                               op0=ALU.mult, op1=ALU.mult)
                        return v.tensor_tensor(
                            out=qTb[:, qc4 * QB + tb * 1024:
                                    qc4 * QB + (tb + 1) * 1024],
                            in0=krop[:], in1=kshu[:], op=ALU.add)
                    ee.V(vqrope, waits=[("g_dma", tabld),
                                        ("sc_done", ee.now("sc_done"))])
            ee.G(lambda g, v=ee.now("g_dma"): g.wait_ge(sems["g_dma"], v))

        ee.barrier()

        with _ctx.ExitStack() as cst:
            def csb(name, shape, dt=F32):
                h = cst.enter_context(nc.sbuf_tensor(name, shape, dt))
                ee.t[name] = h
                return h
            def cps(name, shape, dt=F32):
                h = cst.enter_context(nc.psum_tensor(name, shape, dt))
                ee.t[name] = h
                return h
            khalf = csb("khalf", [128, 2 * T], BF16)
            vhalf = csb("vhalf", [128, 64 * 260], BF16)
            Ebuf = csb("Ebuf", [128, 2 * 1024], BF16)
            q64 = csb("q64", [128, 2 * 512], BF16)
            denb = csb("denb", [128, 16])
            avs = csb("avs", [64, 512])
            denr = csb("denr", [1, 512])
            denrb = csb("denrb", [1, 512], BF16)
            ones64 = csb("ones64", [1, 64], BF16)
            psS = cps("psS", [128, 2 * 1024])
            psAV = cps("psAV", [128, 2 * 512])
            psBC = cps("psBC", [64, 2 * 512])

            vev_done = {0: 0, 1: 0}
            vsc_done = {0: 0, 1: 0}
            q_staged = {0: 0, 1: 0}
            q_last_sc = {0: 0, 1: 0}
            pending = [None]
            prev_exp = [None]

            ee.V(lambda v: v.memset(ones64[:], 1.0))
            def gpkv(g, pnc):
                s = pnc % 2
                _emit_poll(ee, g, psh, FL_KV, want_side=1 - s)
                return None
            ee.Gsw(gpkv, done_sem="g_done")

            for hs in range(2):
                def gkh(g, hs=hs):
                    for lc in range(2):
                        for sd in range(2):
                            g.dma_start(
                                out=khalf[:, lc * T + sd * HALF:
                                          lc * T + (sd + 1) * HALF],
                                in_=psh["kT"][sd * NKV + (2 * hs + lc) * 128:
                                              sd * NKV + (2 * hs + lc + 1) * 128, :]
                                ).then_inc(sems["g_dma"], 16)
                    return None
                ee.G(lambda g, hs=hs: gkh(None if False else g, hs),
                     waits=ee.wv("g_done", "t_done", "sc_done"))
                ee.cnt["g_dma"] += 64
                def gvh(g, hs=hs):
                    for sd in range(2):
                        for hh in range(4):
                            g.dma_start(
                                out=vhalf[:, sd * 32 * 260:(sd + 1) * 32 * 260]
                                .rearrange("p (g r) -> p g r", r=260)
                                [:, :, hh * 65:hh * 65 + 64],
                                in_=psh["v"][sd * HALF:(sd + 1) * HALF,
                                             hs * 256 + hh * 64:
                                             hs * 256 + (hh + 1) * 64]
                                .rearrange("(g p) d -> p g d", p=128)
                                ).then_inc(sems["g_dma"], 16)
                    return None
                ee.G(lambda g, hs=hs: gvh(g, hs))
                ee.cnt["g_dma"] += 128
                ee.V(lambda v: v.memset(
                    vhalf[:].rearrange("p (g h d) -> p g h d", h=4, d=65)
                    [:, :, :, 64:65], 1.0),
                    waits=ee.wv("g_dma"))
                kvld = ee.now("g_dma")

                groups = [(qg, h8) for qg in range(4) for h8 in range(8)]

                def gparm(h8, hs=hs):
                    qh = hs * 8 + h8
                    kvh = qh // 2
                    lci = (kvh - hs * 4) // 2
                    rbase = ((kvh - hs * 4) % 2) * 64
                    qrow = (qh % 2) * 64
                    qchunk = qh // 2
                    return qh, kvh, lci, rbase, qrow, qchunk

                def stage_q(gi, hs=hs):
                    qg, h8 = groups[gi]
                    qh, kvh, lci, rbase, qrow, qchunk = gparm(h8)
                    gpn = (hs * 32 + gi) % 2

                    def vq64(v, qrow=qrow, rbase=rbase, qchunk=qchunk,
                             qg=qg, gpn=gpn):
                        return v.tensor_copy(
                            out=q64[rbase:rbase + 64, gpn * 512:(gpn + 1) * 512],
                            in_=qTb[qrow:qrow + 64,
                                    qchunk * QB + qg * 512:
                                    qchunk * QB + (qg + 1) * 512])
                    ee.V(vq64, waits=[("t_done", q_last_sc[gpn])])
                    q_staged[gpn] = ee.now("v_done")

                stage_q(0)
                for gi, (qg, h8) in enumerate(groups):
                    qh, kvh, lci, rbase, qrow, qchunk = gparm(h8)
                    gp = (hs * 32 + gi) % 2
                    exp_of = {}
                    NU = 32
                    exp_base = ee.now("sc_done")
                    sc_items = []

                    def emit_sc2(u, gp=gp, lci=lci, rbase=rbase,
                                 prev_exp=prev_exp[0]):
                        def tsc2(t, u=u, lci=lci, rbase=rbase, gp=gp):
                            mm = None
                            for j in range(2):
                                kt = u * 2 + j
                                mm = t.matmul(
                                    psS[:, (u % 2) * 1024 + j * 512:
                                        (u % 2) * 1024 + (j + 1) * 512],
                                    khalf[:, lci * T + kt * 128:
                                          lci * T + (kt + 1) * 128]
                                    [rbase:rbase + 64, :],
                                    q64[rbase:rbase + 64,
                                        gp * 512:(gp + 1) * 512],
                                    start=True, stop=True)
                            return mm
                        w = [("g_dma", kvld), ("v_done", q_staged[gp])]
                        if u >= 2:
                            w.append(("sc_done", exp_of[u - 2]))
                        elif prev_exp is not None:
                            w.append(("sc_done", prev_exp[NU - 2 + u]))
                        ee.Tn(tsc2, waits=w, inc=("t_done", 1))

                    def emit_exp2(u):
                        def sexp2(s, u=u):
                            return s.activation(
                                out=Ebuf[:, (u % 2) * 1024:(u % 2 + 1) * 1024],
                                in_=psS[:, (u % 2) * 1024:(u % 2 + 1) * 1024],
                                func=AF.Exp, bias=ee.t["cbias"][:, 2:3])
                        sc_items.append(
                            (sexp2, [("t_done", ee.now("t_done"))], None))
                        exp_of[u] = exp_base + u + 1

                    def emit_av2(u, gp=gp, kvh=kvh):
                        def tav2(t, u=u, kvh=kvh, hs=hs, gp=gp):
                            mm = None
                            hh = kvh - hs * 4
                            for j in range(2):
                                kt = u * 2 + j
                                mm = t.matmul(
                                    psAV[0:65, gp * 512:(gp + 1) * 512],
                                    vhalf[:, kt * 260 + hh * 65:
                                          kt * 260 + (hh + 1) * 65],
                                    Ebuf[:, (u % 2) * 1024 + j * 512:
                                         (u % 2) * 1024 + (j + 1) * 512],
                                    start=(kt == 0), stop=(kt == 2 * NU - 1))
                            return mm
                        w = [("sc_done", exp_of[u])]
                        if u == 0:
                            w.append(("v_done", vev_done[gp]))
                        ee.Tn(tav2, waits=w, inc=("t_done", 1))

                    emit_sc2(0)
                    emit_exp2(0)
                    emit_sc2(1)
                    emit_av2(0)
                    emit_exp2(1)
                    if pending[0] is not None:
                        pending[0]()
                        pending[0] = None
                    if gi + 1 < len(groups):
                        stage_q(gi + 1)
                    for u in range(2, NU):
                        emit_sc2(u)
                        emit_av2(u - 1)
                        emit_exp2(u)
                    q_last_sc[gp] = ee.now("t_done")
                    emit_av2(NU - 1)
                    av31_cnt = ee.now("t_done")
                    ee.Sc_batch(sc_items)
                    prev_exp[0] = exp_of

                    def vev(v, gp=gp):
                        v.tensor_copy(out=denr[:],
                                      in_=psAV[64:65, gp * 512:gp * 512 + 512])
                        v.reciprocal(out=denr[:], in_=denr[:])
                        v.tensor_copy(out=denrb[:], in_=denr[:])
                        return v.tensor_copy(out=avs[:],
                                             in_=psAV[0:64,
                                                      gp * 512:gp * 512 + 512])
                    ee.V(vev, waits=[("t_done", av31_cnt)])
                    vev_done[gp] = ee.now("v_done")

                    def mk_tail(gp=gp, qh=qh, qg=qg, vevd=vev_done[gp],
                                vscd=vsc_done[gp]):
                        def tail():
                            ee.Tn(lambda t: t.matmul(
                                psBC[:, gp * 512:(gp + 1) * 512], ones64[:],
                                denrb[:], start=True, stop=True),
                                waits=[("v_done", max(vevd, vscd))],
                                inc=("t_done", 1))
                            bc_cnt = ee.now("t_done")

                            def vsc(v, qh=qh, qg=qg, gp=gp):
                                qchunk = qh // 2
                                qrow = (qh % 2) * 64
                                return v.scalar_tensor_tensor(
                                    out=attoT[qrow:qrow + 64,
                                              qchunk * QB + qg * 512:
                                              qchunk * QB + (qg + 1) * 512],
                                    in0=avs[:], scalar=1.0,
                                    in1=psBC[:, gp * 512:(gp + 1) * 512],
                                    op0=ALU.mult, op1=ALU.mult)
                            ee.V(vsc, waits=[("t_done", bc_cnt)])
                            vsc_done[gp] = ee.now("v_done")
                        return tail
                    pending[0] = mk_tail()
                if pending[0] is not None:
                    pending[0]()
                    pending[0] = None
        ee.barrier()

        with _ctx.ExitStack() as dst_:
            def dsb(name, shape, dt=F32):
                h = dst_.enter_context(nc.sbuf_tensor(name, shape, dt))
                ee.t[name] = h
                return h
            def dps(name, shape, dt=F32):
                h = dst_.enter_context(nc.psum_tensor(name, shape, dt))
                ee.t[name] = h
                return h
            wob = dsb("wob", [128, NC8 * D], BF16)
            wldo = dsb("wldo", [128, D])
            bobf = dsb("bobf", [1, D], BF16)
            bob = dsb("bob", [128, D])
            ones128 = dsb("ones128", [1, 128], BF16)
            x2q = dsb("x2q", [128, (TT // 128) * D])
            x3st = dsb("x3st", [128, (TT // 128) * D])
            psW = [dps("psW%d" % i, [128, D]) for i in range(4)]

            for c in range(NC8):
                ee.S(lambda e, c=c: e.dma_start(
                    out=wldo[:], in_=inp["L0_attn_b_Wo"][c * 128:(c + 1) * 128, :]),
                    waits=ee.wv("v_done"), inc=("s_dma", 16))
                ee.V(lambda v, c=c: v.tensor_copy(out=wob[:, c * D:(c + 1) * D],
                                                  in_=wldo[:]),
                     waits=ee.wv("s_dma"))
            ee.S(lambda e: e.dma_start(out=wldo[0:1, :], in_=inp["L0_attn_b_bo"][:, :]),
                 inc=("s_dma", 16))

            def vbo(v):
                v.memset(ones128[:], 1.0)
                return v.tensor_copy(out=bobf[:], in_=wldo[0:1, :])
            ee.V(vbo, waits=ee.wv("s_dma"))
            for nh in range(2):
                ee.Tn(lambda t, nh=nh: t.matmul(
                    psW[0][:, nh * 512:(nh + 1) * 512], ones128[:],
                    bobf[:, nh * 512:(nh + 1) * 512], start=True, stop=True),
                    waits=ee.wv("v_done"), inc=("t_done", 1))
            ee.V(lambda v: v.tensor_copy(out=bob[:], in_=psW[0][:]),
                 waits=ee.wv("t_done"))

            for qg in range(4):
                def gx2q(g, pnc, qg=qg):
                    batch, s, h2 = _role(pnc)
                    off = h2 * QB + qg * TT
                    return g.dma_start(
                        out=x2q[:].rearrange("p (c d) -> p c d", d=D),
                        in_=scr["x2"][off:off + TT, :]
                        .rearrange("(c p) d -> p c d", p=128))
                ee.Gsw(gx2q, waits=ee.wv("v_done"), inc=("g_dma", 16))
                x2ld = ee.now("g_dma")
                for ts in range(TT // 128):
                    def tow(t, qg=qg, ts=ts):
                        mm = None
                        for dck in range(NC8):
                            lhs = attoT[:, dck * QB + qg * TT + ts * 128:
                                        dck * QB + qg * TT + (ts + 1) * 128]
                            for nh in range(2):
                                mm = t.matmul(
                                    psW[ts][:, nh * 512:(nh + 1) * 512], lhs,
                                    wob[:, dck * D + nh * 512:dck * D + (nh + 1) * 512],
                                    start=(dck == 0), stop=(dck == NC8 - 1))
                        return mm
                    ee.Tn(tow, waits=[("v_done", ee.now("v_done")),
                                      ("g_dma", x2ld)], inc=("t_done", 1))

                    def vx3(v, ts=ts):
                        sl = slice(ts * D, (ts + 1) * D)
                        v.tensor_tensor(out=x3st[:, sl], in0=psW[ts][:],
                                        in1=bob[:], op=ALU.add)
                        return v.tensor_tensor(out=x3st[:, sl], in0=x3st[:, sl],
                                               in1=x2q[:, sl], op=ALU.add)
                    ee.V(vx3, waits=ee.wv("t_done"))
                ee.G(lambda g, qg=qg: g.dma_start(
                    out=scr["x3"][qg * TT:(qg + 1) * TT, :]
                    .rearrange("(c p) d -> p c d", p=128),
                    in_=x3st[:].rearrange("p (c d) -> p c d", d=D)),
                    waits=ee.wv("v_done"), inc=("g_dma", 16))
            ee.G(lambda g, v=ee.now("g_dma"): g.wait_ge(sems["g_dma"], v))


def emit_p2(ee, stack, inp, scr):
    emit_ffn(ee, stack, inp, scr, "x1", scr["x2"], HALF, dbgtag="f1")


def emit_p4(ee, stack, inp, scr, out):
    emit_ffn(ee, stack, inp, scr, "x3", out, QB, dbgtag="f2")




def _colpack(inputs):
    pk = np.zeros((128, 16 * 8), np.float32)
    src = {"mk": "L0_attn_a_tm_mix_k", "mv": "L0_attn_a_tm_mix_v",
           "mr": "L0_attn_a_tm_mix_r", "cmk": "L0_attn_a_cm_mix_k",
           "cmr": "L0_attn_a_cm_mix_r", "td": "L0_attn_a_time_decay",
           "tf": "L0_attn_a_time_first", "g1": "L0_attn_a_ln1_g",
           "b1": "L0_attn_a_ln1_b", "g2": "L0_attn_a_ln2_g",
           "b2": "L0_attn_a_ln2_b"}
    for k, nm in src.items():
        v = np.asarray(inputs[nm], np.float32).reshape(-1)
        pk[:, CSLOT[k] * 8:CSLOT[k] * 8 + 8] = v.reshape(8, 128).T
    return pk


def _derived_inputs(inputs):
    pos = np.arange(T, dtype=np.float64)
    rows = np.arange(128)
    j = rows % 64
    inv_freq = 1.0 / (10000.0 ** ((2 * (j // 2)) / ROT))
    freqs = pos[None, :] * inv_freq[:, None]
    cos = np.cos(freqs)
    sin = np.sin(freqs) * np.where(j % 2 == 0, -1.0, 1.0)[:, None]
    rot = (j < ROT)[:, None]
    nonce = int(np.random.randint(1 << 20, 1 << 30))
    return {
        "rope_cos": np.where(rot, cos, 1.0).astype(np.float32),
        "rope_sin": np.where(rot, sin, 0.0).astype(np.float32),
        "colpack": _colpack(inputs),
        "eye_bf16": np.eye(128, dtype=ml_dtypes.bfloat16),
        "nonce_row": (nonce + np.arange(16)).astype(np.int32).reshape(1, 16),
    }


def _make_in_map(inputs):
    in_map = {}
    x = np.asarray(inputs["L0_attn_a_x"], dtype=np.float32)
    in_map["L0_attn_a_x"] = np.ascontiguousarray(x.reshape(B * T, D))
    for k, v in inputs.items():
        if k == "L0_attn_a_x":
            continue
        a = np.asarray(v)
        if a.ndim == 1:
            a = a.reshape(1, -1)
        in_map[k] = np.ascontiguousarray(a)
    in_map.update(_derived_inputs(inputs))
    return in_map


def kernel(**inputs):
    in_map = _make_in_map(inputs)
    nc = build_program()
    res = run_bass_kernel_spmd(nc, [in_map] * 8, core_ids=list(range(8)))
    full = np.zeros((B, T, D), dtype=np.float32)
    for rr in res.results:
        pnc = int(round(float(rr["whoami"][0, 0])))
        batch, s, h2 = _role(pnc)
        qstart = s * HALF + h2 * QB
        full[batch, qstart:qstart + QB, :] = rr["out"]
    return full
